# version 19
# baseline (speedup 1.0000x reference)
import numpy as np
import concourse.bass as bass
import concourse.mybir as mybir
from concourse.bass_utils import run_bass_kernel_spmd

F32 = mybir.dt.float32
BF16 = mybir.dt.bfloat16
AF = mybir.ActivationFunctionType
ALU = mybir.AluOpType
AXX = mybir.AxisListType.X

D = 1024
T = 4096
TC = 256
NCH = T // TC
NTS = TC // 128
DG = 8
NST = 16
DCONV = 8
DTR = 64
HF = 4096
NE = 8
DE = 512
NPIECE = 19
PCOLS = 8192
LN_EPS = 1e-5
PIPELINE = True
DBG_CH = -1
CONSTS = [1e-24, 1e-5, 1.0, 1e-30, -0.5, 1024.0, 0.0, 2.0, -1.0]

ENGINES = ("sync", "tensor", "vector", "scalar", "gpsimd")


NLANES = 12
ENG_SKIP_DIST = {"tensor": 0, "vector": 10, "scalar": 10 ** 9, "gpsimd": 6,
                 "sync": 10 ** 9}


class Builder:

    def __init__(self):
        self.ops = []
        self.bufs = {}
        self.lane_ctr = {}
        self.lane_last = {}
        self.eng_seq = {}

    def _buf(self, name):
        b = self.bufs.get(name)
        if b is None:
            b = {"w": {}, "r": {}}
            self.bufs[name] = b
        return b

    ctx = ("init", -1)

    def op(self, eng, fn, reads=(), writes=(), dma=False, tag="", late=False):
        i = len(self.ops)
        if dma:
            q = "q_" + eng
            cnt = self.lane_ctr.get(q, 0)
            self.lane_ctr[q] = cnt + 1
            lane = cnt % NLANES
            ekey = f"{q}_{lane}"
        else:
            ekey = eng
        deps = {}

        myseq = self.eng_seq.get(ekey, 0)

        def dep(m):
            for k, r in m.items():
                if (k == ekey and not dma and not self.ops[r].get("late")
                        and myseq - self.ops[r]["eseq"] > ENG_SKIP_DIST[eng]):
                    continue
                if deps.get(k, -1) < r:
                    deps[k] = r

        for nm in reads:
            dep(self._buf(nm)["w"])
        for nm in writes:
            b = self._buf(nm)
            dep(b["w"])
            dep(b["r"])
        if dma:
            prev = self.lane_last.get(ekey)
            if prev is not None and deps.get(ekey, -1) < prev:
                deps[ekey] = prev
            self.lane_last[ekey] = i
        for nm in reads:
            b = self._buf(nm)
            if b["r"].get(ekey, -1) < i:
                b["r"][ekey] = i
        for nm in writes:
            b = self._buf(nm)
            b["w"] = dict(b["w"])
            b["w"][ekey] = i
            b["r"] = {}
        self.ops.append(dict(eng=eng, ekey=ekey, fn=fn, deps=deps, dma=dma,
                             tag=tag, need_inc=False, rank=None, late=late,
                             eseq=myseq, ctx=Builder.ctx, rd=tuple(reads),
                             wr=tuple(writes)))
        self.eng_seq[ekey] = myseq + 1
        return i

    def dma(self, eng, fn, reads=(), writes=(), tag=""):
        return self.op(eng, fn, reads, writes, dma=True, tag=tag)

    def drain_all(self, eng):
        deps = {ek: i for ek, i in self.lane_last.items()}
        i = len(self.ops)
        self.ops.append(dict(eng=eng, ekey=eng, fn=lambda e: e.nop(),
                             deps=deps, dma=False, tag="final_drain",
                             need_inc=False, rank=None))
        return i

    def finalize(self, nc):
        for o in self.ops:
            for ek, r in o["deps"].items():
                self.ops[r]["need_inc"] = True
        for o in self.ops:
            if o["dma"]:
                o["need_inc"] = True
        counts = {}
        for o in self.ops:
            if o["need_inc"]:
                counts[o["ekey"]] = counts.get(o["ekey"], 0) + (16 if o["dma"] else 1)
                o["rank"] = counts[o["ekey"]]
        self.final_counts = counts
        sems = {ek: nc.alloc_semaphore("sem_" + ek) for ek in counts}
        self.sems = sems
        per_eng = {e: [] for e in ENGINES}
        for o in self.ops:
            per_eng[o["eng"]].append(o)
        waited = {e: {} for e in ENGINES}

        def emit(eng_name, eng):
            for o in per_eng[eng_name]:
                for ek, r in sorted(o["deps"].items()):
                    dep = self.ops[r]
                    val = dep["rank"]
                    assert val is not None, (o["tag"], dep["tag"])
                    if waited[eng_name].get(ek, 0) < val:
                        eng.wait_ge(sems[ek], val)
                        waited[eng_name][ek] = val
                inst = o["fn"](eng)
                if o["need_inc"]:
                    inst.then_inc(sems[o["ekey"]], 16 if o["dma"] else 1)
        return emit


def build_nc(nchunks=NCH, debug=()):
    nc = bass.Bass("TRN2", target_bir_lowering=False)
    B = Builder()

    def din(name, shape):
        return nc.dram_tensor(name, shape, F32, kind="ExternalInput").ap()

    x_d = din("x", [T, D])
    wpf_d = din("wpf", [NPIECE, 128, PCOLS])
    convw_d = din("convw", [D, DCONV])
    convb_d = din("convb", [D])
    xpw_d = din("xpw", [96, D])
    dtw_d = din("dtw", [D, DTR])
    dtb_d = din("dtb", [D])
    alog_d = din("alog", [D, NST])
    dpar_d = din("dpar", [D])
    g1_d = din("g1", [D])
    b1_d = din("b1", [D])
    g2_d = din("g2", [D])
    b2_d = din("b2", [D])
    b1f_d = din("b1f", [HF])
    b2f_d = din("b2f", [D])
    mb1_d = din("mb1", [NE, DE])
    mb2_d = din("mb2", [NE, D])
    cent_d = din("cent", [NE, D])
    wrt_d = din("wrt", [NE, D])
    ident_d = din("ident", [128, 128])
    consts_d = din("consts", [128, len(CONSTS)])
    selbc_d = nc.dram_tensor("selbc", [32, 32 * 128], BF16, kind="ExternalInput").ap()
    selg_d = nc.dram_tensor("selg", [8, 8 * 128], BF16, kind="ExternalInput").ap()

    out_d = nc.dram_tensor("out", [T, D], F32, kind="ExternalOutput").ap()

    wpieces = nc.dram_tensor("wpieces", [NPIECE, 128, PCOLS], BF16).ap()

    def S(name, shape, dt=F32, **kw):
        return nc.alloc_sbuf_tensor(name, shape, dt, **kw)

    x_tm = [S(f"x_tm{i}", [128, NTS * D]) for i in range(2)]
    xn_tm = S("xn_tm", [128, NTS * D])
    xn_bf = [S(f"xn_bf{i}", [128, DG * TC], BF16) for i in range(2)]
    xns_bf = S("xns_bf", [128, DG * TC], BF16)
    s1b_sb = S("s1b_sb", [128, TC])
    rstd2_sb = S("rstd2_sb", [128, TC])
    m2neg_sb = S("m2neg_sb", [128, TC])
    ones_f = S("ones_f", [1, 128])
    xin = S("xin", [128, DG * (TC + 8)])
    sz = S("sz", [128, DG * TC])
    xc = S("xc", [128, DG * TC])
    xc_bf = S("xc_bf", [128, DG * TC], BF16)
    delta = S("delta", [128, DG * TC])
    uful = xn_tm
    dA = [S(f"dA{i}", [128, TC]) for i in range(3)]
    dBx = [S(f"dBx{i}", [128, TC]) for i in range(3)]
    hs = [S(f"hs{i}", [128, TC]) for i in range(3)]
    gy = [S(f"gy{i}", [128, TC], BF16) for i in range(8)]
    crep = [S(f"crep{i}", [128, TC]) for i in range(2)]
    carry = S("carry", [128, DG * NST])
    t1s = [S(f"t1s{i}", [128, TC]) for i in range(2)]
    ymix_bf = [S(f"ymix_bf{i}", [128, DG * TC], BF16) for i in range(2)]
    x1 = S("x1", [128, DG * TC])
    hgm = S("hgm", [128, 32 * TC], BF16)
    xsq_bf = hgm[:, 0:DG * TC]
    lnp_bf = S("lnp_bf", [128, DG * TC], BF16)
    sS = S("sS", [128, DG * TC])
    gfin = S("gfin", [128, DG * TC], BF16)
    brep_sb = [S(f"brep{i}", [128, TC]) for i in range(2)]
    etmp2 = S("etmp2", [128, TC])
    acc = [dA[0], dA[1]]
    etmp = dA[2]
    slots = [S(f"slot{i}", [128, PCOLS], BF16, is_transpose_dest=True)
             for i in range(2)]
    rep_f = x_tm

    statc = S("statc", [128, 16])
    stile_s1 = S("stile_s1", [128, NTS])
    stile_rn = S("stile_rn", [128, NTS])
    trow_s1 = S("trow_s1", [1, TC])
    trow_rn = [S(f"trow_rn{i}", [1, TC]) for i in range(2)]
    r_mu2 = S("r_mu2", [1, TC])
    r_rstd2 = S("r_rstd2", [1, TC])
    r_m2neg = S("r_m2neg", [1, TC])
    r_tmp = S("r_tmp", [1, TC])
    dt_bf = S("dt_bf", [DTR, TC], BF16)
    bc_bf = S("bc_bf", [2 * NST, TC], BF16)
    selbc_s = S("selbc_s", [32, 32 * 128], BF16)
    selg_s = S("selg_s", [8, 8 * 128], BF16)
    ltrow = S("ltrow", [8, TC])
    lt_tm = S("lt_tm", [128, 8 * NTS])
    gmax = S("gmax", [128, 8 * NTS])
    gwrk = S("gwrk", [128, 8 * NTS])
    gmsk = S("gmsk", [128, 8 * NTS])
    gneg = S("gneg", [128, NTS])
    gden = S("gden", [128, NTS])
    gT_bf = S("gT_bf", [8, TC], BF16)
    ident = S("identsb", [128, 128])
    cc = S("cc", [128, len(CONSTS)])
    ident_bf = S("ident_bf", [128, 128], BF16)
    ones_bf = S("ones_bf", [1, 128], BF16)
    zrow_bf = S("zrow_bf", [1, 512], BF16)
    onec_bf = S("onec_bf", [128, 1], BF16)
    Afm = S("Afm", [128, DG * NST])
    convw_s = S("convw_s", [128, DG * DCONV])
    convb_s = S("convb_s", [128, DG])
    dtb_s = S("dtb_s", [128, DG])
    dpar_s = S("dpar_s", [128, DG])
    g1_s = S("g1_s", [128, DG])
    g2_s = S("g2_s", [128, DG])
    b2f_s = S("b2f_s", [128, DG])
    b1_f = S("b1_f", [128, DG])
    b2_f = S("b2_f", [128, DG])
    b1_bf = S("b1_bf", [128, DG], BF16)
    b2ln_bf = S("b2ln_bf", [128, DG], BF16)
    wb1 = S("wb1", [128, 16])
    wb1n = S("wb1n", [128, 16])
    wb2f = S("wb2f", [128, 32])
    b1f_s = S("b1f_s", [128, 32])
    mb1_s = S("mb1_s", [128, NE * 4])
    mb2_bf = S("mb2_bf", [8, D], BF16)
    cw_bf = S("cw_bf", [128, DG * 16], BF16)
    rc_s = S("rc_s", [8, 4])
    xpT_bf = S("xpT_bf", [128, DG * 96], BF16)
    dtT_bf = S("dtT_bf", [DTR, DG * 128], BF16)
    wos = S("wos", [128, DG])
    wos_bf = S("wos_bf", [128, DG], BF16)

    csq = xn_tm[0:8, 0:1024]
    mb2_f = xn_tm[32:40, 0:1024]
    dtw_s = xn_tm[0:128, 1024:1536]
    xpw_s = x1[0:96, 0:1024]
    cent_s = x1[0:8, 1024:2048]
    wrt_s = x1[32:40, 1024:2048]
    pb = [nc.alloc_psum_tensor(f"pb{i}", [128, 512], F32) for i in range(8)]

    dbg_d = {}
    dbg_src = {}

    def half(bank, h):
        return pb[bank][:, h * TC:(h + 1) * TC]

    B.dma("gpsimd", lambda e: e.dma_start(out=ident[:, :], in_=ident_d),
          writes=["ident"], tag="ld_ident")
    B.dma("gpsimd", lambda e: e.dma_start(out=cc[:, :], in_=consts_d),
          writes=["cc"], tag="ld_cc")
    B.dma("gpsimd", lambda e: e.dma_start(out=selbc_s[:, :], in_=selbc_d),
          writes=["selbc_s"], tag="ld_selbc")
    B.dma("gpsimd", lambda e: e.dma_start(out=selg_s[:, :], in_=selg_d),
          writes=["selg_s"], tag="ld_selg")
    for ci, cval in enumerate(CONSTS):
        nc.const_aps.register(F32, cval, cc[:, ci:ci + 1])
    B.op("vector", lambda e: e.tensor_copy(out=ident_bf[:, :], in_=ident[:, :]),
         reads=["ident"], writes=["ident_bf"])
    B.op("vector", lambda e: e.memset(ones_bf[:, :], 1.0), writes=["ones_bf"])
    B.op("vector", lambda e: e.memset(ones_f[:, :], 1.0), writes=["ones_f"])
    B.op("vector", lambda e: e.memset(zrow_bf[:, :], 0.0), writes=["zrow_bf"])
    B.op("vector", lambda e: e.memset(onec_bf[:, :], 1.0), writes=["onec_bf"])
    B.op("gpsimd", lambda e: e.memset(carry[:, :], 0.0), writes=["carry"])
    for dg in range(DG):
        o = dg * (TC + 8)
        B.op("gpsimd", lambda e, o=o: e.memset(xin[:, o:o + 8], 0.0),
             writes=["xin_head"])

    def vec_load(dst, src_ap, name):
        B.dma("gpsimd",
              lambda e: e.dma_start(out=dst[:, :],
                                    in_=src_ap.rearrange("(g p) -> p g", p=128),
                                    allow_slow_non_contiguous=True),
              writes=[name], tag="ld_" + name)

    vec_load(convb_s, convb_d, "convb_s")
    vec_load(dtb_s, dtb_d, "dtb_s")
    vec_load(dpar_s, dpar_d, "dpar_s")
    vec_load(g1_s, g1_d, "g1_s")
    vec_load(g2_s, g2_d, "g2_s")
    vec_load(b2f_s, b2f_d, "b2f_s")
    vec_load(b1_f, b1_d, "b1_f")
    vec_load(b2_f, b2_d, "b2_f")
    B.op("vector", lambda e: e.tensor_copy(out=b1_bf[:, :], in_=b1_f[:, :]),
         reads=["b1_f"], writes=["b1_bf"])
    B.op("vector", lambda e: e.tensor_copy(out=b2ln_bf[:, :], in_=b2_f[:, :]),
         reads=["b2_f"], writes=["b2ln_bf"])
    B.dma("gpsimd", lambda e: e.dma_start(
        out=convw_s[:, :], in_=convw_d.rearrange("(g p) k -> p g k", p=128),
        allow_slow_non_contiguous=True),
        writes=["convw_s"], tag="ld_convw")
    B.dma("gpsimd", lambda e: e.dma_start(
        out=Afm[:, :], in_=alog_d.rearrange("(g p) n -> p g n", p=128),
        allow_slow_non_contiguous=True),
        writes=["Afm"], tag="ld_alog")
    B.dma("gpsimd", lambda e: e.dma_start(
        out=mb1_s[:, :], in_=mb1_d.rearrange("e (t p) -> p e t", p=128),
        allow_slow_non_contiguous=True),
        writes=["mb1_s"], tag="ld_mb1")
    B.dma("gpsimd", lambda e: e.dma_start(
        out=b1f_s[:, :], in_=b1f_d.rearrange("(t p) -> p t", p=128),
        allow_slow_non_contiguous=True),
        writes=["b1f_s"], tag="ld_b1f")
    B.dma("gpsimd", lambda e: e.dma_start(out=cent_s, in_=cent_d),
          writes=["x1"], tag="ld_cent")
    B.dma("gpsimd", lambda e: e.dma_start(out=wrt_s, in_=wrt_d),
          writes=["x1"], tag="ld_wrt")
    B.dma("gpsimd", lambda e: e.dma_start(out=mb2_f, in_=mb2_d),
          writes=["xn_tm"], tag="ld_mb2")
    B.op("vector", lambda e: e.tensor_copy(out=mb2_bf[:, :], in_=mb2_f),
         reads=["xn_tm"], writes=["mb2_bf"])
    B.dma("gpsimd", lambda e: e.dma_start(out=xpw_s, in_=xpw_d),
          writes=["x1"], tag="ld_xpw")
    B.dma("gpsimd", lambda e: e.dma_start(
        out=dtw_s, in_=dtw_d.rearrange("(g p) r -> p g r", p=128),
        allow_slow_non_contiguous=True),
        writes=["xn_tm"], tag="ld_dtw")

    B.op("scalar", lambda e: e.activation(out=Afm[:, :], in_=Afm[:, :], func=AF.Exp),
         reads=["Afm"], writes=["Afm"], tag="mkA")
    B.op("vector", lambda e: e.tensor_scalar_mul(Afm[:, :], Afm[:, :], -1.0),
         reads=["Afm"], writes=["Afm"])

    B.op("scalar", lambda e: e.activation(out=csq, in_=cent_s,
                                          func=AF.Square, accum_out=rc_s[:, 0:1]),
         reads=["x1"], writes=["xn_tm", "rc_s"], tag="rc_sq", late=True)
    B.op("scalar", lambda e: e.activation(out=rc_s[:, 1:2], in_=rc_s[:, 0:1],
                                          func=AF.Ln, bias=1e-24),
         reads=["rc_s", "cc"], writes=["rc_s"])
    B.op("scalar", lambda e: e.activation(out=rc_s[:, 2:3], in_=rc_s[:, 1:2],
                                          func=AF.Exp, scale=-0.5),
         reads=["rc_s", "cc"], writes=["rc_s"])

    for kc in range(DG):
        B.op("tensor", lambda e, kc=kc: e.transpose(
            out=pb[6][0:128, 0:96], in_=xpw_s[:, kc * 128:(kc + 1) * 128],
            identity=ident[0:96, 0:96]),
            reads=["x1", "ident", "pb6"], writes=["pb6"], tag=f"tr_xp{kc}")
        B.op("scalar", lambda e, kc=kc: e.copy(
            out=xpT_bf[:, kc * 96:(kc + 1) * 96], in_=pb[6][0:128, 0:96]),
            reads=["pb6"], writes=["xpT_bf"])
        B.op("tensor", lambda e, kc=kc: e.transpose(
            out=pb[7][0:DTR, 0:128], in_=dtw_s[:, kc * DTR:(kc + 1) * DTR],
            identity=ident[:, :]),
            reads=["xn_tm", "ident", "pb7"], writes=["pb7"], tag=f"tr_dt{kc}")
        B.op("scalar", lambda e, kc=kc: e.copy(
            out=dtT_bf[:, kc * 128:(kc + 1) * 128], in_=pb[7][0:DTR, 0:128]),
            reads=["pb7"], writes=["dtT_bf"])
        B.op("tensor", lambda e, kc=kc: e.transpose(
            out=pb[6][0:128, 96:104], in_=cent_s[:, kc * 128:(kc + 1) * 128],
            identity=ident[0:8, 0:8]),
            reads=["x1", "ident", "pb6"], writes=["pb6"], tag=f"tr_ce{kc}")
        B.op("tensor", lambda e, kc=kc: e.transpose(
            out=pb[6][0:128, 104:112], in_=wrt_s[:, kc * 128:(kc + 1) * 128],
            identity=ident[32:40, 32:40]),
            reads=["x1", "ident", "pb6"], writes=["pb6"])
        B.op("scalar", lambda e, kc=kc: e.copy(
            out=cw_bf[:, kc * 16:(kc + 1) * 16], in_=pb[6][0:128, 96:112]),
            reads=["pb6"], writes=["cw_bf"])

    LSZ = 1024
    NLANE = 2 * (PCOLS // LSZ)
    lane_free = list(range(NLANE))

    def lane_view(j):
        return slots[j // (PCOLS // LSZ)][
            :, (j % (PCOLS // LSZ)) * LSZ:((j % (PCOLS // LSZ)) + 1) * LSZ]

    wp_written = set()
    rgen_cell = [None]

    def load_lane(ch, pi, c0):
        while (pi, c0) not in wp_written and rgen_cell[0] is not None:
            if next(rgen_cell[0], "END") == "END":
                rgen_cell[0] = None
        assert (pi, c0) in wp_written, (pi, c0)
        j = alloc_lane()
        lv = lane_view(j)
        B.dma("sync", lambda e, pi=pi, lv=lv, c0=c0: e.dma_start(
            out=lv, in_=wpieces[pi][:, c0:c0 + LSZ]),
            reads=[f"wp{pi}"], writes=[f"lane{j}"], tag=f"c{ch}_p{pi}_{c0}")
        return lv, f"lane{j}", j

    def free_lane(j):
        lane_free.append(j)

    def alloc_lane():
        assert lane_free, "lane ring exhausted: missing free_lane"
        return lane_free.pop(0)

    def repack_gen():
        Builder.ctx = ("repack", -1)
        rp_stage = [(x1[0:128, 0:2048], "x1"), (sS[:, 0:2048], "sS")]
        rp_cast = [(lnp_bf, "lnp_bf"), (gfin, "gfin")]
        for pi in range(NPIECE):
            for q4 in range(4):
                p = (pi * 4 + q4) % 2
                sl, sln = rp_stage[p]
                cd, cdn = rp_cast[p]
                c0 = q4 * 2048
                B.dma("sync" if p == 0 else "scalar",
                      lambda e, pi=pi, c0=c0, sl=sl: e.dma_start(
                    out=sl[:, 0:2048], in_=wpf_d[pi][:, c0:c0 + 2048]),
                    writes=[sln], tag=f"rp_l{pi}_{c0}")
                if p == 0:
                    B.op("vector", lambda e, sl=sl, cd=cd: e.tensor_copy(
                        out=cd[:, 0:2048], in_=sl[:, 0:2048]),
                        reads=[sln], writes=[cdn], tag=f"rp_c{pi}_{c0}")
                else:
                    B.op("scalar", lambda e, sl=sl, cd=cd: e.copy(
                        out=cd[:, 0:2048], in_=sl[:, 0:2048]),
                        reads=[sln], writes=[cdn], tag=f"rp_c{pi}_{c0}")
                B.dma("gpsimd", lambda e, pi=pi, c0=c0, cd=cd: e.dma_start(
                    out=wpieces[pi][:, c0:c0 + 2048], in_=cd[:, 0:2048]),
                    reads=[cdn], writes=[f"wp{pi}"], tag=f"rp_s{pi}_{c0}")
                wp_written.add((pi, c0))
                wp_written.add((pi, c0 + 1024))
                yield


    def zero_mm(bank, tag):
        B.op("tensor", lambda e, bank=bank: e.matmul(
            out=pb[bank][:, 0:512], lhsT=zrow_bf[:, 0:128], rhs=zrow_bf[:, 0:512],
            start=True, stop=False),
            reads=["zrow_bf", f"pb{bank}"], writes=[f"pb{bank}"], tag=tag)

    def front_head(ch):
        Builder.ctx = ("front", ch)
        """P0..P3 (generator): x load, stats, xn, in_proj, conv, dbl, delta."""
        t0 = ch * TC
        xb = x_tm[ch % 2]
        xbn = f"x_tm{ch % 2}"
        pz = ch % 2
        xnb = xn_bf[pz]
        xnbn = f"xn_bf{pz}"
        B.dma("gpsimd", lambda e, t0=t0, xb=xb: e.dma_start(
            out=xb[:, :], in_=x_d[t0:t0 + TC, :].rearrange("(s p) d -> p s d", p=128)),
            writes=[xbn], tag=f"c{ch}_ldx")

        def ld_k(kc, rnd):
            return load_lane(ch, kc // 4, (kc % 4) * 2048 + rnd * 1024)

        for ts in range(NTS):
            B.op("vector", lambda e, ts=ts, xb=xb: e.tensor_scalar(
                out=xn_tm[:, ts * D:(ts + 1) * D], in0=xb[:, ts * D:(ts + 1) * D],
                scalar1=1.0, scalar2=0.0, op0=ALU.mult, op1=ALU.add,
                accum_out=statc[:, ts * 8:ts * 8 + 1]),
                reads=[xbn],
                writes=["xn_tm", "statc"] + [f"u{g}" for g in range(DG)],
                tag=f"c{ch}_sx{ts}",
                late=True)
            B.op("vector", lambda e, ts=ts, xb=xb: e.scalar_tensor_tensor(
                out=xn_tm[:, ts * D:(ts + 1) * D], in0=xb[:, ts * D:(ts + 1) * D],
                scalar=1.0, in1=xb[:, ts * D:(ts + 1) * D], op0=ALU.mult,
                op1=ALU.mult, accum_out=statc[:, ts * 8 + 1:ts * 8 + 2]),
                reads=[xbn],
                writes=["xn_tm", "statc"] + [f"u{g}" for g in range(DG)],
                tag=f"c{ch}_sxx{ts}",
                late=True)
        for ts in range(NTS):
            c = ts * 8
            B.op("vector", lambda e, c=c: e.tensor_scalar(
                out=statc[:, c + 2:c + 3], in0=statc[:, c:c + 1],
                scalar1=1.0 / D, scalar2=None, op0=ALU.mult),
                reads=["statc"], writes=["statc"])
            B.op("vector", lambda e, c=c: e.tensor_scalar(
                out=statc[:, c + 7:c + 8], in0=statc[:, c + 1:c + 2],
                scalar1=1.0 / D, scalar2=None, op0=ALU.mult),
                reads=["statc"], writes=["statc"])
            B.op("vector", lambda e, c=c: e.scalar_tensor_tensor(
                out=statc[:, c + 3:c + 4], in0=statc[:, c + 2:c + 3], scalar=-1.0,
                in1=statc[:, c + 2:c + 3], op0=ALU.mult, op1=ALU.mult),
                reads=["statc"], writes=["statc"])
            B.op("vector", lambda e, c=c: e.tensor_add(
                out=statc[:, c + 3:c + 4], in0=statc[:, c + 7:c + 8],
                in1=statc[:, c + 3:c + 4]),
                reads=["statc"], writes=["statc"])
            B.op("scalar", lambda e, c=c: e.activation(
                out=statc[:, c + 4:c + 5], in_=statc[:, c + 3:c + 4],
                func=AF.Ln, bias=LN_EPS),
                reads=["statc", "cc"], writes=["statc"], tag=f"c{ch}_l0{c}")
            B.op("scalar", lambda e, c=c: e.activation(
                out=statc[:, c + 4:c + 5], in_=statc[:, c + 4:c + 5],
                func=AF.Exp, scale=-0.5),
                reads=["statc", "cc"], writes=["statc"])
            B.op("vector", lambda e, c=c: e.scalar_tensor_tensor(
                out=statc[:, c + 7:c + 8], in0=statc[:, c + 4:c + 5], scalar=1.0,
                in1=statc[:, c + 4:c + 5], op0=ALU.mult, op1=ALU.mult),
                reads=["statc"], writes=["statc"])
            B.op("vector", lambda e, c=c: e.tensor_mul(
                out=statc[:, c + 7:c + 8], in0=statc[:, c + 7:c + 8],
                in1=statc[:, c + 3:c + 4]),
                reads=["statc"], writes=["statc"])
            B.op("scalar", lambda e, c=c: e.activation(
                out=statc[:, c + 5:c + 6], in_=statc[:, c + 7:c + 8],
                func=AF.Ln, bias=LN_EPS),
                reads=["statc", "cc"], writes=["statc"])
            B.op("scalar", lambda e, c=c: e.activation(
                out=statc[:, c + 5:c + 6], in_=statc[:, c + 5:c + 6],
                func=AF.Exp, scale=-0.5),
                reads=["statc", "cc"], writes=["statc"])
            B.op("scalar", lambda e, c=c: e.activation(
                out=statc[:, c + 6:c + 7], in_=statc[:, c + 7:c + 8],
                func=AF.Ln, scale=float(D), bias=1e-30),
                reads=["statc", "cc"], writes=["statc"])
            B.op("scalar", lambda e, c=c: e.activation(
                out=statc[:, c + 6:c + 7], in_=statc[:, c + 6:c + 7],
                func=AF.Exp, scale=-0.5),
                reads=["statc", "cc"], writes=["statc"])
            B.op("vector", lambda e, c=c, ts=ts: e.tensor_copy(
                out=stile_s1[:, ts:ts + 1], in_=statc[:, c + 5:c + 6]),
                reads=["statc"], writes=["stile"])
            B.op("vector", lambda e, c=c, ts=ts: e.tensor_copy(
                out=stile_rn[:, ts:ts + 1], in_=statc[:, c + 6:c + 7]),
                reads=["statc"], writes=["stile"])
            yield "H"
        for ts in range(NTS):
            c = ts * 8
            B.op("vector", lambda e, c=c, ts=ts, xb=xb: e.tensor_scalar(
                out=xn_tm[:, ts * D:(ts + 1) * D], in0=xb[:, ts * D:(ts + 1) * D],
                scalar1=statc[:, c + 2:c + 3], scalar2=statc[:, c + 4:c + 5],
                op0=ALU.subtract, op1=ALU.mult),
                reads=[xbn, "statc"],
                writes=["xn_tm"] + [f"u{g}" for g in range(DG)],
                tag=f"c{ch}_xn{ts}")

        for ts in range(NTS):
            B.op("tensor", lambda e, ts=ts: e.transpose(
                out=pb[2][0:1, 0:128], in_=stile_s1[:, ts:ts + 1],
                identity=ident[:, :]),
                reads=["stile", "ident", "pb2"], writes=["pb2"], tag=f"c{ch}_trs{ts}")
            B.op("vector", lambda e, ts=ts: e.tensor_copy(
                out=trow_s1[:, ts * 128:(ts + 1) * 128], in_=pb[2][0:1, 0:128]),
                reads=["pb2"], writes=["trow"])
            B.op("tensor", lambda e, ts=ts: e.transpose(
                out=pb[3][0:1, 0:128], in_=stile_rn[:, ts:ts + 1],
                identity=ident[:, :]),
                reads=["stile", "ident", "pb3"], writes=["pb3"], tag=f"c{ch}_trr{ts}")
            B.op("vector", lambda e, ts=ts, pz=pz: e.tensor_copy(
                out=trow_rn[pz][:, ts * 128:(ts + 1) * 128], in_=pb[3][0:1, 0:128]),
                reads=["pb3"], writes=[f"trow_rn{pz}"])
        B.op("tensor", lambda e: e.matmul(
            out=pb[2][:, 0:TC], lhsT=ones_f[:, :], rhs=trow_s1[:, :],
            start=True, stop=True),
            reads=["ones_f", "trow", "pb2"], writes=["pb2"], tag=f"c{ch}_bs1")
        B.op("vector", lambda e: e.tensor_copy(out=s1b_sb[:, :], in_=pb[2][:, 0:TC]),
             reads=["pb2"], writes=["s1b_sb"], tag=f"c{ch}_es1b")
        yield "H"

        for ti in range(8 * NTS):
            ts, dg = divmod(ti, 8)
            bank = ti % 2
            B.op("tensor", lambda e, ts=ts, dg=dg, bank=bank: e.transpose(
                out=pb[bank][:, 0:128],
                in_=xn_tm[:, ts * D + dg * 128:ts * D + (dg + 1) * 128],
                identity=ident[:, :]),
                reads=["xn_tm", "ident", f"pb{bank}"], writes=[f"pb{bank}"],
                tag=f"c{ch}_txn{ts}_{dg}")
            B.op("vector", lambda e, ts=ts, dg=dg, bank=bank: e.scalar_tensor_tensor(
                out=xns_bf[:, dg * TC + ts * 128:dg * TC + (ts + 1) * 128],
                in0=pb[bank][:, 0:128], scalar=g1_s[:, dg:dg + 1],
                in1=s1b_sb[:, ts * 128:(ts + 1) * 128], op0=ALU.mult, op1=ALU.mult),
                reads=[f"pb{bank}", "g1_s", "s1b_sb"], writes=["xns_bf"],
                tag=f"c{ch}_xns{ts}_{dg}")
            B.op("vector", lambda e, ts=ts, dg=dg, bank=bank, xnb=xnb: e.tensor_copy(
                out=xnb[:, dg * TC + ts * 128:dg * TC + (ts + 1) * 128],
                in_=pb[bank][:, 0:128]),
                reads=[f"pb{bank}"], writes=[xnbn], tag=f"c{ch}_exn{ts}_{dg}")
            if ti % 4 == 3:
                yield "H"

        if ch > 0:
            for dg in range(DG):
                o = dg * (TC + 8)
                B.op("vector", lambda e, o=o: e.tensor_copy(
                    out=xin[:, o + 1:o + 8], in_=xin[:, o + TC + 1:o + TC + 8]),
                    reads=["xin"], writes=["xin_head"], tag=f"c{ch}_cvh{dg}")

        if ch == 0:
            for kc in range(8):
                lns = [ld_k(kc, 0), ld_k(kc, 1)]
                for et in range(16):
                    sl, sn, _ = lns[et // 8]
                    B.op("tensor", lambda e, sl=sl, et=et, kc=kc: e.matmul(
                        out=pb[0][:, et:et + 1],
                        lhsT=sl[:, (et % 8) * 128:(et % 8 + 1) * 128],
                        rhs=b1_bf[:, kc:kc + 1],
                        start=(et == 0 and kc == 0), stop=(et == 15 and kc == 7)),
                        reads=[sn, "b1_bf", "pb0"], writes=["pb0"],
                        tag=f"wb1_{et}_{kc}")
                free_lane(lns[0][2])
                free_lane(lns[1][2])
                yield "H"
            B.op("vector", lambda e: e.tensor_copy(out=wb1[:, :], in_=pb[0][:, 0:16]),
                 reads=["pb0"], writes=["wb1"], tag="wb1_ev")
            B.op("vector", lambda e: e.tensor_scalar(
                out=wb1n[:, :], in0=wb1[:, :], scalar1=-1.0, scalar2=None,
                op0=ALU.mult),
                reads=["wb1"], writes=["wb1n"], tag="wb1n")

        def conv_unit(dg):
            o = dg * (TC + 8)
            a = acc[dg % 2]
            an = f"dA{dg % 2}"
            B.op("vector", lambda e, o=o, a=a, dg=dg: e.tensor_scalar(
                out=a[:, :], in0=xin[:, o + 1:o + 1 + TC],
                scalar1=convw_s[:, dg * DCONV:dg * DCONV + 1],
                scalar2=convb_s[:, dg:dg + 1], op0=ALU.mult, op1=ALU.add),
                reads=["xin", "xin_head", "convw_s", "convb_s"], writes=[an],
                tag=f"c{ch}_cv0_{dg}")
            for k in range(1, DCONV):
                B.op("vector", lambda e, o=o, a=a, dg=dg, k=k, an=an:
                     e.scalar_tensor_tensor(
                        out=a[:, :], in0=xin[:, o + 1 + k:o + 1 + k + TC],
                        scalar=convw_s[:, dg * DCONV + k:dg * DCONV + k + 1],
                        in1=a[:, :], op0=ALU.mult, op1=ALU.add),
                     reads=["xin", "xin_head", "convw_s", an], writes=[an],
                     tag=f"c{ch}_cv{k}_{dg}")
            B.op("scalar", lambda e, a=a, dg=dg, an=an: e.activation(
                out=xc[:, dg * TC:(dg + 1) * TC], in_=a[:, :], func=AF.Silu),
                reads=[an], writes=["xc"], tag=f"c{ch}_xc{dg}")

        for rnd in range(2):
            for b in range(4):
                zero_mm(b, f"c{ch}_iz{rnd}_{b}")
            pend = None
            for kc in range(8):
                sl, sn, jl = pend if pend is not None else ld_k(kc, rnd)
                pend = ld_k(kc + 1, rnd) if kc < 7 else None
                for et in range(rnd * 8, rnd * 8 + 8):
                    bank, hh = et % 4, (et // 4) % 2
                    last = (et == rnd * 8 + 4 + bank and kc == 7)
                    B.op("tensor", lambda e, sl=sl, et=et, kc=kc, rnd=rnd,
                         bank=bank, hh=hh, last=last: e.matmul(
                            out=half(bank, hh),
                            lhsT=sl[:, (et - rnd * 8) * 128:
                                    (et - rnd * 8 + 1) * 128],
                            rhs=xns_bf[:, kc * TC:(kc + 1) * TC],
                            start=False, stop=last),
                         reads=[sn, "xns_bf", f"pb{bank}"], writes=[f"pb{bank}"],
                         tag=f"c{ch}_inp{et}_{kc}")
                free_lane(jl)
                if rnd == 1:
                    conv_unit(kc)
                yield "H"
            for et in range(rnd * 8, rnd * 8 + 8):
                bank, hh = et % 4, (et // 4) % 2
                if et < 8:
                    o = et * (TC + 8) + 8
                    B.op("scalar", lambda e, bank=bank, hh=hh, o=o, et=et:
                         e.activation(
                            out=xin[:, o:o + TC], in_=half(bank, hh),
                            func=AF.Identity, bias=wb1[:, et:et + 1]),
                         reads=[f"pb{bank}", "wb1", "xin_head"], writes=["xin"],
                         tag=f"c{ch}_exin{et}")
                else:
                    dg = et - 8
                    B.op("scalar", lambda e, bank=bank, hh=hh, dg=dg, et=et:
                         e.activation(
                            out=sz[:, dg * TC:(dg + 1) * TC], in_=half(bank, hh),
                            func=AF.Silu, bias=wb1[:, et:et + 1]),
                         reads=[f"pb{bank}", "wb1"], writes=["sz"],
                         tag=f"c{ch}_esz{dg}")
            yield "H"

        for dg in range(DG):
            B.op("vector", lambda e, dg=dg: e.tensor_copy(
                out=xc_bf[:, dg * TC:(dg + 1) * TC],
                in_=xc[:, dg * TC:(dg + 1) * TC]),
                reads=["xc"], writes=["xc_bf"], tag=f"c{ch}_xcbf{dg}")
            if dg % 4 == 3:
                yield "Hc"

        for kc in range(8):
            B.op("tensor", lambda e, kc=kc: e.matmul(
                out=pb[2][0:96, 0:TC], lhsT=xpT_bf[:, kc * 96:(kc + 1) * 96],
                rhs=xc_bf[:, kc * TC:(kc + 1) * TC],
                start=(kc == 0), stop=(kc == 7)),
                reads=["xpT_bf", "xc_bf", "pb2"], writes=["pb2"], tag=f"c{ch}_xp{kc}")
        B.op("scalar", lambda e: e.copy(out=dt_bf[:, :], in_=pb[2][0:DTR, 0:TC]),
             reads=["pb2"], writes=["dt_bf"], tag=f"c{ch}_edt")
        B.op("scalar", lambda e: e.copy(out=bc_bf[:, :], in_=pb[2][DTR:96, 0:TC]),
             reads=["pb2"], writes=["bc_bf"], tag=f"c{ch}_ebc")
        yield "Hc"
        for dg in range(DG):
            bank = dg % 2
            B.op("tensor", lambda e, dg=dg, bank=bank: e.matmul(
                out=pb[bank][:, 0:TC], lhsT=dtT_bf[:, dg * 128:(dg + 1) * 128],
                rhs=dt_bf[:, :], start=True, stop=True),
                reads=["dtT_bf", "dt_bf", f"pb{bank}"], writes=[f"pb{bank}"],
                tag=f"c{ch}_dtp{dg}")
            B.op("scalar", lambda e, dg=dg, bank=bank: e.activation(
                out=etmp[:, :], in_=pb[bank][:, 0:TC], func=AF.Exp,
                bias=dtb_s[:, dg:dg + 1]),
                reads=[f"pb{bank}", "dtb_s"], writes=["dA2"], tag=f"c{ch}_sp1_{dg}")
            B.op("scalar", lambda e, dg=dg: e.activation(
                out=delta[:, dg * TC:(dg + 1) * TC], in_=etmp[:, :], func=AF.Ln,
                bias=1.0),
                reads=["dA2", "cc"], writes=[f"delta{dg}"], tag=f"c{ch}_sp2_{dg}")
            if dg % 2 == 1:
                yield "Hc"
        for dg in range(DG):
            B.op("vector", lambda e, dg=dg: e.tensor_mul(
                out=uful[:, dg * TC:(dg + 1) * TC],
                in0=delta[:, dg * TC:(dg + 1) * TC],
                in1=xc[:, dg * TC:(dg + 1) * TC]),
                reads=[f"delta{dg}", "xc"], writes=["xn_tm", f"u{dg}"],
                tag=f"c{ch}_u{dg}")
        if ch == DBG_CH:
            for nm, hdl, tok in [("delta", delta, "delta0"), ("xcbf", xc_bf, "xc_bf"),
                                 ("dtbf", dt_bf, "dt_bf"), ("bcbf", bc_bf, "bc_bf"),
                                 ("xinx", xin, "xin"), ("xnsx", xns_bf, "xns_bf"),
                                 ("ufulx", uful, "xn_tm"), ("xcf", xc, "xc")]:
                dd = nc.dram_tensor(f"dbg_{nm}", [hdl.shape[0], hdl.shape[1]],
                                    hdl.dtype if hasattr(hdl, "dtype") else F32,
                                    kind="ExternalOutput").ap()
                B.dma("gpsimd", lambda e, dd=dd, hdl=hdl: e.dma_start(
                    out=dd, in_=hdl[:, :] if hasattr(hdl, "shape") else hdl),
                    reads=[tok], writes=[f"dbg{nm}"], tag=f"dbg_{nm}")

    def front_scan(ch):
        Builder.ctx = ("front", ch)
        """P4 scan: 2 passes of 4 d-groups; y in banks 0/1, B/C selector
        broadcasts in banks 2/3. ysum matmuls are emitted one state late so
        the tensor engine never blocks on the scan chain."""
        pz = ch % 2
        for p2 in range(2):
            dgs = [p2 * 4 + i for i in range(4)]
            zero_mm(0, f"c{ch}_yz{p2}_0")
            zero_mm(1, f"c{ch}_yz{p2}_1")
            pend = []
            for n in range(NST):
                bB = 2 + n % 2
                bC = 3 - n % 2
                B.op("tensor", lambda e, n=n, bB=bB: e.matmul(
                    out=pb[bB][:, 0:TC], lhsT=selbc_s[:, n * 128:(n + 1) * 128],
                    rhs=bc_bf[:, :], start=True, stop=True),
                    reads=["selbc_s", "bc_bf", f"pb{bB}"], writes=[f"pb{bB}"],
                    tag=f"c{ch}_brep{p2}_{n}")
                B.op("tensor", lambda e, n=n, bC=bC: e.matmul(
                    out=pb[bC][:, 0:TC],
                    lhsT=selbc_s[:, (NST + n) * 128:(NST + n + 1) * 128],
                    rhs=bc_bf[:, :], start=True, stop=True),
                    reads=["selbc_s", "bc_bf", f"pb{bC}"], writes=[f"pb{bC}"],
                    tag=f"c{ch}_crepm{p2}_{n}")
                cr = crep[n % 2]
                crn = f"crep{n % 2}"
                B.op("scalar", lambda e, cr=cr, bC=bC: e.copy(
                    out=cr[:, :], in_=pb[bC][:, 0:TC]),
                    reads=[f"pb{bC}"], writes=[crn], tag=f"c{ch}_ecr{p2}_{n}")
                for i, dg in enumerate(dgs):
                    par = (n * 4 + i) % 3
                    gpar = (n * 4 + i) % 8
                    col = dg * NST + n
                    B.op("scalar", lambda e, dg=dg, n=n, par=par: e.activation(
                        out=dA[par][:, :], in_=delta[:, dg * TC:(dg + 1) * TC],
                        func=AF.Exp,
                        scale=Afm[:, dg * NST + n:dg * NST + n + 1]),
                        reads=[f"delta{dg}", "Afm"], writes=[f"dA{par}"],
                        tag=f"c{ch}_dA{dg}_{n}")
                    B.op("vector", lambda e, dg=dg, par=par, bB=bB: e.tensor_mul(
                        out=dBx[par][:, :], in0=uful[:, dg * TC:(dg + 1) * TC],
                        in1=pb[bB][:, 0:TC]),
                        reads=[f"u{dg}", f"pb{bB}"], writes=[f"dBx{par}"],
                        tag=f"c{ch}_dBx{dg}_{n}")
                    B.op("vector", lambda e, par=par, col=col: e.tensor_tensor_scan(
                        out=hs[par][:, :], data0=dA[par][:, :], data1=dBx[par][:, :],
                        initial=carry[:, col:col + 1], op0=ALU.mult, op1=ALU.add),
                        reads=[f"dA{par}", f"dBx{par}", "carry"],
                        writes=[f"hs{par}"], tag=f"c{ch}_scan{dg}_{n}")
                    B.op("gpsimd", lambda e, par=par, col=col: e.tensor_copy(
                        out=carry[:, col:col + 1], in_=hs[par][:, TC - 1:TC]),
                        reads=[f"hs{par}"], writes=["carry"],
                        tag=f"c{ch}_cw{dg}_{n}")
                    B.op("gpsimd", lambda e, par=par, gpar=gpar, cr=cr, crn=crn:
                         e.tensor_mul(
                            out=gy[gpar][:, :], in0=hs[par][:, :], in1=cr[:, :]),
                         reads=[f"hs{par}", crn], writes=[f"gy{gpar}"],
                         tag=f"c{ch}_gy{dg}_{n}")
                for (gpar_, i_, n_) in pend:
                    B.op("tensor", lambda e, gpar_=gpar_, i_=i_, n_=n_: e.matmul(
                        out=pb[i_ % 2][:, (i_ // 2) * TC:(i_ // 2 + 1) * TC],
                        lhsT=ident_bf[:, :], rhs=gy[gpar_][:, :],
                        start=False, stop=False),
                        reads=["ident_bf", f"gy{gpar_}", f"pb{i_ % 2}"],
                        writes=[f"pb{i_ % 2}"], tag=f"c{ch}_ysum{i_}_{n_}")
                pend = [((n * 4 + i) % 8, i, n) for i in range(4)]
                yield "S"
            for (gpar_, i_, n_) in pend:
                B.op("tensor", lambda e, gpar_=gpar_, i_=i_, n_=n_: e.matmul(
                    out=pb[i_ % 2][:, (i_ // 2) * TC:(i_ // 2 + 1) * TC],
                    lhsT=ident_bf[:, :], rhs=gy[gpar_][:, :],
                    start=False, stop=(i_ >= 2)),
                    reads=["ident_bf", f"gy{gpar_}", f"pb{i_ % 2}"],
                    writes=[f"pb{i_ % 2}"], tag=f"c{ch}_ysumf{i_}")
            for i, dg in enumerate(dgs):
                t1 = t1s[i % 2]
                t1n = f"t1s{i % 2}"
                B.op("vector", lambda e, dg=dg, i=i, t1=t1: e.scalar_tensor_tensor(
                    out=t1[:, :], in0=xc[:, dg * TC:(dg + 1) * TC],
                    scalar=dpar_s[:, dg:dg + 1],
                    in1=pb[i % 2][:, (i // 2) * TC:(i // 2 + 1) * TC],
                    op0=ALU.mult, op1=ALU.add),
                    reads=["xc", "dpar_s", f"pb{i % 2}"], writes=[t1n],
                    tag=f"c{ch}_ym1_{dg}")
                B.op("vector", lambda e, dg=dg, t1=t1, t1n=t1n, pz=pz: e.tensor_mul(
                    out=ymix_bf[pz][:, dg * TC:(dg + 1) * TC], in0=t1[:, :],
                    in1=sz[:, dg * TC:(dg + 1) * TC]),
                    reads=[t1n, "sz"], writes=[f"ymix_bf{pz}"],
                    tag=f"c{ch}_ym2_{dg}")
            yield "S"

    def back(ch):
        Builder.ctx = ("back", ch)
        """P5..P8 for chunk ch on banks 4..7. Yields at unit boundaries."""
        t0 = ch * TC
        xb = x_tm[ch % 2]
        xbn = f"x_tm{ch % 2}"
        pz = ch % 2
        xnb = xn_bf[pz]
        xnbn = f"xn_bf{pz}"
        ymx = ymix_bf[pz]
        ymxn = f"ymix_bf{pz}"

        olanes = [load_lane(ch, 2, kc * 1024) for kc in range(8)]
        if ch == 0:
            for kc in range(8):
                sl, sn, _ = olanes[kc]
                B.op("vector", lambda e, sl=sl, kc=kc: e.reduce_sum(
                    out=wos[:, kc:kc + 1], in_=sl[:, :], axis=AXX),
                    reads=[sn], writes=["wos"], tag=f"wos{kc}")
            B.op("vector", lambda e: e.tensor_copy(out=wos_bf[:, :], in_=wos[:, :]),
                 reads=["wos"], writes=["wos_bf"])
        for dt in range(8):
            bank = 4 + dt % 2
            for kc in range(8):
                sl, sn, _ = olanes[kc]
                B.op("tensor", lambda e, sl=sl, dt=dt, kc=kc, bank=bank:
                     e.matmul(
                        out=pb[bank][:, 0:TC],
                        lhsT=sl[:, dt * 128:(dt + 1) * 128],
                        rhs=ymx[:, kc * TC:(kc + 1) * TC],
                        start=(kc == 0), stop=(kc == 7)),
                     reads=[sn, ymxn, f"pb{bank}"], writes=[f"pb{bank}"],
                     tag=f"c{ch}_op{dt}_{kc}")
            B.op("vector", lambda e, dt=dt, bank=bank, xnb=xnb: e.tensor_add(
                out=x1[:, dt * TC:(dt + 1) * TC],
                in0=xnb[:, dt * TC:(dt + 1) * TC], in1=pb[bank][:, 0:TC]),
                reads=[xnbn, f"pb{bank}"], writes=["x1"], tag=f"c{ch}_x1_{dt}")
            yield
        for _, _, jl in olanes:
            free_lane(jl)
        for kc in range(8):
            B.op("tensor", lambda e, kc=kc: e.matmul(
                out=pb[6][0:1, 0:TC], lhsT=wos_bf[:, kc:kc + 1],
                rhs=ymx[:, kc * TC:(kc + 1) * TC],
                start=(kc == 0), stop=(kc == 7)),
                reads=["wos_bf", ymxn, "pb6"], writes=["pb6"],
                tag=f"c{ch}_smo{kc}")
        yield

        for dg in range(DG):
            B.op("scalar", lambda e, dg=dg: e.activation(
                out=xsq_bf[:, dg * TC:(dg + 1) * TC],
                in_=x1[:, dg * TC:(dg + 1) * TC], func=AF.Square),
                reads=["x1"], writes=[f"hgm{dg // 4}"], tag=f"c{ch}_xsq{dg}")
        for kc in range(8):
            B.op("tensor", lambda e, kc=kc: e.matmul(
                out=pb[7][0:1, 0:TC], lhsT=onec_bf[:, :],
                rhs=xsq_bf[:, kc * TC:(kc + 1) * TC],
                start=(kc == 0), stop=(kc == 7)),
                reads=["onec_bf", f"hgm{kc // 4}", "pb7"], writes=["pb7"],
                tag=f"c{ch}_ssq{kc}")
        B.op("vector", lambda e: e.tensor_scalar(
            out=r_mu2[:, :], in0=pb[6][0:1, 0:TC], scalar1=1.0 / D, scalar2=None,
            op0=ALU.mult),
            reads=["pb6"], writes=["rowb"], tag=f"c{ch}_mu2")
        B.op("vector", lambda e: e.tensor_scalar(
            out=r_tmp[:, :], in0=pb[7][0:1, 0:TC], scalar1=1.0 / D, scalar2=None,
            op0=ALU.mult),
            reads=["pb7"], writes=["rowb"])
        B.op("vector", lambda e: e.scalar_tensor_tensor(
            out=r_rstd2[:, :], in0=r_mu2[:, :], scalar=-1.0, in1=r_mu2[:, :],
            op0=ALU.mult, op1=ALU.mult),
            reads=["rowb"], writes=["rowb"])
        B.op("vector", lambda e: e.tensor_add(
            out=r_rstd2[:, :], in0=r_tmp[:, :], in1=r_rstd2[:, :]),
            reads=["rowb"], writes=["rowb"])
        B.op("scalar", lambda e: e.activation(
            out=r_rstd2[:, :], in_=r_rstd2[:, :], func=AF.Ln, bias=LN_EPS),
            reads=["rowb", "cc"], writes=["rowb"], tag=f"c{ch}_lr2")
        B.op("scalar", lambda e: e.activation(
            out=r_rstd2[:, :], in_=r_rstd2[:, :], func=AF.Exp, scale=-0.5),
            reads=["rowb", "cc"], writes=["rowb"])
        B.op("vector", lambda e: e.scalar_tensor_tensor(
            out=r_m2neg[:, :], in0=r_mu2[:, :], scalar=-1.0, in1=r_rstd2[:, :],
            op0=ALU.mult, op1=ALU.mult),
            reads=["rowb"], writes=["rowb"])
        B.op("tensor", lambda e: e.matmul(
            out=pb[6][:, 0:TC], lhsT=ones_f[:, :], rhs=r_rstd2[:, :],
            start=True, stop=True),
            reads=["ones_f", "rowb", "pb6"], writes=["pb6"], tag=f"c{ch}_br2")
        B.op("vector", lambda e: e.tensor_copy(out=rstd2_sb[:, :], in_=pb[6][:, 0:TC]),
             reads=["pb6"], writes=["rstd2_sb"])
        B.op("tensor", lambda e: e.matmul(
            out=pb[7][:, 0:TC], lhsT=ones_f[:, :], rhs=r_m2neg[:, :],
            start=True, stop=True),
            reads=["ones_f", "rowb", "pb7"], writes=["pb7"], tag=f"c{ch}_bm2")
        B.op("vector", lambda e: e.tensor_copy(out=m2neg_sb[:, :], in_=pb[7][:, 0:TC]),
             reads=["pb7"], writes=["m2neg_sb"])
        for dg in range(DG):
            B.op("vector", lambda e, dg=dg: e.scalar_tensor_tensor(
                out=etmp2[:, :], in0=x1[:, dg * TC:(dg + 1) * TC],
                scalar=g2_s[:, dg:dg + 1], in1=rstd2_sb[:, :],
                op0=ALU.mult, op1=ALU.mult),
                reads=["x1", "g2_s", "rstd2_sb"], writes=["etmp2"],
                tag=f"c{ch}_lnp1_{dg}")
            B.op("vector", lambda e, dg=dg: e.scalar_tensor_tensor(
                out=lnp_bf[:, dg * TC:(dg + 1) * TC], in0=m2neg_sb[:, :],
                scalar=g2_s[:, dg:dg + 1], in1=etmp2[:, :],
                op0=ALU.mult, op1=ALU.add),
                reads=["m2neg_sb", "g2_s", "etmp2"], writes=["lnp_bf"],
                tag=f"c{ch}_lnp2_{dg}")
        yield

        B.op("tensor", lambda e, pz=pz: e.matmul(
            out=pb[4][0:8, 0:TC], lhsT=ones_f[:, 0:8], rhs=trow_rn[pz][:, :],
            start=True, stop=True),
            reads=["ones_f", f"trow_rn{pz}", "pb4"], writes=["pb4"],
            tag=f"c{ch}_brn")
        for kc in range(8):
            B.op("tensor", lambda e, kc=kc, xnb=xnb: e.matmul(
                out=pb[5][0:8, 0:TC], lhsT=cw_bf[:, kc * 16:kc * 16 + 8],
                rhs=xnb[:, kc * TC:(kc + 1) * TC],
                start=(kc == 0), stop=False),
                reads=["cw_bf", xnbn, "pb5"], writes=["pb5"], tag=f"c{ch}_rt{kc}")
            B.op("tensor", lambda e, kc=kc, xnb=xnb: e.matmul(
                out=pb[5][0:8, TC:2 * TC], lhsT=cw_bf[:, kc * 16 + 8:kc * 16 + 16],
                rhs=xnb[:, kc * TC:(kc + 1) * TC],
                start=False, stop=(kc == 7)),
                reads=["cw_bf", xnbn, "pb5"], writes=["pb5"], tag=f"c{ch}_rw{kc}")
        B.op("vector", lambda e: e.tensor_scalar(
            out=ltrow[:, :], in0=pb[5][0:8, 0:TC], scalar1=rc_s[:, 2:3],
            scalar2=None, op0=ALU.mult),
            reads=["pb5", "rc_s"], writes=["ltrow"], tag=f"c{ch}_lg1")
        B.op("vector", lambda e: e.tensor_mul(
            out=ltrow[:, :], in0=ltrow[:, :], in1=pb[4][0:8, 0:TC]),
            reads=["ltrow", "pb4"], writes=["ltrow"])
        B.op("vector", lambda e: e.tensor_add(
            out=ltrow[:, :], in0=ltrow[:, :], in1=pb[5][0:8, TC:2 * TC]),
            reads=["ltrow", "pb5"], writes=["ltrow"])
        for ts in range(NTS):
            bank = 6 + ts % 2
            B.op("tensor", lambda e, ts=ts, bank=bank: e.transpose(
                out=pb[bank][0:128, 0:8], in_=ltrow[:, ts * 128:(ts + 1) * 128],
                identity=ident[0:8, 0:8]),
                reads=["ltrow", "ident", f"pb{bank}"], writes=[f"pb{bank}"],
                tag=f"c{ch}_lt{ts}")
            B.op("vector", lambda e, ts=ts, bank=bank: e.tensor_copy(
                out=lt_tm[:, ts * 8:(ts + 1) * 8], in_=pb[bank][0:128, 0:8]),
                reads=[f"pb{bank}"], writes=["lt_tm"])
            B.op("vector", lambda e, ts=ts: e.max(
                out=gmax[:, ts * 8:(ts + 1) * 8], in_=lt_tm[:, ts * 8:(ts + 1) * 8]),
                reads=["lt_tm"], writes=["gmax"], tag=f"c{ch}_gmx{ts}")
            B.op("vector", lambda e, ts=ts: e.tensor_scalar(
                out=gmsk[:, ts * 8:(ts + 1) * 8], in0=lt_tm[:, ts * 8:(ts + 1) * 8],
                scalar1=gmax[:, ts * 8 + 1:ts * 8 + 2], scalar2=None,
                op0=ALU.is_ge),
                reads=["lt_tm", "gmax"], writes=["gmsk"])
            B.op("vector", lambda e, ts=ts: e.tensor_scalar(
                out=gneg[:, ts:ts + 1], in0=gmax[:, ts * 8:ts * 8 + 1],
                scalar1=-1.0, scalar2=None, op0=ALU.mult),
                reads=["gmax"], writes=["gneg"])
            B.op("scalar", lambda e, ts=ts: e.activation(
                out=gwrk[:, ts * 8:(ts + 1) * 8], in_=lt_tm[:, ts * 8:(ts + 1) * 8],
                func=AF.Exp, bias=gneg[:, ts:ts + 1]),
                reads=["lt_tm", "gneg"], writes=["gwrk"], tag=f"c{ch}_gex{ts}")
            B.op("vector", lambda e, ts=ts: e.tensor_mul(
                out=gwrk[:, ts * 8:(ts + 1) * 8], in0=gwrk[:, ts * 8:(ts + 1) * 8],
                in1=gmsk[:, ts * 8:(ts + 1) * 8]),
                reads=["gwrk", "gmsk"], writes=["gwrk"])
            B.op("vector", lambda e, ts=ts: e.reduce_sum(
                out=gden[:, ts:ts + 1], in_=gwrk[:, ts * 8:(ts + 1) * 8], axis=AXX),
                reads=["gwrk"], writes=["gden"])
            B.op("vector", lambda e, ts=ts: e.reciprocal(
                out=gden[:, ts:ts + 1], in_=gden[:, ts:ts + 1]),
                reads=["gden"], writes=["gden"])
            B.op("vector", lambda e, ts=ts: e.tensor_scalar(
                out=gwrk[:, ts * 8:(ts + 1) * 8], in0=gwrk[:, ts * 8:(ts + 1) * 8],
                scalar1=gden[:, ts:ts + 1], scalar2=None, op0=ALU.mult),
                reads=["gwrk", "gden"], writes=["gwrk"], tag=f"c{ch}_gnm{ts}")
            B.op("tensor", lambda e, ts=ts, bank=bank: e.transpose(
                out=pb[bank][0:8, 0:128], in_=gwrk[:, ts * 8:(ts + 1) * 8],
                identity=ident[:, :]),
                reads=["gwrk", "ident", f"pb{bank}"], writes=[f"pb{bank}"],
                tag=f"c{ch}_gtr{ts}")
            B.op("scalar", lambda e, ts=ts, bank=bank: e.copy(
                out=gT_bf[:, ts * 128:(ts + 1) * 128], in_=pb[bank][0:8, 0:128]),
                reads=[f"pb{bank}"], writes=["gT_bf"])
        yield

        grepsb = ymix_bf[pz]
        grepsn = ymxn
        for ei in range(NE):
            mlanes = [load_lane(ch, 3 + ei, kb * 1024) for kb in range(4)]
            B.op("tensor", lambda e, ei=ei: e.matmul(
                out=pb[6][:, 0:TC], lhsT=selg_s[:, ei * 128:(ei + 1) * 128],
                rhs=gT_bf[:, :], start=True, stop=True),
                reads=["selg_s", "gT_bf", "pb6"], writes=["pb6"],
                tag=f"c{ch}_grep{ei}")
            B.op("scalar", lambda e, ei=ei, grepsb=grepsb: e.copy(
                out=grepsb[:, ei * TC:(ei + 1) * TC], in_=pb[6][:, 0:TC]),
                reads=["pb6"], writes=[grepsn], tag=f"c{ch}_egrep{ei}")
            for ht in range(4):
                hb = 4 + ht % 2
                for kc in range(8):
                    sl, sn, _ = mlanes[kc // 2]
                    B.op("tensor", lambda e, sl=sl, ht=ht, kc=kc,
                         hb=hb, xnb=xnb: e.matmul(
                            out=pb[hb][:, 0:TC],
                            lhsT=sl[:, (kc % 2) * 512 + ht * 128:
                                    (kc % 2) * 512 + (ht + 1) * 128],
                            rhs=xnb[:, kc * TC:(kc + 1) * TC],
                            start=(kc == 0), stop=(kc == 7)),
                         reads=[sn, xnbn, f"pb{hb}"], writes=[f"pb{hb}"],
                         tag=f"c{ch}_mh{ei}_{ht}_{kc}")
                hi = ei * 4 + ht
                B.op("scalar", lambda e, ei=ei, ht=ht, hb=hb, hi=hi:
                     e.activation(
                        out=hgm[:, hi * TC:(hi + 1) * TC], in_=pb[hb][:, 0:TC],
                        func=AF.Identity,
                        bias=mb1_s[:, ei * 4 + ht:ei * 4 + ht + 1]),
                     reads=[f"pb{hb}", "mb1_s"], writes=[f"hgm{hi // 4}"],
                     tag=f"c{ch}_mg{ei}_{ht}")
                yield
            for _, _, jl in mlanes:
                free_lane(jl)
        for j in range(16):
            B.op("scalar", lambda e, j=j: e.activation(
                out=hgm[:, j * 512:(j + 1) * 512],
                in_=hgm[:, j * 512:(j + 1) * 512], func=AF.Gelu),
                reads=[f"hgm{j // 2}"], writes=[f"hgm{j // 2}"],
                tag=f"c{ch}_gelu{j}")
        for hi in range(32):
            ei = hi // 4
            B.op("vector", lambda e, hi=hi, ei=ei, grepsb=grepsb: e.tensor_mul(
                out=hgm[:, hi * TC:(hi + 1) * TC],
                in0=hgm[:, hi * TC:(hi + 1) * TC],
                in1=grepsb[:, ei * TC:(ei + 1) * TC]),
                reads=[f"hgm{hi // 4}", grepsn], writes=[f"hgm{hi // 4}"],
                tag=f"c{ch}_mhg{hi}")
        yield

        for b in range(4):
            zero_mm(4 + b, f"c{ch}_mz{b}")
        m2p = None
        for ei in range(NE):
            for ht in range(4):
                hi = ei * 4 + ht
                sl, sn, jl = (m2p if m2p is not None
                              else load_lane(ch, 3 + ei, 4096 + ht * 1024))
                nei, nht = (ei, ht + 1) if ht < 3 else (ei + 1, 0)
                m2p = (load_lane(ch, 3 + nei, 4096 + nht * 1024)
                       if nei < NE else None)
                for dt in range(8):
                    bank, hh = 4 + dt % 4, dt // 4
                    B.op("tensor", lambda e, sl=sl, ht=ht, dt=dt,
                         bank=bank, hh=hh, hi=hi: e.matmul(
                            out=half(bank, hh),
                            lhsT=sl[:, dt * 128:(dt + 1) * 128],
                            rhs=hgm[:, hi * TC:(hi + 1) * TC],
                            start=False, stop=False),
                         reads=[sn, f"hgm{hi // 4}", f"pb{bank}"],
                         writes=[f"pb{bank}"],
                         tag=f"c{ch}_mo{ei}_{ht}_{dt}")
                free_lane(jl)
                yield
        for dt in range(8):
            bank, hh = 4 + dt % 4, dt // 4
            B.op("tensor", lambda e, dt=dt, bank=bank, hh=hh: e.matmul(
                out=half(bank, hh), lhsT=mb2_bf[:, dt * 128:(dt + 1) * 128],
                rhs=gT_bf[:, :], start=False, stop=(dt >= 4)),
                reads=["mb2_bf", "gT_bf", f"pb{bank}"], writes=[f"pb{bank}"],
                tag=f"c{ch}_mb2_{dt}")
        for dt in range(8):
            bank, hh = 4 + dt % 4, dt // 4
            B.op("scalar", lambda e, dt=dt, bank=bank, hh=hh: e.activation(
                out=gfin[:, dt * TC:(dt + 1) * TC], in_=half(bank, hh),
                func=AF.Silu),
                reads=[f"pb{bank}"], writes=["gfin"], tag=f"c{ch}_gfin{dt}")
        yield

        if ch == 0:
            for hg in range(8):
                fblanes = [load_lane(ch, 11 + hg, kb * 1024) for kb in range(4)]
                for htl in range(4):
                    ht = hg * 4 + htl
                    for kc in range(8):
                        sl, sn, _ = fblanes[kc // 2]
                        B.op("tensor", lambda e, sl=sl, htl=htl,
                             kc=kc, ht=ht: e.matmul(
                                out=pb[7][:, ht:ht + 1],
                                lhsT=sl[:, (kc % 2) * 512 + htl * 128:
                                        (kc % 2) * 512 + (htl + 1) * 128],
                                rhs=b2ln_bf[:, kc:kc + 1],
                                start=(ht == 0 and kc == 0),
                                stop=(ht == 31 and kc == 7)),
                             reads=[sn, "b2ln_bf", "pb7"], writes=["pb7"],
                             tag=f"wb2f_{ht}_{kc}")
                for _, _, jl in fblanes:
                    free_lane(jl)
            B.op("vector", lambda e: e.tensor_add(
                out=wb2f[:, :], in0=pb[7][:, 0:32], in1=b1f_s[:, :]),
                reads=["pb7", "b1f_s"], writes=["wb2f"], tag="wb2f_ev")
        for hg in range(8):
            f1lanes = [load_lane(ch, 11 + hg, kb * 1024) for kb in range(4)]
            for htl in range(4):
                ht = hg * 4 + htl
                hb = 4 + ht % 2
                for kc in range(8):
                    sl, sn, _ = f1lanes[kc // 2]
                    B.op("tensor", lambda e, sl=sl, htl=htl, kc=kc,
                         hb=hb: e.matmul(
                            out=pb[hb][:, 0:TC],
                            lhsT=sl[:, (kc % 2) * 512 + htl * 128:
                                    (kc % 2) * 512 + (htl + 1) * 128],
                            rhs=lnp_bf[:, kc * TC:(kc + 1) * TC],
                            start=(kc == 0), stop=(kc == 7)),
                         reads=[sn, "lnp_bf", f"pb{hb}"], writes=[f"pb{hb}"],
                         tag=f"c{ch}_f1_{hg}_{htl}_{kc}")
                hp = htl % 2
                B.op("scalar", lambda e, hb=hb, hp=hp, ht=ht: e.activation(
                    out=hgm[:, ht * TC:(ht + 1) * TC], in_=pb[hb][:, 0:TC],
                    func=AF.Relu, bias=wb2f[:, ht:ht + 1]),
                    reads=[f"pb{hb}", "wb2f"], writes=[f"hgm{ht // 4}"],
                    tag=f"c{ch}_frl{ht}")
                yield
            for _, _, jl in f1lanes:
                free_lane(jl)

        for b in range(4):
            zero_mm(4 + b, f"c{ch}_fz{b}")
        f2p = None
        for hg in range(8):
            for htl in range(4):
                ht = hg * 4 + htl
                sl, sn, jl = (f2p if f2p is not None
                              else load_lane(ch, 11 + hg, 4096 + htl * 1024))
                nhg, nhtl = (hg, htl + 1) if htl < 3 else (hg + 1, 0)
                f2p = (load_lane(ch, 11 + nhg, 4096 + nhtl * 1024)
                       if nhg < 8 else None)
                for dt in range(8):
                    bank, hh = 4 + dt % 4, dt // 4
                    lastf2 = (hg == 7 and htl == 3 and dt >= 4)
                    B.op("tensor", lambda e, sl=sl, htl=htl, dt=dt,
                         bank=bank, hh=hh, ht=ht, lastf2=lastf2: e.matmul(
                            out=half(bank, hh),
                            lhsT=sl[:, dt * 128:(dt + 1) * 128],
                            rhs=hgm[:, ht * TC:(ht + 1) * TC],
                            start=False, stop=lastf2),
                         reads=[sn, f"hgm{ht // 4}", f"pb{bank}"],
                         writes=[f"pb{bank}"],
                         tag=f"c{ch}_f2_{hg}_{htl}_{dt}")
                free_lane(jl)
                yield
        for dt in range(8):
            bank, hh = 4 + dt % 4, dt // 4
            B.op("vector", lambda e, dt=dt, bank=bank, hh=hh: e.scalar_tensor_tensor(
                out=sS[:, dt * TC:(dt + 1) * TC], in0=half(bank, hh),
                scalar=b2f_s[:, dt:dt + 1], in1=x1[:, dt * TC:(dt + 1) * TC],
                op0=ALU.add, op1=ALU.add),
                reads=[f"pb{bank}", "b2f_s", "x1"], writes=["sS"],
                tag=f"c{ch}_sf{dt}")
            B.op("vector", lambda e, dt=dt: e.tensor_mul(
                out=sS[:, dt * TC:(dt + 1) * TC], in0=sS[:, dt * TC:(dt + 1) * TC],
                in1=gfin[:, dt * TC:(dt + 1) * TC]),
                reads=["sS", "gfin"], writes=["sS"], tag=f"c{ch}_sg{dt}")
        yield

        for ti in range(8 * NTS):
            ts, dg = divmod(ti, 8)
            bank = 6 + (ti % 2)
            B.op("tensor", lambda e, ts=ts, dg=dg, bank=bank: e.transpose(
                out=pb[bank][:, 0:128],
                in_=sS[:, dg * TC + ts * 128:dg * TC + (ts + 1) * 128],
                identity=ident[:, :]),
                reads=["sS", "ident", f"pb{bank}"], writes=[f"pb{bank}"],
                tag=f"c{ch}_tsg{ts}_{dg}")
            B.op("vector", lambda e, ts=ts, dg=dg, bank=bank, xb=xb: e.tensor_add(
                out=xb[:, ts * D + dg * 128:ts * D + (dg + 1) * 128],
                in0=xb[:, ts * D + dg * 128:ts * D + (dg + 1) * 128],
                in1=pb[bank][:, 0:128]),
                reads=[xbn, f"pb{bank}"], writes=[xbn],
                tag=f"c{ch}_oadd{ts}_{dg}")
            if ti % 4 == 3:
                yield
        B.dma("gpsimd", lambda e, t0=t0, xb=xb: e.dma_start(
            out=out_d[t0:t0 + TC, :].rearrange("(s p) d -> p s d", p=128),
            in_=xb[:, :]),
            reads=[xbn], writes=["out_d"], tag=f"c{ch}_st")
        yield

    def run_gen(g):
        for _ in g:
            pass

    def chain(*gens):
        for g in gens:
            for v in g:
                yield v

    if PIPELINE:
        prev_back = repack_gen()
        rgen_cell[0] = prev_back
        for ch in range(nchunks):
            fgen = chain(front_head(ch), front_scan(ch))
            if prev_back is None:
                run_gen(fgen)
            else:
                bgen = prev_back
                done_f = done_b = False
                budget = 0.0
                while not (done_f and done_b):
                    ph = None
                    if not done_f:
                        Builder.ctx = ("front", ch)
                        ph = next(fgen, "END")
                        done_f = ph == "END"
                    budget += 1e9 if done_f else {"S": 3.4, "H": 2.0,
                                  "Hc": 0.5}.get(ph, 1.0)
                    while budget >= 1.0 and not done_b:
                        Builder.ctx = ("back", ch - 1)
                        done_b = next(bgen, "END") == "END"
                        budget -= 1.0
            prev_back = back(ch)
        run_gen(prev_back)
    else:
        run_gen(repack_gen())
        for ch in range(nchunks):
            run_gen(chain(front_head(ch), front_scan(ch)))
            run_gen(back(ch))

    B.drain_all("sync")
    return nc, B


def _run(nc, B, in_maps, core_ids):
    emit = B.finalize(nc)
    with nc.Block() as block:
        block.sync(lambda e: emit("sync", e))
        block.tensor(lambda e: emit("tensor", e))
        block.vector(lambda e: emit("vector", e))
        block.scalar(lambda e: emit("scalar", e))
        block.gpsimd(lambda e: emit("gpsimd", e))
    return run_bass_kernel_spmd(nc, in_maps, core_ids=core_ids)


def _mk_sel(n):
    import ml_dtypes
    s = np.zeros((n, n, 128), np.float32)
    for k in range(n):
        s[k, k, :] = 1.0
    return s.reshape(n, n * 128).astype(ml_dtypes.bfloat16)


def _mk_wpf(inputs):
    f = np.float32

    def blkT(a, nb):
        C = a.shape[1]
        return np.ascontiguousarray(a).reshape(nb, 128, C).transpose(
            1, 0, 2).reshape(128, nb * C)

    inwT = inputs["L0_attn_in_proj_w"].T.astype(f)
    outwT = inputs["L0_attn_out_proj_w"].T.astype(f)
    w1T = inputs["L0_attn_w1"].T.astype(f)
    w2T = inputs["L0_attn_w2"].T.astype(f)
    mw1 = np.asarray(inputs["L0_ffn_w1"], f)
    mw2 = np.asarray(inputs["L0_ffn_w2"], f)
    wpf = np.empty((NPIECE, 128, PCOLS), f)
    for ph in range(2):
        wpf[ph] = blkT(inwT[ph * 512:(ph + 1) * 512], 4)
    wpf[2] = blkT(outwT, 8)
    for ei in range(NE):
        wpf[3 + ei, :, 0:4096] = blkT(mw1[ei], 8)
        wpf[3 + ei, :, 4096:8192] = blkT(mw2[ei], 4)
    for hg in range(8):
        wpf[11 + hg, :, 0:4096] = blkT(w1T[:, hg * 512:(hg + 1) * 512], 8)
        wpf[11 + hg, :, 4096:8192] = blkT(w2T[hg * 512:(hg + 1) * 512], 4)
    return wpf


def make_common(inputs):
    cv = lambda k: np.ascontiguousarray(inputs[k], dtype=np.float32)
    return {
        "wpf": _mk_wpf(inputs), "convw": cv("L0_attn_conv_w"),
        "convb": cv("L0_attn_conv_b"), "xpw": cv("L0_attn_x_proj_w"),
        "dtw": cv("L0_attn_dt_proj_w"), "dtb": cv("L0_attn_dt_proj_b"),
        "alog": cv("L0_attn_A_log"), "dpar": cv("L0_attn_D_param"),
        "g1": cv("L0_attn_ln1_g"),
        "b1": cv("L0_attn_ln1_b"), "g2": cv("L0_attn_ln2_g"),
        "b2": cv("L0_attn_ln2_b"),
        "b1f": cv("L0_attn_b1"),
        "b2f": cv("L0_attn_b2"),
        "mb1": cv("L0_ffn_b1"),
        "mb2": cv("L0_ffn_b2"), "cent": cv("L0_ffn_centroids"),
        "wrt": cv("L0_ffn_w_route"),
        "ident": np.eye(128, dtype=np.float32),
        "consts": np.tile(np.array(CONSTS, dtype=np.float32), (128, 1)),
        "selbc": _mk_sel(32),
        "selg": _mk_sel(8),
    }


def kernel(**inputs):
    x_full = np.ascontiguousarray(inputs["L0_attn_x"], dtype=np.float32)
    common = make_common(inputs)
    nc, B = build_nc()
    in_maps = []
    for c in range(8):
        m = dict(common)
        m["x"] = np.ascontiguousarray(x_full[c])
        in_maps.append(m)
    res = _run(nc, B, in_maps, list(range(8)))
    out = np.stack([np.asarray(res.results[c]["out"]) for c in range(8)], axis=0)
    return out.astype(np.float32)


# version 20
# speedup vs baseline: 1.0177x; 1.0177x over previous
import numpy as np
import concourse.bass as bass
import concourse.mybir as mybir
from concourse.bass_utils import run_bass_kernel_spmd

F32 = mybir.dt.float32
BF16 = mybir.dt.bfloat16
AF = mybir.ActivationFunctionType
ALU = mybir.AluOpType
AXX = mybir.AxisListType.X

D = 1024
T = 4096
TC = 256
NCH = T // TC
NTS = TC // 128
DG = 8
NST = 16
DCONV = 8
DTR = 64
HF = 4096
NE = 8
DE = 512
NPIECE = 19
PCOLS = 8192
LN_EPS = 1e-5
PIPELINE = True
DBG_CH = -1
CONSTS = [1e-24, 1e-5, 1.0, 1e-30, -0.5, 1024.0, 0.0, 2.0, -1.0]

ENGINES = ("sync", "tensor", "vector", "scalar", "gpsimd")


NLANES = 12
ENG_SKIP_DIST = {"tensor": 0, "vector": 10, "scalar": 10 ** 9, "gpsimd": 6,
                 "sync": 10 ** 9}


class Builder:

    def __init__(self):
        self.ops = []
        self.bufs = {}
        self.lane_ctr = {}
        self.lane_last = {}
        self.eng_seq = {}

    def _buf(self, name):
        b = self.bufs.get(name)
        if b is None:
            b = {"w": {}, "r": {}}
            self.bufs[name] = b
        return b

    ctx = ("init", -1)

    def op(self, eng, fn, reads=(), writes=(), dma=False, tag="", late=False):
        i = len(self.ops)
        if dma:
            q = "q_" + eng
            cnt = self.lane_ctr.get(q, 0)
            self.lane_ctr[q] = cnt + 1
            lane = cnt % NLANES
            ekey = f"{q}_{lane}"
        else:
            ekey = eng
        deps = {}

        myseq = self.eng_seq.get(ekey, 0)

        def dep(m):
            for k, r in m.items():
                if (k == ekey and not dma and not self.ops[r].get("late")
                        and myseq - self.ops[r]["eseq"] > ENG_SKIP_DIST[eng]):
                    continue
                if deps.get(k, -1) < r:
                    deps[k] = r

        for nm in reads:
            dep(self._buf(nm)["w"])
        for nm in writes:
            b = self._buf(nm)
            dep(b["w"])
            dep(b["r"])
        if dma:
            prev = self.lane_last.get(ekey)
            if prev is not None and deps.get(ekey, -1) < prev:
                deps[ekey] = prev
            self.lane_last[ekey] = i
        for nm in reads:
            b = self._buf(nm)
            if b["r"].get(ekey, -1) < i:
                b["r"][ekey] = i
        for nm in writes:
            b = self._buf(nm)
            b["w"] = dict(b["w"])
            b["w"][ekey] = i
            b["r"] = {}
        self.ops.append(dict(eng=eng, ekey=ekey, fn=fn, deps=deps, dma=dma,
                             tag=tag, need_inc=False, rank=None, late=late,
                             eseq=myseq, ctx=Builder.ctx, rd=tuple(reads),
                             wr=tuple(writes)))
        self.eng_seq[ekey] = myseq + 1
        return i

    def dma(self, eng, fn, reads=(), writes=(), tag=""):
        return self.op(eng, fn, reads, writes, dma=True, tag=tag)

    def drain_all(self, eng):
        deps = {ek: i for ek, i in self.lane_last.items()}
        i = len(self.ops)
        self.ops.append(dict(eng=eng, ekey=eng, fn=lambda e: e.nop(),
                             deps=deps, dma=False, tag="final_drain",
                             need_inc=False, rank=None))
        return i

    def finalize(self, nc):
        for o in self.ops:
            for ek, r in o["deps"].items():
                self.ops[r]["need_inc"] = True
        for o in self.ops:
            if o["dma"]:
                o["need_inc"] = True
        counts = {}
        for o in self.ops:
            if o["need_inc"]:
                counts[o["ekey"]] = counts.get(o["ekey"], 0) + (16 if o["dma"] else 1)
                o["rank"] = counts[o["ekey"]]
        self.final_counts = counts
        sems = {ek: nc.alloc_semaphore("sem_" + ek) for ek in counts}
        self.sems = sems
        per_eng = {e: [] for e in ENGINES}
        for o in self.ops:
            per_eng[o["eng"]].append(o)
        waited = {e: {} for e in ENGINES}

        def emit(eng_name, eng):
            for o in per_eng[eng_name]:
                for ek, r in sorted(o["deps"].items()):
                    dep = self.ops[r]
                    val = dep["rank"]
                    assert val is not None, (o["tag"], dep["tag"])
                    if waited[eng_name].get(ek, 0) < val:
                        eng.wait_ge(sems[ek], val)
                        waited[eng_name][ek] = val
                inst = o["fn"](eng)
                if o["need_inc"]:
                    inst.then_inc(sems[o["ekey"]], 16 if o["dma"] else 1)
        return emit


def build_nc(nchunks=NCH, debug=()):
    nc = bass.Bass("TRN2", target_bir_lowering=False)
    B = Builder()

    def din(name, shape):
        return nc.dram_tensor(name, shape, F32, kind="ExternalInput").ap()

    x_d = din("x", [T, D])
    wpf_d = din("wpf", [NPIECE, 128, PCOLS])
    convw_d = din("convw", [D, DCONV])
    convb_d = din("convb", [D])
    xpw_d = din("xpw", [96, D])
    dtw_d = din("dtw", [D, DTR])
    dtb_d = din("dtb", [D])
    alog_d = din("alog", [D, NST])
    dpar_d = din("dpar", [D])
    g1_d = din("g1", [D])
    b1_d = din("b1", [D])
    g2_d = din("g2", [D])
    b2_d = din("b2", [D])
    b1f_d = din("b1f", [HF])
    b2f_d = din("b2f", [D])
    mb1_d = din("mb1", [NE, DE])
    mb2_d = din("mb2", [NE, D])
    cent_d = din("cent", [NE, D])
    wrt_d = din("wrt", [NE, D])
    ident_d = din("ident", [128, 128])
    consts_d = din("consts", [128, len(CONSTS)])
    selbc_d = nc.dram_tensor("selbc", [32, 32 * 128], BF16, kind="ExternalInput").ap()
    selg_d = nc.dram_tensor("selg", [8, 8 * 128], BF16, kind="ExternalInput").ap()

    out_d = nc.dram_tensor("out", [T, D], F32, kind="ExternalOutput").ap()

    wpieces = nc.dram_tensor("wpieces", [NPIECE, 128, PCOLS], BF16).ap()

    def S(name, shape, dt=F32, **kw):
        return nc.alloc_sbuf_tensor(name, shape, dt, **kw)

    x_tm = [S(f"x_tm{i}", [128, NTS * D]) for i in range(2)]
    xn_tm = S("xn_tm", [128, NTS * D])
    xn_bf = [S(f"xn_bf{i}", [128, DG * TC], BF16) for i in range(2)]
    xns_bf = S("xns_bf", [128, DG * TC], BF16)
    s1b_sb = S("s1b_sb", [128, TC])
    rstd2_sb = S("rstd2_sb", [128, TC])
    m2neg_sb = S("m2neg_sb", [128, TC])
    ones_f = S("ones_f", [1, 128])
    xin = S("xin", [128, DG * (TC + 8)])
    sz = S("sz", [128, DG * TC])
    xc = S("xc", [128, DG * TC])
    xc_bf = S("xc_bf", [128, DG * TC], BF16)
    delta = S("delta", [128, DG * TC])
    uful = xn_tm
    dA = [S(f"dA{i}", [128, TC]) for i in range(3)]
    dBx = [S(f"dBx{i}", [128, TC]) for i in range(3)]
    hs = [S(f"hs{i}", [128, TC]) for i in range(3)]
    gy = [S(f"gy{i}", [128, TC], BF16) for i in range(8)]
    crep = [S(f"crep{i}", [128, TC]) for i in range(2)]
    carry = S("carry", [128, DG * NST])
    t1s = [S(f"t1s{i}", [128, TC]) for i in range(2)]
    ymix_bf = [S(f"ymix_bf{i}", [128, DG * TC], BF16) for i in range(2)]
    x1 = S("x1", [128, DG * TC])
    hgm = S("hgm", [128, 32 * TC], BF16)
    xsq_bf = hgm[:, 0:DG * TC]
    lnp_bf = S("lnp_bf", [128, DG * TC], BF16)
    sS = S("sS", [128, DG * TC])
    gfin = S("gfin", [128, DG * TC], BF16)
    brep_sb = [S(f"brep{i}", [128, TC]) for i in range(2)]
    etmp2 = S("etmp2", [128, TC])
    acc = [dA[0], dA[1]]
    etmp = dA[2]
    slots = [S(f"slot{i}", [128, PCOLS], BF16, is_transpose_dest=True)
             for i in range(2)]
    rep_f = x_tm

    statc = S("statc", [128, 16])
    stile_s1 = S("stile_s1", [128, NTS])
    stile_rn = S("stile_rn", [128, NTS])
    trow_s1 = S("trow_s1", [1, TC])
    trow_rn = [S(f"trow_rn{i}", [1, TC]) for i in range(2)]
    r_mu2 = S("r_mu2", [1, TC])
    r_rstd2 = S("r_rstd2", [1, TC])
    r_m2neg = S("r_m2neg", [1, TC])
    r_tmp = S("r_tmp", [1, TC])
    dt_bf = S("dt_bf", [DTR, TC], BF16)
    bc_bf = S("bc_bf", [2 * NST, TC], BF16)
    selbc_s = S("selbc_s", [32, 32 * 128], BF16)
    selg_s = S("selg_s", [8, 8 * 128], BF16)
    ltrow = S("ltrow", [8, TC])
    lt_tm = S("lt_tm", [128, 8 * NTS])
    gmax = S("gmax", [128, 8 * NTS])
    gwrk = S("gwrk", [128, 8 * NTS])
    gmsk = S("gmsk", [128, 8 * NTS])
    gneg = S("gneg", [128, NTS])
    gden = S("gden", [128, NTS])
    gT_bf = S("gT_bf", [8, TC], BF16)
    ident = S("identsb", [128, 128])
    cc = S("cc", [128, len(CONSTS)])
    ident_bf = S("ident_bf", [128, 128], BF16)
    ones_bf = S("ones_bf", [1, 128], BF16)
    zrow_bf = S("zrow_bf", [1, 512], BF16)
    onec_bf = S("onec_bf", [128, 1], BF16)
    Afm = S("Afm", [128, DG * NST])
    convw_s = S("convw_s", [128, DG * DCONV])
    convb_s = S("convb_s", [128, DG])
    dtb_s = S("dtb_s", [128, DG])
    dpar_s = S("dpar_s", [128, DG])
    g1_s = S("g1_s", [128, DG])
    g2_s = S("g2_s", [128, DG])
    b2f_s = S("b2f_s", [128, DG])
    b1_f = S("b1_f", [128, DG])
    b2_f = S("b2_f", [128, DG])
    b1_bf = S("b1_bf", [128, DG], BF16)
    b2ln_bf = S("b2ln_bf", [128, DG], BF16)
    wb1 = S("wb1", [128, 16])
    wb1n = S("wb1n", [128, 16])
    wb2f = S("wb2f", [128, 32])
    b1f_s = S("b1f_s", [128, 32])
    mb1_s = S("mb1_s", [128, NE * 4])
    mb2_bf = S("mb2_bf", [8, D], BF16)
    cw_bf = S("cw_bf", [128, DG * 16], BF16)
    rc_s = S("rc_s", [8, 4])
    xpT_bf = S("xpT_bf", [128, DG * 96], BF16)
    dtT_bf = S("dtT_bf", [DTR, DG * 128], BF16)
    wos = S("wos", [128, DG])
    wos_bf = S("wos_bf", [128, DG], BF16)

    csq = xn_tm[0:8, 0:1024]
    mb2_f = xn_tm[32:40, 0:1024]
    dtw_s = xn_tm[0:128, 1024:1536]
    xpw_s = x1[0:96, 0:1024]
    cent_s = x1[0:8, 1024:2048]
    wrt_s = x1[32:40, 1024:2048]
    pb = [nc.alloc_psum_tensor(f"pb{i}", [128, 512], F32) for i in range(8)]

    dbg_d = {}
    dbg_src = {}

    def half(bank, h):
        return pb[bank][:, h * TC:(h + 1) * TC]

    B.dma("gpsimd", lambda e: e.dma_start(out=ident[:, :], in_=ident_d),
          writes=["ident"], tag="ld_ident")
    B.dma("gpsimd", lambda e: e.dma_start(out=cc[:, :], in_=consts_d),
          writes=["cc"], tag="ld_cc")
    B.dma("gpsimd", lambda e: e.dma_start(out=selbc_s[:, :], in_=selbc_d),
          writes=["selbc_s"], tag="ld_selbc")
    B.dma("gpsimd", lambda e: e.dma_start(out=selg_s[:, :], in_=selg_d),
          writes=["selg_s"], tag="ld_selg")
    for ci, cval in enumerate(CONSTS):
        nc.const_aps.register(F32, cval, cc[:, ci:ci + 1])
    B.op("vector", lambda e: e.tensor_copy(out=ident_bf[:, :], in_=ident[:, :]),
         reads=["ident"], writes=["ident_bf"])
    B.op("vector", lambda e: e.memset(ones_bf[:, :], 1.0), writes=["ones_bf"])
    B.op("vector", lambda e: e.memset(ones_f[:, :], 1.0), writes=["ones_f"])
    B.op("vector", lambda e: e.memset(zrow_bf[:, :], 0.0), writes=["zrow_bf"])
    B.op("vector", lambda e: e.memset(onec_bf[:, :], 1.0), writes=["onec_bf"])
    B.op("gpsimd", lambda e: e.memset(carry[:, :], 0.0), writes=["carry"])
    for dg in range(DG):
        o = dg * (TC + 8)
        B.op("gpsimd", lambda e, o=o: e.memset(xin[:, o:o + 8], 0.0),
             writes=["xin_head"])

    def vec_load(dst, src_ap, name):
        B.dma("gpsimd",
              lambda e: e.dma_start(out=dst[:, :],
                                    in_=src_ap.rearrange("(g p) -> p g", p=128),
                                    allow_slow_non_contiguous=True),
              writes=[name], tag="ld_" + name)

    vec_load(convb_s, convb_d, "convb_s")
    vec_load(dtb_s, dtb_d, "dtb_s")
    vec_load(dpar_s, dpar_d, "dpar_s")
    vec_load(g1_s, g1_d, "g1_s")
    vec_load(g2_s, g2_d, "g2_s")
    vec_load(b2f_s, b2f_d, "b2f_s")
    vec_load(b1_f, b1_d, "b1_f")
    vec_load(b2_f, b2_d, "b2_f")
    B.op("vector", lambda e: e.tensor_copy(out=b1_bf[:, :], in_=b1_f[:, :]),
         reads=["b1_f"], writes=["b1_bf"])
    B.op("vector", lambda e: e.tensor_copy(out=b2ln_bf[:, :], in_=b2_f[:, :]),
         reads=["b2_f"], writes=["b2ln_bf"])
    B.dma("gpsimd", lambda e: e.dma_start(
        out=convw_s[:, :], in_=convw_d.rearrange("(g p) k -> p g k", p=128),
        allow_slow_non_contiguous=True),
        writes=["convw_s"], tag="ld_convw")
    B.dma("gpsimd", lambda e: e.dma_start(
        out=Afm[:, :], in_=alog_d.rearrange("(g p) n -> p g n", p=128),
        allow_slow_non_contiguous=True),
        writes=["Afm"], tag="ld_alog")
    B.dma("gpsimd", lambda e: e.dma_start(
        out=mb1_s[:, :], in_=mb1_d.rearrange("e (t p) -> p e t", p=128),
        allow_slow_non_contiguous=True),
        writes=["mb1_s"], tag="ld_mb1")
    B.dma("gpsimd", lambda e: e.dma_start(
        out=b1f_s[:, :], in_=b1f_d.rearrange("(t p) -> p t", p=128),
        allow_slow_non_contiguous=True),
        writes=["b1f_s"], tag="ld_b1f")
    B.dma("gpsimd", lambda e: e.dma_start(out=cent_s, in_=cent_d),
          writes=["x1"], tag="ld_cent")
    B.dma("gpsimd", lambda e: e.dma_start(out=wrt_s, in_=wrt_d),
          writes=["x1"], tag="ld_wrt")
    B.dma("gpsimd", lambda e: e.dma_start(out=mb2_f, in_=mb2_d),
          writes=["xn_tm"], tag="ld_mb2")
    B.op("vector", lambda e: e.tensor_copy(out=mb2_bf[:, :], in_=mb2_f),
         reads=["xn_tm"], writes=["mb2_bf"])
    B.dma("gpsimd", lambda e: e.dma_start(out=xpw_s, in_=xpw_d),
          writes=["x1"], tag="ld_xpw")
    B.dma("gpsimd", lambda e: e.dma_start(
        out=dtw_s, in_=dtw_d.rearrange("(g p) r -> p g r", p=128),
        allow_slow_non_contiguous=True),
        writes=["xn_tm"], tag="ld_dtw")

    B.op("scalar", lambda e: e.activation(out=Afm[:, :], in_=Afm[:, :], func=AF.Exp),
         reads=["Afm"], writes=["Afm"], tag="mkA")
    B.op("vector", lambda e: e.tensor_scalar_mul(Afm[:, :], Afm[:, :], -1.0),
         reads=["Afm"], writes=["Afm"])

    B.op("scalar", lambda e: e.activation(out=csq, in_=cent_s,
                                          func=AF.Square, accum_out=rc_s[:, 0:1]),
         reads=["x1"], writes=["xn_tm", "rc_s"], tag="rc_sq", late=True)
    B.op("scalar", lambda e: e.activation(out=rc_s[:, 1:2], in_=rc_s[:, 0:1],
                                          func=AF.Ln, bias=1e-24),
         reads=["rc_s", "cc"], writes=["rc_s"])
    B.op("scalar", lambda e: e.activation(out=rc_s[:, 2:3], in_=rc_s[:, 1:2],
                                          func=AF.Exp, scale=-0.5),
         reads=["rc_s", "cc"], writes=["rc_s"])

    for kc in range(DG):
        B.op("tensor", lambda e, kc=kc: e.transpose(
            out=pb[6][0:128, 0:96], in_=xpw_s[:, kc * 128:(kc + 1) * 128],
            identity=ident[0:96, 0:96]),
            reads=["x1", "ident", "pb6"], writes=["pb6"], tag=f"tr_xp{kc}")
        B.op("scalar", lambda e, kc=kc: e.copy(
            out=xpT_bf[:, kc * 96:(kc + 1) * 96], in_=pb[6][0:128, 0:96]),
            reads=["pb6"], writes=["xpT_bf"])
        B.op("tensor", lambda e, kc=kc: e.transpose(
            out=pb[7][0:DTR, 0:128], in_=dtw_s[:, kc * DTR:(kc + 1) * DTR],
            identity=ident[:, :]),
            reads=["xn_tm", "ident", "pb7"], writes=["pb7"], tag=f"tr_dt{kc}")
        B.op("scalar", lambda e, kc=kc: e.copy(
            out=dtT_bf[:, kc * 128:(kc + 1) * 128], in_=pb[7][0:DTR, 0:128]),
            reads=["pb7"], writes=["dtT_bf"])
        B.op("tensor", lambda e, kc=kc: e.transpose(
            out=pb[6][0:128, 96:104], in_=cent_s[:, kc * 128:(kc + 1) * 128],
            identity=ident[0:8, 0:8]),
            reads=["x1", "ident", "pb6"], writes=["pb6"], tag=f"tr_ce{kc}")
        B.op("tensor", lambda e, kc=kc: e.transpose(
            out=pb[6][0:128, 104:112], in_=wrt_s[:, kc * 128:(kc + 1) * 128],
            identity=ident[32:40, 32:40]),
            reads=["x1", "ident", "pb6"], writes=["pb6"])
        B.op("scalar", lambda e, kc=kc: e.copy(
            out=cw_bf[:, kc * 16:(kc + 1) * 16], in_=pb[6][0:128, 96:112]),
            reads=["pb6"], writes=["cw_bf"])

    LSZ = 1024
    NLANE = 2 * (PCOLS // LSZ)
    lane_free = list(range(NLANE))

    def lane_view(j):
        return slots[j // (PCOLS // LSZ)][
            :, (j % (PCOLS // LSZ)) * LSZ:((j % (PCOLS // LSZ)) + 1) * LSZ]

    wp_written = set()
    rgen_cell = [None]

    def load_lane(ch, pi, c0):
        while (pi, c0) not in wp_written and rgen_cell[0] is not None:
            if next(rgen_cell[0], "END") == "END":
                rgen_cell[0] = None
        assert (pi, c0) in wp_written, (pi, c0)
        j = alloc_lane()
        lv = lane_view(j)
        B.dma("sync", lambda e, pi=pi, lv=lv, c0=c0: e.dma_start(
            out=lv, in_=wpieces[pi][:, c0:c0 + LSZ]),
            reads=[f"wp{pi}"], writes=[f"lane{j}"], tag=f"c{ch}_p{pi}_{c0}")
        return lv, f"lane{j}", j

    def free_lane(j):
        lane_free.append(j)

    def alloc_lane():
        assert lane_free, "lane ring exhausted: missing free_lane"
        return lane_free.pop(0)

    def repack_gen():
        Builder.ctx = ("repack", -1)
        rp_stage = [(x1[0:128, 0:2048], "x1"), (sS[:, 0:2048], "sS")]
        rp_cast = [(lnp_bf, "lnp_bf"), (gfin, "gfin")]
        for pi in range(NPIECE):
            for q4 in range(4):
                p = (pi * 4 + q4) % 2
                sl, sln = rp_stage[p]
                cd, cdn = rp_cast[p]
                c0 = q4 * 2048
                B.dma("sync" if p == 0 else "scalar",
                      lambda e, pi=pi, c0=c0, sl=sl: e.dma_start(
                    out=sl[:, 0:2048], in_=wpf_d[pi][:, c0:c0 + 2048]),
                    writes=[sln], tag=f"rp_l{pi}_{c0}")
                if p == 0:
                    B.op("vector", lambda e, sl=sl, cd=cd: e.tensor_copy(
                        out=cd[:, 0:2048], in_=sl[:, 0:2048]),
                        reads=[sln], writes=[cdn], tag=f"rp_c{pi}_{c0}")
                else:
                    B.op("scalar", lambda e, sl=sl, cd=cd: e.copy(
                        out=cd[:, 0:2048], in_=sl[:, 0:2048]),
                        reads=[sln], writes=[cdn], tag=f"rp_c{pi}_{c0}")
                B.dma("gpsimd", lambda e, pi=pi, c0=c0, cd=cd: e.dma_start(
                    out=wpieces[pi][:, c0:c0 + 2048], in_=cd[:, 0:2048]),
                    reads=[cdn], writes=[f"wp{pi}"], tag=f"rp_s{pi}_{c0}")
                wp_written.add((pi, c0))
                wp_written.add((pi, c0 + 1024))
                yield


    def zero_mm(bank, tag):
        B.op("tensor", lambda e, bank=bank: e.matmul(
            out=pb[bank][:, 0:512], lhsT=zrow_bf[:, 0:128], rhs=zrow_bf[:, 0:512],
            start=True, stop=False),
            reads=["zrow_bf", f"pb{bank}"], writes=[f"pb{bank}"], tag=tag)

    def front_head(ch):
        Builder.ctx = ("front", ch)
        """P0..P3 (generator): x load, stats, xn, in_proj, conv, dbl, delta."""
        t0 = ch * TC
        xb = x_tm[ch % 2]
        xbn = f"x_tm{ch % 2}"
        pz = ch % 2
        xnb = xn_bf[pz]
        xnbn = f"xn_bf{pz}"
        B.dma("gpsimd", lambda e, t0=t0, xb=xb: e.dma_start(
            out=xb[:, :], in_=x_d[t0:t0 + TC, :].rearrange("(s p) d -> p s d", p=128)),
            writes=[xbn], tag=f"c{ch}_ldx")

        def ld_k(kc, rnd):
            return load_lane(ch, kc // 4, (kc % 4) * 2048 + rnd * 1024)

        for ts in range(NTS):
            B.op("vector", lambda e, ts=ts, xb=xb: e.tensor_scalar(
                out=xn_tm[:, ts * D:(ts + 1) * D], in0=xb[:, ts * D:(ts + 1) * D],
                scalar1=1.0, scalar2=0.0, op0=ALU.mult, op1=ALU.add,
                accum_out=statc[:, ts * 8:ts * 8 + 1]),
                reads=[xbn],
                writes=["xn_tm", "statc"] + [f"u{g}" for g in range(DG)],
                tag=f"c{ch}_sx{ts}",
                late=True)
            B.op("vector", lambda e, ts=ts, xb=xb: e.scalar_tensor_tensor(
                out=xn_tm[:, ts * D:(ts + 1) * D], in0=xb[:, ts * D:(ts + 1) * D],
                scalar=1.0, in1=xb[:, ts * D:(ts + 1) * D], op0=ALU.mult,
                op1=ALU.mult, accum_out=statc[:, ts * 8 + 1:ts * 8 + 2]),
                reads=[xbn],
                writes=["xn_tm", "statc"] + [f"u{g}" for g in range(DG)],
                tag=f"c{ch}_sxx{ts}",
                late=True)
        for ts in range(NTS):
            c = ts * 8
            B.op("vector", lambda e, c=c: e.tensor_scalar(
                out=statc[:, c + 2:c + 3], in0=statc[:, c:c + 1],
                scalar1=1.0 / D, scalar2=None, op0=ALU.mult),
                reads=["statc"], writes=["statc"])
            B.op("vector", lambda e, c=c: e.tensor_scalar(
                out=statc[:, c + 7:c + 8], in0=statc[:, c + 1:c + 2],
                scalar1=1.0 / D, scalar2=None, op0=ALU.mult),
                reads=["statc"], writes=["statc"])
            B.op("vector", lambda e, c=c: e.scalar_tensor_tensor(
                out=statc[:, c + 3:c + 4], in0=statc[:, c + 2:c + 3], scalar=-1.0,
                in1=statc[:, c + 2:c + 3], op0=ALU.mult, op1=ALU.mult),
                reads=["statc"], writes=["statc"])
            B.op("vector", lambda e, c=c: e.tensor_add(
                out=statc[:, c + 3:c + 4], in0=statc[:, c + 7:c + 8],
                in1=statc[:, c + 3:c + 4]),
                reads=["statc"], writes=["statc"])
            B.op("scalar", lambda e, c=c: e.activation(
                out=statc[:, c + 4:c + 5], in_=statc[:, c + 3:c + 4],
                func=AF.Ln, bias=LN_EPS),
                reads=["statc", "cc"], writes=["statc"], tag=f"c{ch}_l0{c}")
            B.op("scalar", lambda e, c=c: e.activation(
                out=statc[:, c + 4:c + 5], in_=statc[:, c + 4:c + 5],
                func=AF.Exp, scale=-0.5),
                reads=["statc", "cc"], writes=["statc"])
            B.op("vector", lambda e, c=c: e.scalar_tensor_tensor(
                out=statc[:, c + 7:c + 8], in0=statc[:, c + 4:c + 5], scalar=1.0,
                in1=statc[:, c + 4:c + 5], op0=ALU.mult, op1=ALU.mult),
                reads=["statc"], writes=["statc"])
            B.op("vector", lambda e, c=c: e.tensor_mul(
                out=statc[:, c + 7:c + 8], in0=statc[:, c + 7:c + 8],
                in1=statc[:, c + 3:c + 4]),
                reads=["statc"], writes=["statc"])
            B.op("scalar", lambda e, c=c: e.activation(
                out=statc[:, c + 5:c + 6], in_=statc[:, c + 7:c + 8],
                func=AF.Ln, bias=LN_EPS),
                reads=["statc", "cc"], writes=["statc"])
            B.op("scalar", lambda e, c=c: e.activation(
                out=statc[:, c + 5:c + 6], in_=statc[:, c + 5:c + 6],
                func=AF.Exp, scale=-0.5),
                reads=["statc", "cc"], writes=["statc"])
            B.op("scalar", lambda e, c=c: e.activation(
                out=statc[:, c + 6:c + 7], in_=statc[:, c + 7:c + 8],
                func=AF.Ln, scale=float(D), bias=1e-30),
                reads=["statc", "cc"], writes=["statc"])
            B.op("scalar", lambda e, c=c: e.activation(
                out=statc[:, c + 6:c + 7], in_=statc[:, c + 6:c + 7],
                func=AF.Exp, scale=-0.5),
                reads=["statc", "cc"], writes=["statc"])
            B.op("vector", lambda e, c=c, ts=ts: e.tensor_copy(
                out=stile_s1[:, ts:ts + 1], in_=statc[:, c + 5:c + 6]),
                reads=["statc"], writes=["stile"])
            B.op("vector", lambda e, c=c, ts=ts: e.tensor_copy(
                out=stile_rn[:, ts:ts + 1], in_=statc[:, c + 6:c + 7]),
                reads=["statc"], writes=["stile"])
            yield "H"
        for ts in range(NTS):
            c = ts * 8
            B.op("vector", lambda e, c=c, ts=ts, xb=xb: e.tensor_scalar(
                out=xn_tm[:, ts * D:(ts + 1) * D], in0=xb[:, ts * D:(ts + 1) * D],
                scalar1=statc[:, c + 2:c + 3], scalar2=statc[:, c + 4:c + 5],
                op0=ALU.subtract, op1=ALU.mult),
                reads=[xbn, "statc"],
                writes=["xn_tm"] + [f"u{g}" for g in range(DG)],
                tag=f"c{ch}_xn{ts}")

        for ts in range(NTS):
            B.op("tensor", lambda e, ts=ts: e.transpose(
                out=pb[2][0:1, 0:128], in_=stile_s1[:, ts:ts + 1],
                identity=ident[:, :]),
                reads=["stile", "ident", "pb2"], writes=["pb2"], tag=f"c{ch}_trs{ts}")
            B.op("vector", lambda e, ts=ts: e.tensor_copy(
                out=trow_s1[:, ts * 128:(ts + 1) * 128], in_=pb[2][0:1, 0:128]),
                reads=["pb2"], writes=["trow"])
            B.op("tensor", lambda e, ts=ts: e.transpose(
                out=pb[3][0:1, 0:128], in_=stile_rn[:, ts:ts + 1],
                identity=ident[:, :]),
                reads=["stile", "ident", "pb3"], writes=["pb3"], tag=f"c{ch}_trr{ts}")
            B.op("vector", lambda e, ts=ts, pz=pz: e.tensor_copy(
                out=trow_rn[pz][:, ts * 128:(ts + 1) * 128], in_=pb[3][0:1, 0:128]),
                reads=["pb3"], writes=[f"trow_rn{pz}"])
        B.op("tensor", lambda e: e.matmul(
            out=pb[2][:, 0:TC], lhsT=ones_f[:, :], rhs=trow_s1[:, :],
            start=True, stop=True),
            reads=["ones_f", "trow", "pb2"], writes=["pb2"], tag=f"c{ch}_bs1")
        B.op("vector", lambda e: e.tensor_copy(out=s1b_sb[:, :], in_=pb[2][:, 0:TC]),
             reads=["pb2"], writes=["s1b_sb"], tag=f"c{ch}_es1b")
        yield "H"

        for ti in range(8 * NTS):
            ts, dg = divmod(ti, 8)
            bank = ti % 2
            B.op("tensor", lambda e, ts=ts, dg=dg, bank=bank: e.transpose(
                out=pb[bank][:, 0:128],
                in_=xn_tm[:, ts * D + dg * 128:ts * D + (dg + 1) * 128],
                identity=ident[:, :]),
                reads=["xn_tm", "ident", f"pb{bank}"], writes=[f"pb{bank}"],
                tag=f"c{ch}_txn{ts}_{dg}")
            B.op("vector", lambda e, ts=ts, dg=dg, bank=bank: e.scalar_tensor_tensor(
                out=xns_bf[:, dg * TC + ts * 128:dg * TC + (ts + 1) * 128],
                in0=pb[bank][:, 0:128], scalar=g1_s[:, dg:dg + 1],
                in1=s1b_sb[:, ts * 128:(ts + 1) * 128], op0=ALU.mult, op1=ALU.mult),
                reads=[f"pb{bank}", "g1_s", "s1b_sb"], writes=["xns_bf"],
                tag=f"c{ch}_xns{ts}_{dg}")
            B.op("vector", lambda e, ts=ts, dg=dg, bank=bank, xnb=xnb: e.tensor_copy(
                out=xnb[:, dg * TC + ts * 128:dg * TC + (ts + 1) * 128],
                in_=pb[bank][:, 0:128]),
                reads=[f"pb{bank}"], writes=[xnbn], tag=f"c{ch}_exn{ts}_{dg}")
            if ti % 4 == 3:
                yield "H"

        if ch > 0:
            for dg in range(DG):
                o = dg * (TC + 8)
                B.op("vector", lambda e, o=o: e.tensor_copy(
                    out=xin[:, o + 1:o + 8], in_=xin[:, o + TC + 1:o + TC + 8]),
                    reads=["xin"], writes=["xin_head"], tag=f"c{ch}_cvh{dg}")

        if ch == 0:
            for kc in range(8):
                lns = [ld_k(kc, 0), ld_k(kc, 1)]
                for et in range(16):
                    sl, sn, _ = lns[et // 8]
                    B.op("tensor", lambda e, sl=sl, et=et, kc=kc: e.matmul(
                        out=pb[0][:, et:et + 1],
                        lhsT=sl[:, (et % 8) * 128:(et % 8 + 1) * 128],
                        rhs=b1_bf[:, kc:kc + 1],
                        start=(et == 0 and kc == 0), stop=(et == 15 and kc == 7)),
                        reads=[sn, "b1_bf", "pb0"], writes=["pb0"],
                        tag=f"wb1_{et}_{kc}")
                free_lane(lns[0][2])
                free_lane(lns[1][2])
                yield "H"
            B.op("vector", lambda e: e.tensor_copy(out=wb1[:, :], in_=pb[0][:, 0:16]),
                 reads=["pb0"], writes=["wb1"], tag="wb1_ev")
            B.op("vector", lambda e: e.tensor_scalar(
                out=wb1n[:, :], in0=wb1[:, :], scalar1=-1.0, scalar2=None,
                op0=ALU.mult),
                reads=["wb1"], writes=["wb1n"], tag="wb1n")

        def conv_unit(dg):
            o = dg * (TC + 8)
            a = acc[dg % 2]
            an = f"dA{dg % 2}"
            B.op("vector", lambda e, o=o, a=a, dg=dg: e.tensor_scalar(
                out=a[:, :], in0=xin[:, o + 1:o + 1 + TC],
                scalar1=convw_s[:, dg * DCONV:dg * DCONV + 1],
                scalar2=convb_s[:, dg:dg + 1], op0=ALU.mult, op1=ALU.add),
                reads=["xin", "xin_head", "convw_s", "convb_s"], writes=[an],
                tag=f"c{ch}_cv0_{dg}")
            for k in range(1, DCONV):
                B.op("vector", lambda e, o=o, a=a, dg=dg, k=k, an=an:
                     e.scalar_tensor_tensor(
                        out=a[:, :], in0=xin[:, o + 1 + k:o + 1 + k + TC],
                        scalar=convw_s[:, dg * DCONV + k:dg * DCONV + k + 1],
                        in1=a[:, :], op0=ALU.mult, op1=ALU.add),
                     reads=["xin", "xin_head", "convw_s", an], writes=[an],
                     tag=f"c{ch}_cv{k}_{dg}")
            B.op("scalar", lambda e, a=a, dg=dg, an=an: e.activation(
                out=xc[:, dg * TC:(dg + 1) * TC], in_=a[:, :], func=AF.Silu),
                reads=[an], writes=["xc"], tag=f"c{ch}_xc{dg}")

        for rnd in range(2):
            for b in range(4):
                zero_mm(b, f"c{ch}_iz{rnd}_{b}")
            pend = None
            for kc in range(8):
                sl, sn, jl = pend if pend is not None else ld_k(kc, rnd)
                pend = ld_k(kc + 1, rnd) if kc < 7 else None
                for et in range(rnd * 8, rnd * 8 + 8):
                    bank, hh = et % 4, (et // 4) % 2
                    last = (et == rnd * 8 + 4 + bank and kc == 7)
                    B.op("tensor", lambda e, sl=sl, et=et, kc=kc, rnd=rnd,
                         bank=bank, hh=hh, last=last: e.matmul(
                            out=half(bank, hh),
                            lhsT=sl[:, (et - rnd * 8) * 128:
                                    (et - rnd * 8 + 1) * 128],
                            rhs=xns_bf[:, kc * TC:(kc + 1) * TC],
                            start=False, stop=last),
                         reads=[sn, "xns_bf", f"pb{bank}"], writes=[f"pb{bank}"],
                         tag=f"c{ch}_inp{et}_{kc}")
                free_lane(jl)
                if rnd == 1:
                    conv_unit(kc)
                yield "H"
            for et in range(rnd * 8, rnd * 8 + 8):
                bank, hh = et % 4, (et // 4) % 2
                if et < 8:
                    o = et * (TC + 8) + 8
                    B.op("scalar", lambda e, bank=bank, hh=hh, o=o, et=et:
                         e.activation(
                            out=xin[:, o:o + TC], in_=half(bank, hh),
                            func=AF.Identity, bias=wb1[:, et:et + 1]),
                         reads=[f"pb{bank}", "wb1", "xin_head"], writes=["xin"],
                         tag=f"c{ch}_exin{et}")
                else:
                    dg = et - 8
                    B.op("scalar", lambda e, bank=bank, hh=hh, dg=dg, et=et:
                         e.activation(
                            out=sz[:, dg * TC:(dg + 1) * TC], in_=half(bank, hh),
                            func=AF.Silu, bias=wb1[:, et:et + 1]),
                         reads=[f"pb{bank}", "wb1"], writes=["sz"],
                         tag=f"c{ch}_esz{dg}")
            yield "H"

        for dg in range(DG):
            B.op("vector", lambda e, dg=dg: e.tensor_copy(
                out=xc_bf[:, dg * TC:(dg + 1) * TC],
                in_=xc[:, dg * TC:(dg + 1) * TC]),
                reads=["xc"], writes=["xc_bf"], tag=f"c{ch}_xcbf{dg}")
            if dg % 4 == 3:
                yield "Hc"

        for kc in range(8):
            B.op("tensor", lambda e, kc=kc: e.matmul(
                out=pb[2][0:96, 0:TC], lhsT=xpT_bf[:, kc * 96:(kc + 1) * 96],
                rhs=xc_bf[:, kc * TC:(kc + 1) * TC],
                start=(kc == 0), stop=(kc == 7)),
                reads=["xpT_bf", "xc_bf", "pb2"], writes=["pb2"], tag=f"c{ch}_xp{kc}")
        B.op("scalar", lambda e: e.copy(out=dt_bf[:, :], in_=pb[2][0:DTR, 0:TC]),
             reads=["pb2"], writes=["dt_bf"], tag=f"c{ch}_edt")
        B.op("scalar", lambda e: e.copy(out=bc_bf[:, :], in_=pb[2][DTR:96, 0:TC]),
             reads=["pb2"], writes=["bc_bf"], tag=f"c{ch}_ebc")
        yield "Hc"
        for dg in range(DG):
            bank = dg % 2
            B.op("tensor", lambda e, dg=dg, bank=bank: e.matmul(
                out=pb[bank][:, 0:TC], lhsT=dtT_bf[:, dg * 128:(dg + 1) * 128],
                rhs=dt_bf[:, :], start=True, stop=True),
                reads=["dtT_bf", "dt_bf", f"pb{bank}"], writes=[f"pb{bank}"],
                tag=f"c{ch}_dtp{dg}")
            B.op("scalar", lambda e, dg=dg, bank=bank: e.activation(
                out=etmp[:, :], in_=pb[bank][:, 0:TC], func=AF.Exp,
                bias=dtb_s[:, dg:dg + 1]),
                reads=[f"pb{bank}", "dtb_s"], writes=["dA2"], tag=f"c{ch}_sp1_{dg}")
            B.op("scalar", lambda e, dg=dg: e.activation(
                out=delta[:, dg * TC:(dg + 1) * TC], in_=etmp[:, :], func=AF.Ln,
                bias=1.0),
                reads=["dA2", "cc"], writes=[f"delta{dg}"], tag=f"c{ch}_sp2_{dg}")
            if dg % 2 == 1:
                yield "Hc"
        for dg in range(DG):
            B.op("vector", lambda e, dg=dg: e.tensor_mul(
                out=uful[:, dg * TC:(dg + 1) * TC],
                in0=delta[:, dg * TC:(dg + 1) * TC],
                in1=xc[:, dg * TC:(dg + 1) * TC]),
                reads=[f"delta{dg}", "xc"], writes=["xn_tm", f"u{dg}"],
                tag=f"c{ch}_u{dg}")
        if ch == DBG_CH:
            for nm, hdl, tok in [("delta", delta, "delta0"), ("xcbf", xc_bf, "xc_bf"),
                                 ("dtbf", dt_bf, "dt_bf"), ("bcbf", bc_bf, "bc_bf"),
                                 ("xinx", xin, "xin"), ("xnsx", xns_bf, "xns_bf"),
                                 ("ufulx", uful, "xn_tm"), ("xcf", xc, "xc")]:
                dd = nc.dram_tensor(f"dbg_{nm}", [hdl.shape[0], hdl.shape[1]],
                                    hdl.dtype if hasattr(hdl, "dtype") else F32,
                                    kind="ExternalOutput").ap()
                B.dma("gpsimd", lambda e, dd=dd, hdl=hdl: e.dma_start(
                    out=dd, in_=hdl[:, :] if hasattr(hdl, "shape") else hdl),
                    reads=[tok], writes=[f"dbg{nm}"], tag=f"dbg_{nm}")

    def front_scan(ch):
        Builder.ctx = ("front", ch)
        """P4 scan: 2 passes of 4 d-groups; y in banks 0/1, B/C selector
        broadcasts in banks 2/3. ysum matmuls are emitted one state late so
        the tensor engine never blocks on the scan chain."""
        pz = ch % 2
        for p2 in range(2):
            dgs = [p2 * 4 + i for i in range(4)]
            zero_mm(0, f"c{ch}_yz{p2}_0")
            zero_mm(1, f"c{ch}_yz{p2}_1")
            pend = []
            for n in range(NST):
                bB = 2 + n % 2
                bC = 3 - n % 2
                B.op("tensor", lambda e, n=n, bB=bB: e.matmul(
                    out=pb[bB][:, 0:TC], lhsT=selbc_s[:, n * 128:(n + 1) * 128],
                    rhs=bc_bf[:, :], start=True, stop=True),
                    reads=["selbc_s", "bc_bf", f"pb{bB}"], writes=[f"pb{bB}"],
                    tag=f"c{ch}_brep{p2}_{n}")
                B.op("tensor", lambda e, n=n, bC=bC: e.matmul(
                    out=pb[bC][:, 0:TC],
                    lhsT=selbc_s[:, (NST + n) * 128:(NST + n + 1) * 128],
                    rhs=bc_bf[:, :], start=True, stop=True),
                    reads=["selbc_s", "bc_bf", f"pb{bC}"], writes=[f"pb{bC}"],
                    tag=f"c{ch}_crepm{p2}_{n}")
                cr = crep[n % 2]
                crn = f"crep{n % 2}"
                B.op("scalar", lambda e, cr=cr, bC=bC: e.copy(
                    out=cr[:, :], in_=pb[bC][:, 0:TC]),
                    reads=[f"pb{bC}"], writes=[crn], tag=f"c{ch}_ecr{p2}_{n}")
                for i, dg in enumerate(dgs):
                    par = (n * 4 + i) % 3
                    gpar = (n * 4 + i) % 8
                    col = dg * NST + n
                    B.op("scalar", lambda e, dg=dg, n=n, par=par: e.activation(
                        out=dA[par][:, :], in_=delta[:, dg * TC:(dg + 1) * TC],
                        func=AF.Exp,
                        scale=Afm[:, dg * NST + n:dg * NST + n + 1]),
                        reads=[f"delta{dg}", "Afm"], writes=[f"dA{par}"],
                        tag=f"c{ch}_dA{dg}_{n}")
                    B.op("vector", lambda e, dg=dg, par=par, bB=bB: e.tensor_mul(
                        out=dBx[par][:, :], in0=uful[:, dg * TC:(dg + 1) * TC],
                        in1=pb[bB][:, 0:TC]),
                        reads=[f"u{dg}", f"pb{bB}"], writes=[f"dBx{par}"],
                        tag=f"c{ch}_dBx{dg}_{n}")
                    B.op("vector", lambda e, par=par, col=col: e.tensor_tensor_scan(
                        out=hs[par][:, :], data0=dA[par][:, :], data1=dBx[par][:, :],
                        initial=carry[:, col:col + 1], op0=ALU.mult, op1=ALU.add),
                        reads=[f"dA{par}", f"dBx{par}", "carry"],
                        writes=[f"hs{par}"], tag=f"c{ch}_scan{dg}_{n}")
                    B.op("gpsimd", lambda e, par=par, col=col: e.tensor_copy(
                        out=carry[:, col:col + 1], in_=hs[par][:, TC - 1:TC]),
                        reads=[f"hs{par}"], writes=["carry"],
                        tag=f"c{ch}_cw{dg}_{n}")
                    B.op("gpsimd", lambda e, par=par, gpar=gpar, cr=cr, crn=crn:
                         e.tensor_mul(
                            out=gy[gpar][:, :], in0=hs[par][:, :], in1=cr[:, :]),
                         reads=[f"hs{par}", crn], writes=[f"gy{gpar}"],
                         tag=f"c{ch}_gy{dg}_{n}")
                for (gpar_, i_, n_) in pend:
                    B.op("tensor", lambda e, gpar_=gpar_, i_=i_, n_=n_: e.matmul(
                        out=pb[i_ % 2][:, (i_ // 2) * TC:(i_ // 2 + 1) * TC],
                        lhsT=ident_bf[:, :], rhs=gy[gpar_][:, :],
                        start=False, stop=False),
                        reads=["ident_bf", f"gy{gpar_}", f"pb{i_ % 2}"],
                        writes=[f"pb{i_ % 2}"], tag=f"c{ch}_ysum{i_}_{n_}")
                pend = [((n * 4 + i) % 8, i, n) for i in range(4)]
                yield "S"
            for (gpar_, i_, n_) in pend:
                B.op("tensor", lambda e, gpar_=gpar_, i_=i_, n_=n_: e.matmul(
                    out=pb[i_ % 2][:, (i_ // 2) * TC:(i_ // 2 + 1) * TC],
                    lhsT=ident_bf[:, :], rhs=gy[gpar_][:, :],
                    start=False, stop=(i_ >= 2)),
                    reads=["ident_bf", f"gy{gpar_}", f"pb{i_ % 2}"],
                    writes=[f"pb{i_ % 2}"], tag=f"c{ch}_ysumf{i_}")
            for i, dg in enumerate(dgs):
                t1 = t1s[i % 2]
                t1n = f"t1s{i % 2}"
                B.op("vector", lambda e, dg=dg, i=i, t1=t1: e.scalar_tensor_tensor(
                    out=t1[:, :], in0=xc[:, dg * TC:(dg + 1) * TC],
                    scalar=dpar_s[:, dg:dg + 1],
                    in1=pb[i % 2][:, (i // 2) * TC:(i // 2 + 1) * TC],
                    op0=ALU.mult, op1=ALU.add),
                    reads=["xc", "dpar_s", f"pb{i % 2}"], writes=[t1n],
                    tag=f"c{ch}_ym1_{dg}")
                B.op("vector", lambda e, dg=dg, t1=t1, t1n=t1n, pz=pz: e.tensor_mul(
                    out=ymix_bf[pz][:, dg * TC:(dg + 1) * TC], in0=t1[:, :],
                    in1=sz[:, dg * TC:(dg + 1) * TC]),
                    reads=[t1n, "sz"], writes=[f"ym{pz}_{dg}"],
                    tag=f"c{ch}_ym2_{dg}")
            yield "S"

    def back(ch):
        Builder.ctx = ("back", ch)
        """P5..P8 for chunk ch on banks 4..7. Yields at unit boundaries."""
        t0 = ch * TC
        xb = x_tm[ch % 2]
        xbn = f"x_tm{ch % 2}"
        pz = ch % 2
        xnb = xn_bf[pz]
        xnbn = f"xn_bf{pz}"
        ymx = ymix_bf[pz]
        ymxn = f"ymix_bf{pz}"

        olanes = [load_lane(ch, 2, kc * 1024) for kc in range(8)]
        if ch == 0:
            for kc in range(8):
                sl, sn, _ = olanes[kc]
                B.op("vector", lambda e, sl=sl, kc=kc: e.reduce_sum(
                    out=wos[:, kc:kc + 1], in_=sl[:, :], axis=AXX),
                    reads=[sn], writes=["wos"], tag=f"wos{kc}")
            B.op("vector", lambda e: e.tensor_copy(out=wos_bf[:, :], in_=wos[:, :]),
                 reads=["wos"], writes=["wos_bf"])
        for b in range(4):
            zero_mm(4 + b, f"c{ch}_oz{b}")
        for kc in range(8):
            sl, sn, _ = olanes[kc]
            for dt in range(8):
                bank, hh = 4 + dt % 4, dt // 4
                last = (kc == 7 and dt >= 4)
                B.op("tensor", lambda e, sl=sl, dt=dt, kc=kc,
                     bank=bank, hh=hh, last=last: e.matmul(
                        out=half(bank, hh),
                        lhsT=sl[:, dt * 128:(dt + 1) * 128],
                        rhs=ymx[:, kc * TC:(kc + 1) * TC],
                        start=False, stop=last),
                     reads=[sn, f"ym{pz}_{kc}", f"pb{bank}"],
                     writes=[f"pb{bank}"],
                     tag=f"c{ch}_op{dt}_{kc}")
            yield
        for dt in range(8):
            bank, hh = 4 + dt % 4, dt // 4
            B.op("vector", lambda e, dt=dt, bank=bank, hh=hh, xnb=xnb:
                 e.tensor_add(
                    out=x1[:, dt * TC:(dt + 1) * TC],
                    in0=xnb[:, dt * TC:(dt + 1) * TC], in1=half(bank, hh)),
                reads=[xnbn, f"pb{bank}"], writes=["x1"], tag=f"c{ch}_x1_{dt}")
        yield
        for _, _, jl in olanes:
            free_lane(jl)
        for kc in range(8):
            B.op("tensor", lambda e, kc=kc: e.matmul(
                out=pb[6][0:1, 0:TC], lhsT=wos_bf[:, kc:kc + 1],
                rhs=ymx[:, kc * TC:(kc + 1) * TC],
                start=(kc == 0), stop=(kc == 7)),
                reads=["wos_bf", f"ym{pz}_{kc}", "pb6"], writes=["pb6"],
                tag=f"c{ch}_smo{kc}")
        yield

        for dg in range(DG):
            B.op("scalar", lambda e, dg=dg: e.activation(
                out=xsq_bf[:, dg * TC:(dg + 1) * TC],
                in_=x1[:, dg * TC:(dg + 1) * TC], func=AF.Square),
                reads=["x1"], writes=[f"hgm{dg // 4}"], tag=f"c{ch}_xsq{dg}")
        for kc in range(8):
            B.op("tensor", lambda e, kc=kc: e.matmul(
                out=pb[7][0:1, 0:TC], lhsT=onec_bf[:, :],
                rhs=xsq_bf[:, kc * TC:(kc + 1) * TC],
                start=(kc == 0), stop=(kc == 7)),
                reads=["onec_bf", f"hgm{kc // 4}", "pb7"], writes=["pb7"],
                tag=f"c{ch}_ssq{kc}")
        B.op("vector", lambda e: e.tensor_scalar(
            out=r_mu2[:, :], in0=pb[6][0:1, 0:TC], scalar1=1.0 / D, scalar2=None,
            op0=ALU.mult),
            reads=["pb6"], writes=["rowb"], tag=f"c{ch}_mu2")
        B.op("vector", lambda e: e.tensor_scalar(
            out=r_tmp[:, :], in0=pb[7][0:1, 0:TC], scalar1=1.0 / D, scalar2=None,
            op0=ALU.mult),
            reads=["pb7"], writes=["rowb"])
        B.op("vector", lambda e: e.scalar_tensor_tensor(
            out=r_rstd2[:, :], in0=r_mu2[:, :], scalar=-1.0, in1=r_mu2[:, :],
            op0=ALU.mult, op1=ALU.mult),
            reads=["rowb"], writes=["rowb"])
        B.op("vector", lambda e: e.tensor_add(
            out=r_rstd2[:, :], in0=r_tmp[:, :], in1=r_rstd2[:, :]),
            reads=["rowb"], writes=["rowb"])
        B.op("scalar", lambda e: e.activation(
            out=r_rstd2[:, :], in_=r_rstd2[:, :], func=AF.Ln, bias=LN_EPS),
            reads=["rowb", "cc"], writes=["rowb"], tag=f"c{ch}_lr2")
        B.op("scalar", lambda e: e.activation(
            out=r_rstd2[:, :], in_=r_rstd2[:, :], func=AF.Exp, scale=-0.5),
            reads=["rowb", "cc"], writes=["rowb"])
        B.op("vector", lambda e: e.scalar_tensor_tensor(
            out=r_m2neg[:, :], in0=r_mu2[:, :], scalar=-1.0, in1=r_rstd2[:, :],
            op0=ALU.mult, op1=ALU.mult),
            reads=["rowb"], writes=["rowb"])
        B.op("tensor", lambda e: e.matmul(
            out=pb[6][:, 0:TC], lhsT=ones_f[:, :], rhs=r_rstd2[:, :],
            start=True, stop=True),
            reads=["ones_f", "rowb", "pb6"], writes=["pb6"], tag=f"c{ch}_br2")
        B.op("vector", lambda e: e.tensor_copy(out=rstd2_sb[:, :], in_=pb[6][:, 0:TC]),
             reads=["pb6"], writes=["rstd2_sb"])
        B.op("tensor", lambda e: e.matmul(
            out=pb[7][:, 0:TC], lhsT=ones_f[:, :], rhs=r_m2neg[:, :],
            start=True, stop=True),
            reads=["ones_f", "rowb", "pb7"], writes=["pb7"], tag=f"c{ch}_bm2")
        B.op("vector", lambda e: e.tensor_copy(out=m2neg_sb[:, :], in_=pb[7][:, 0:TC]),
             reads=["pb7"], writes=["m2neg_sb"])
        for dg in range(DG):
            B.op("vector", lambda e, dg=dg: e.scalar_tensor_tensor(
                out=etmp2[:, :], in0=x1[:, dg * TC:(dg + 1) * TC],
                scalar=g2_s[:, dg:dg + 1], in1=rstd2_sb[:, :],
                op0=ALU.mult, op1=ALU.mult),
                reads=["x1", "g2_s", "rstd2_sb"], writes=["etmp2"],
                tag=f"c{ch}_lnp1_{dg}")
            B.op("vector", lambda e, dg=dg: e.scalar_tensor_tensor(
                out=lnp_bf[:, dg * TC:(dg + 1) * TC], in0=m2neg_sb[:, :],
                scalar=g2_s[:, dg:dg + 1], in1=etmp2[:, :],
                op0=ALU.mult, op1=ALU.add),
                reads=["m2neg_sb", "g2_s", "etmp2"], writes=["lnp_bf"],
                tag=f"c{ch}_lnp2_{dg}")
        yield

        B.op("tensor", lambda e, pz=pz: e.matmul(
            out=pb[4][0:8, 0:TC], lhsT=ones_f[:, 0:8], rhs=trow_rn[pz][:, :],
            start=True, stop=True),
            reads=["ones_f", f"trow_rn{pz}", "pb4"], writes=["pb4"],
            tag=f"c{ch}_brn")
        for kc in range(8):
            B.op("tensor", lambda e, kc=kc, xnb=xnb: e.matmul(
                out=pb[5][0:8, 0:TC], lhsT=cw_bf[:, kc * 16:kc * 16 + 8],
                rhs=xnb[:, kc * TC:(kc + 1) * TC],
                start=(kc == 0), stop=False),
                reads=["cw_bf", xnbn, "pb5"], writes=["pb5"], tag=f"c{ch}_rt{kc}")
            B.op("tensor", lambda e, kc=kc, xnb=xnb: e.matmul(
                out=pb[5][0:8, TC:2 * TC], lhsT=cw_bf[:, kc * 16 + 8:kc * 16 + 16],
                rhs=xnb[:, kc * TC:(kc + 1) * TC],
                start=False, stop=(kc == 7)),
                reads=["cw_bf", xnbn, "pb5"], writes=["pb5"], tag=f"c{ch}_rw{kc}")
        B.op("vector", lambda e: e.tensor_scalar(
            out=ltrow[:, :], in0=pb[5][0:8, 0:TC], scalar1=rc_s[:, 2:3],
            scalar2=None, op0=ALU.mult),
            reads=["pb5", "rc_s"], writes=["ltrow"], tag=f"c{ch}_lg1")
        B.op("vector", lambda e: e.tensor_mul(
            out=ltrow[:, :], in0=ltrow[:, :], in1=pb[4][0:8, 0:TC]),
            reads=["ltrow", "pb4"], writes=["ltrow"])
        B.op("vector", lambda e: e.tensor_add(
            out=ltrow[:, :], in0=ltrow[:, :], in1=pb[5][0:8, TC:2 * TC]),
            reads=["ltrow", "pb5"], writes=["ltrow"])
        for ts in range(NTS):
            bank = 6 + ts % 2
            B.op("tensor", lambda e, ts=ts, bank=bank: e.transpose(
                out=pb[bank][0:128, 0:8], in_=ltrow[:, ts * 128:(ts + 1) * 128],
                identity=ident[0:8, 0:8]),
                reads=["ltrow", "ident", f"pb{bank}"], writes=[f"pb{bank}"],
                tag=f"c{ch}_lt{ts}")
            B.op("vector", lambda e, ts=ts, bank=bank: e.tensor_copy(
                out=lt_tm[:, ts * 8:(ts + 1) * 8], in_=pb[bank][0:128, 0:8]),
                reads=[f"pb{bank}"], writes=["lt_tm"])
            B.op("vector", lambda e, ts=ts: e.max(
                out=gmax[:, ts * 8:(ts + 1) * 8], in_=lt_tm[:, ts * 8:(ts + 1) * 8]),
                reads=["lt_tm"], writes=["gmax"], tag=f"c{ch}_gmx{ts}")
            B.op("vector", lambda e, ts=ts: e.tensor_scalar(
                out=gmsk[:, ts * 8:(ts + 1) * 8], in0=lt_tm[:, ts * 8:(ts + 1) * 8],
                scalar1=gmax[:, ts * 8 + 1:ts * 8 + 2], scalar2=None,
                op0=ALU.is_ge),
                reads=["lt_tm", "gmax"], writes=["gmsk"])
            B.op("vector", lambda e, ts=ts: e.tensor_scalar(
                out=gneg[:, ts:ts + 1], in0=gmax[:, ts * 8:ts * 8 + 1],
                scalar1=-1.0, scalar2=None, op0=ALU.mult),
                reads=["gmax"], writes=["gneg"])
            B.op("scalar", lambda e, ts=ts: e.activation(
                out=gwrk[:, ts * 8:(ts + 1) * 8], in_=lt_tm[:, ts * 8:(ts + 1) * 8],
                func=AF.Exp, bias=gneg[:, ts:ts + 1]),
                reads=["lt_tm", "gneg"], writes=["gwrk"], tag=f"c{ch}_gex{ts}")
            B.op("vector", lambda e, ts=ts: e.tensor_mul(
                out=gwrk[:, ts * 8:(ts + 1) * 8], in0=gwrk[:, ts * 8:(ts + 1) * 8],
                in1=gmsk[:, ts * 8:(ts + 1) * 8]),
                reads=["gwrk", "gmsk"], writes=["gwrk"])
            B.op("vector", lambda e, ts=ts: e.reduce_sum(
                out=gden[:, ts:ts + 1], in_=gwrk[:, ts * 8:(ts + 1) * 8], axis=AXX),
                reads=["gwrk"], writes=["gden"])
            B.op("vector", lambda e, ts=ts: e.reciprocal(
                out=gden[:, ts:ts + 1], in_=gden[:, ts:ts + 1]),
                reads=["gden"], writes=["gden"])
            B.op("vector", lambda e, ts=ts: e.tensor_scalar(
                out=gwrk[:, ts * 8:(ts + 1) * 8], in0=gwrk[:, ts * 8:(ts + 1) * 8],
                scalar1=gden[:, ts:ts + 1], scalar2=None, op0=ALU.mult),
                reads=["gwrk", "gden"], writes=["gwrk"], tag=f"c{ch}_gnm{ts}")
            B.op("tensor", lambda e, ts=ts, bank=bank: e.transpose(
                out=pb[bank][0:8, 0:128], in_=gwrk[:, ts * 8:(ts + 1) * 8],
                identity=ident[:, :]),
                reads=["gwrk", "ident", f"pb{bank}"], writes=[f"pb{bank}"],
                tag=f"c{ch}_gtr{ts}")
            B.op("scalar", lambda e, ts=ts, bank=bank: e.copy(
                out=gT_bf[:, ts * 128:(ts + 1) * 128], in_=pb[bank][0:8, 0:128]),
                reads=[f"pb{bank}"], writes=["gT_bf"])
        yield

        grepsb = ymix_bf[pz]
        for ei in range(NE):
            mlanes = [load_lane(ch, 3 + ei, kb * 1024) for kb in range(4)]
            B.op("tensor", lambda e, ei=ei: e.matmul(
                out=pb[6][:, 0:TC], lhsT=selg_s[:, ei * 128:(ei + 1) * 128],
                rhs=gT_bf[:, :], start=True, stop=True),
                reads=["selg_s", "gT_bf", "pb6"], writes=["pb6"],
                tag=f"c{ch}_grep{ei}")
            B.op("scalar", lambda e, ei=ei, grepsb=grepsb: e.copy(
                out=grepsb[:, ei * TC:(ei + 1) * TC], in_=pb[6][:, 0:TC]),
                reads=["pb6"], writes=[f"ym{pz}_{ei}"], tag=f"c{ch}_egrep{ei}")
            for ht in range(4):
                hb = 4 + ht % 2
                for kc in range(8):
                    sl, sn, _ = mlanes[kc // 2]
                    B.op("tensor", lambda e, sl=sl, ht=ht, kc=kc,
                         hb=hb, xnb=xnb: e.matmul(
                            out=pb[hb][:, 0:TC],
                            lhsT=sl[:, (kc % 2) * 512 + ht * 128:
                                    (kc % 2) * 512 + (ht + 1) * 128],
                            rhs=xnb[:, kc * TC:(kc + 1) * TC],
                            start=(kc == 0), stop=(kc == 7)),
                         reads=[sn, xnbn, f"pb{hb}"], writes=[f"pb{hb}"],
                         tag=f"c{ch}_mh{ei}_{ht}_{kc}")
                hi = ei * 4 + ht
                B.op("scalar", lambda e, ei=ei, ht=ht, hb=hb, hi=hi:
                     e.activation(
                        out=hgm[:, hi * TC:(hi + 1) * TC], in_=pb[hb][:, 0:TC],
                        func=AF.Identity,
                        bias=mb1_s[:, ei * 4 + ht:ei * 4 + ht + 1]),
                     reads=[f"pb{hb}", "mb1_s"], writes=[f"hgm{hi // 4}"],
                     tag=f"c{ch}_mg{ei}_{ht}")
                yield
            for _, _, jl in mlanes:
                free_lane(jl)
        for j in range(16):
            B.op("scalar", lambda e, j=j: e.activation(
                out=hgm[:, j * 512:(j + 1) * 512],
                in_=hgm[:, j * 512:(j + 1) * 512], func=AF.Gelu),
                reads=[f"hgm{j // 2}"], writes=[f"hgm{j // 2}"],
                tag=f"c{ch}_gelu{j}")
        for hi in range(32):
            ei = hi // 4
            B.op("gpsimd", lambda e, hi=hi, ei=ei, grepsb=grepsb: e.tensor_mul(
                out=hgm[:, hi * TC:(hi + 1) * TC],
                in0=hgm[:, hi * TC:(hi + 1) * TC],
                in1=grepsb[:, ei * TC:(ei + 1) * TC]),
                reads=[f"hgm{hi // 4}", f"ym{pz}_{ei}"],
                writes=[f"hgm{hi // 4}"],
                tag=f"c{ch}_mhg{hi}")
        yield

        for b in range(4):
            zero_mm(4 + b, f"c{ch}_mz{b}")
        m2p = None
        for ei in range(NE):
            for ht in range(4):
                hi = ei * 4 + ht
                sl, sn, jl = (m2p if m2p is not None
                              else load_lane(ch, 3 + ei, 4096 + ht * 1024))
                nei, nht = (ei, ht + 1) if ht < 3 else (ei + 1, 0)
                m2p = (load_lane(ch, 3 + nei, 4096 + nht * 1024)
                       if nei < NE else None)
                for dt in range(8):
                    bank, hh = 4 + dt % 4, dt // 4
                    B.op("tensor", lambda e, sl=sl, ht=ht, dt=dt,
                         bank=bank, hh=hh, hi=hi: e.matmul(
                            out=half(bank, hh),
                            lhsT=sl[:, dt * 128:(dt + 1) * 128],
                            rhs=hgm[:, hi * TC:(hi + 1) * TC],
                            start=False, stop=False),
                         reads=[sn, f"hgm{hi // 4}", f"pb{bank}"],
                         writes=[f"pb{bank}"],
                         tag=f"c{ch}_mo{ei}_{ht}_{dt}")
                free_lane(jl)
                yield
        for dt in range(8):
            bank, hh = 4 + dt % 4, dt // 4
            B.op("tensor", lambda e, dt=dt, bank=bank, hh=hh: e.matmul(
                out=half(bank, hh), lhsT=mb2_bf[:, dt * 128:(dt + 1) * 128],
                rhs=gT_bf[:, :], start=False, stop=(dt >= 4)),
                reads=["mb2_bf", "gT_bf", f"pb{bank}"], writes=[f"pb{bank}"],
                tag=f"c{ch}_mb2_{dt}")
        for dt in range(8):
            bank, hh = 4 + dt % 4, dt // 4
            B.op("scalar", lambda e, dt=dt, bank=bank, hh=hh: e.activation(
                out=gfin[:, dt * TC:(dt + 1) * TC], in_=half(bank, hh),
                func=AF.Silu),
                reads=[f"pb{bank}"], writes=["gfin"], tag=f"c{ch}_gfin{dt}")
        yield

        if ch == 0:
            for hg in range(8):
                fblanes = [load_lane(ch, 11 + hg, kb * 1024) for kb in range(4)]
                for htl in range(4):
                    ht = hg * 4 + htl
                    for kc in range(8):
                        sl, sn, _ = fblanes[kc // 2]
                        B.op("tensor", lambda e, sl=sl, htl=htl,
                             kc=kc, ht=ht: e.matmul(
                                out=pb[7][:, ht:ht + 1],
                                lhsT=sl[:, (kc % 2) * 512 + htl * 128:
                                        (kc % 2) * 512 + (htl + 1) * 128],
                                rhs=b2ln_bf[:, kc:kc + 1],
                                start=(ht == 0 and kc == 0),
                                stop=(ht == 31 and kc == 7)),
                             reads=[sn, "b2ln_bf", "pb7"], writes=["pb7"],
                             tag=f"wb2f_{ht}_{kc}")
                for _, _, jl in fblanes:
                    free_lane(jl)
            B.op("vector", lambda e: e.tensor_add(
                out=wb2f[:, :], in0=pb[7][:, 0:32], in1=b1f_s[:, :]),
                reads=["pb7", "b1f_s"], writes=["wb2f"], tag="wb2f_ev")
        for hg in range(8):
            f1lanes = [load_lane(ch, 11 + hg, kb * 1024) for kb in range(4)]
            for htl in range(4):
                ht = hg * 4 + htl
                hb = 4 + ht % 2
                for kc in range(8):
                    sl, sn, _ = f1lanes[kc // 2]
                    B.op("tensor", lambda e, sl=sl, htl=htl, kc=kc,
                         hb=hb: e.matmul(
                            out=pb[hb][:, 0:TC],
                            lhsT=sl[:, (kc % 2) * 512 + htl * 128:
                                    (kc % 2) * 512 + (htl + 1) * 128],
                            rhs=lnp_bf[:, kc * TC:(kc + 1) * TC],
                            start=(kc == 0), stop=(kc == 7)),
                         reads=[sn, "lnp_bf", f"pb{hb}"], writes=[f"pb{hb}"],
                         tag=f"c{ch}_f1_{hg}_{htl}_{kc}")
                hp = htl % 2
                B.op("scalar", lambda e, hb=hb, hp=hp, ht=ht: e.activation(
                    out=hgm[:, ht * TC:(ht + 1) * TC], in_=pb[hb][:, 0:TC],
                    func=AF.Relu, bias=wb2f[:, ht:ht + 1]),
                    reads=[f"pb{hb}", "wb2f"], writes=[f"hgm{ht // 4}"],
                    tag=f"c{ch}_frl{ht}")
                yield
            for _, _, jl in f1lanes:
                free_lane(jl)

        for b in range(4):
            zero_mm(4 + b, f"c{ch}_fz{b}")
        f2p = None
        for hg in range(8):
            for htl in range(4):
                ht = hg * 4 + htl
                sl, sn, jl = (f2p if f2p is not None
                              else load_lane(ch, 11 + hg, 4096 + htl * 1024))
                nhg, nhtl = (hg, htl + 1) if htl < 3 else (hg + 1, 0)
                f2p = (load_lane(ch, 11 + nhg, 4096 + nhtl * 1024)
                       if nhg < 8 else None)
                for dt in range(8):
                    bank, hh = 4 + dt % 4, dt // 4
                    lastf2 = (hg == 7 and htl == 3 and dt >= 4)
                    B.op("tensor", lambda e, sl=sl, htl=htl, dt=dt,
                         bank=bank, hh=hh, ht=ht, lastf2=lastf2: e.matmul(
                            out=half(bank, hh),
                            lhsT=sl[:, dt * 128:(dt + 1) * 128],
                            rhs=hgm[:, ht * TC:(ht + 1) * TC],
                            start=False, stop=lastf2),
                         reads=[sn, f"hgm{ht // 4}", f"pb{bank}"],
                         writes=[f"pb{bank}"],
                         tag=f"c{ch}_f2_{hg}_{htl}_{dt}")
                free_lane(jl)
                yield
        for dt in range(8):
            bank, hh = 4 + dt % 4, dt // 4
            B.op("vector", lambda e, dt=dt, bank=bank, hh=hh: e.scalar_tensor_tensor(
                out=sS[:, dt * TC:(dt + 1) * TC], in0=half(bank, hh),
                scalar=b2f_s[:, dt:dt + 1], in1=x1[:, dt * TC:(dt + 1) * TC],
                op0=ALU.add, op1=ALU.add),
                reads=[f"pb{bank}", "b2f_s", "x1"], writes=["sS"],
                tag=f"c{ch}_sf{dt}")
            B.op("vector", lambda e, dt=dt: e.tensor_mul(
                out=sS[:, dt * TC:(dt + 1) * TC], in0=sS[:, dt * TC:(dt + 1) * TC],
                in1=gfin[:, dt * TC:(dt + 1) * TC]),
                reads=["sS", "gfin"], writes=["sS"], tag=f"c{ch}_sg{dt}")
        yield

        for ti in range(8 * NTS):
            ts, dg = divmod(ti, 8)
            bank = 6 + (ti % 2)
            B.op("tensor", lambda e, ts=ts, dg=dg, bank=bank: e.transpose(
                out=pb[bank][:, 0:128],
                in_=sS[:, dg * TC + ts * 128:dg * TC + (ts + 1) * 128],
                identity=ident[:, :]),
                reads=["sS", "ident", f"pb{bank}"], writes=[f"pb{bank}"],
                tag=f"c{ch}_tsg{ts}_{dg}")
            B.op("vector", lambda e, ts=ts, dg=dg, bank=bank, xb=xb: e.tensor_add(
                out=xb[:, ts * D + dg * 128:ts * D + (dg + 1) * 128],
                in0=xb[:, ts * D + dg * 128:ts * D + (dg + 1) * 128],
                in1=pb[bank][:, 0:128]),
                reads=[xbn, f"pb{bank}"], writes=[xbn],
                tag=f"c{ch}_oadd{ts}_{dg}")
            if ti % 4 == 3:
                yield
        B.dma("gpsimd", lambda e, t0=t0, xb=xb: e.dma_start(
            out=out_d[t0:t0 + TC, :].rearrange("(s p) d -> p s d", p=128),
            in_=xb[:, :]),
            reads=[xbn], writes=["out_d"], tag=f"c{ch}_st")
        yield

    def run_gen(g):
        for _ in g:
            pass

    def chain(*gens):
        for g in gens:
            for v in g:
                yield v

    if PIPELINE:
        prev_back = repack_gen()
        rgen_cell[0] = prev_back
        for ch in range(nchunks):
            fgen = chain(front_head(ch), front_scan(ch))
            if prev_back is None:
                run_gen(fgen)
            else:
                bgen = prev_back
                done_f = done_b = False
                budget = 0.0
                while not (done_f and done_b):
                    ph = None
                    if not done_f:
                        Builder.ctx = ("front", ch)
                        ph = next(fgen, "END")
                        done_f = ph == "END"
                    budget += 1e9 if done_f else {"S": 3.4, "H": 2.0,
                                  "Hc": 0.5}.get(ph, 1.0)
                    while budget >= 1.0 and not done_b:
                        Builder.ctx = ("back", ch - 1)
                        done_b = next(bgen, "END") == "END"
                        budget -= 1.0
            prev_back = back(ch)
        run_gen(prev_back)
    else:
        run_gen(repack_gen())
        for ch in range(nchunks):
            run_gen(chain(front_head(ch), front_scan(ch)))
            run_gen(back(ch))

    B.drain_all("sync")
    return nc, B


def _run(nc, B, in_maps, core_ids):
    emit = B.finalize(nc)
    with nc.Block() as block:
        block.sync(lambda e: emit("sync", e))
        block.tensor(lambda e: emit("tensor", e))
        block.vector(lambda e: emit("vector", e))
        block.scalar(lambda e: emit("scalar", e))
        block.gpsimd(lambda e: emit("gpsimd", e))
    return run_bass_kernel_spmd(nc, in_maps, core_ids=core_ids)


def _mk_sel(n):
    import ml_dtypes
    s = np.zeros((n, n, 128), np.float32)
    for k in range(n):
        s[k, k, :] = 1.0
    return s.reshape(n, n * 128).astype(ml_dtypes.bfloat16)


def _mk_wpf(inputs):
    f = np.float32

    def blkT(a, nb):
        C = a.shape[1]
        return np.ascontiguousarray(a).reshape(nb, 128, C).transpose(
            1, 0, 2).reshape(128, nb * C)

    inwT = inputs["L0_attn_in_proj_w"].T.astype(f)
    outwT = inputs["L0_attn_out_proj_w"].T.astype(f)
    w1T = inputs["L0_attn_w1"].T.astype(f)
    w2T = inputs["L0_attn_w2"].T.astype(f)
    mw1 = np.asarray(inputs["L0_ffn_w1"], f)
    mw2 = np.asarray(inputs["L0_ffn_w2"], f)
    wpf = np.empty((NPIECE, 128, PCOLS), f)
    for ph in range(2):
        wpf[ph] = blkT(inwT[ph * 512:(ph + 1) * 512], 4)
    wpf[2] = blkT(outwT, 8)
    for ei in range(NE):
        wpf[3 + ei, :, 0:4096] = blkT(mw1[ei], 8)
        wpf[3 + ei, :, 4096:8192] = blkT(mw2[ei], 4)
    for hg in range(8):
        wpf[11 + hg, :, 0:4096] = blkT(w1T[:, hg * 512:(hg + 1) * 512], 8)
        wpf[11 + hg, :, 4096:8192] = blkT(w2T[hg * 512:(hg + 1) * 512], 4)
    return wpf


def make_common(inputs):
    cv = lambda k: np.ascontiguousarray(inputs[k], dtype=np.float32)
    return {
        "wpf": _mk_wpf(inputs), "convw": cv("L0_attn_conv_w"),
        "convb": cv("L0_attn_conv_b"), "xpw": cv("L0_attn_x_proj_w"),
        "dtw": cv("L0_attn_dt_proj_w"), "dtb": cv("L0_attn_dt_proj_b"),
        "alog": cv("L0_attn_A_log"), "dpar": cv("L0_attn_D_param"),
        "g1": cv("L0_attn_ln1_g"),
        "b1": cv("L0_attn_ln1_b"), "g2": cv("L0_attn_ln2_g"),
        "b2": cv("L0_attn_ln2_b"),
        "b1f": cv("L0_attn_b1"),
        "b2f": cv("L0_attn_b2"),
        "mb1": cv("L0_ffn_b1"),
        "mb2": cv("L0_ffn_b2"), "cent": cv("L0_ffn_centroids"),
        "wrt": cv("L0_ffn_w_route"),
        "ident": np.eye(128, dtype=np.float32),
        "consts": np.tile(np.array(CONSTS, dtype=np.float32), (128, 1)),
        "selbc": _mk_sel(32),
        "selg": _mk_sel(8),
    }


def kernel(**inputs):
    x_full = np.ascontiguousarray(inputs["L0_attn_x"], dtype=np.float32)
    common = make_common(inputs)
    nc, B = build_nc()
    in_maps = []
    for c in range(8):
        m = dict(common)
        m["x"] = np.ascontiguousarray(x_full[c])
        in_maps.append(m)
    res = _run(nc, B, in_maps, list(range(8)))
    out = np.stack([np.asarray(res.results[c]["out"]) for c in range(8)], axis=0)
    return out.astype(np.float32)


# version 21 (final)
# speedup vs baseline: 1.0235x; 1.0057x over previous
import numpy as np
import concourse.bass as bass
import concourse.mybir as mybir
from concourse.bass_utils import run_bass_kernel_spmd

F32 = mybir.dt.float32
BF16 = mybir.dt.bfloat16
AF = mybir.ActivationFunctionType
ALU = mybir.AluOpType
AXX = mybir.AxisListType.X

D = 1024
T = 4096
TC = 256
NCH = T // TC
NTS = TC // 128
DG = 8
NST = 16
DCONV = 8
DTR = 64
HF = 4096
NE = 8
DE = 512
NPIECE = 19
PCOLS = 8192
LN_EPS = 1e-5
PIPELINE = True
DBG_CH = -1
CONSTS = [1e-24, 1e-5, 1.0, 1e-30, -0.5, 1024.0, 0.0, 2.0, -1.0]

ENGINES = ("sync", "tensor", "vector", "scalar", "gpsimd")


NLANES = 12
ENG_SKIP_DIST = {"tensor": 0, "vector": 10, "scalar": 10 ** 9, "gpsimd": 6,
                 "sync": 10 ** 9}


class Builder:

    def __init__(self):
        self.ops = []
        self.bufs = {}
        self.lane_ctr = {}
        self.lane_last = {}
        self.eng_seq = {}

    def _buf(self, name):
        b = self.bufs.get(name)
        if b is None:
            b = {"w": {}, "r": {}}
            self.bufs[name] = b
        return b

    ctx = ("init", -1)

    def op(self, eng, fn, reads=(), writes=(), dma=False, tag="", late=False):
        i = len(self.ops)
        if dma:
            q = "q_" + eng
            cnt = self.lane_ctr.get(q, 0)
            self.lane_ctr[q] = cnt + 1
            lane = cnt % NLANES
            ekey = f"{q}_{lane}"
        else:
            ekey = eng
        deps = {}

        myseq = self.eng_seq.get(ekey, 0)

        def dep(m):
            for k, r in m.items():
                if (k == ekey and not dma and not self.ops[r].get("late")
                        and myseq - self.ops[r]["eseq"] > ENG_SKIP_DIST[eng]):
                    continue
                if deps.get(k, -1) < r:
                    deps[k] = r

        for nm in reads:
            dep(self._buf(nm)["w"])
        for nm in writes:
            b = self._buf(nm)
            dep(b["w"])
            dep(b["r"])
        if dma:
            prev = self.lane_last.get(ekey)
            if prev is not None and deps.get(ekey, -1) < prev:
                deps[ekey] = prev
            self.lane_last[ekey] = i
        for nm in reads:
            b = self._buf(nm)
            if b["r"].get(ekey, -1) < i:
                b["r"][ekey] = i
        for nm in writes:
            b = self._buf(nm)
            b["w"] = dict(b["w"])
            b["w"][ekey] = i
            b["r"] = {}
        self.ops.append(dict(eng=eng, ekey=ekey, fn=fn, deps=deps, dma=dma,
                             tag=tag, need_inc=False, rank=None, late=late,
                             eseq=myseq, ctx=Builder.ctx, rd=tuple(reads),
                             wr=tuple(writes)))
        self.eng_seq[ekey] = myseq + 1
        return i

    def dma(self, eng, fn, reads=(), writes=(), tag=""):
        return self.op(eng, fn, reads, writes, dma=True, tag=tag)

    def drain_all(self, eng):
        deps = {ek: i for ek, i in self.lane_last.items()}
        i = len(self.ops)
        self.ops.append(dict(eng=eng, ekey=eng, fn=lambda e: e.nop(),
                             deps=deps, dma=False, tag="final_drain",
                             need_inc=False, rank=None))
        return i

    def finalize(self, nc):
        for o in self.ops:
            for ek, r in o["deps"].items():
                self.ops[r]["need_inc"] = True
        for o in self.ops:
            if o["dma"]:
                o["need_inc"] = True
        counts = {}
        for o in self.ops:
            if o["need_inc"]:
                counts[o["ekey"]] = counts.get(o["ekey"], 0) + (16 if o["dma"] else 1)
                o["rank"] = counts[o["ekey"]]
        self.final_counts = counts
        sems = {ek: nc.alloc_semaphore("sem_" + ek) for ek in counts}
        self.sems = sems
        per_eng = {e: [] for e in ENGINES}
        for o in self.ops:
            per_eng[o["eng"]].append(o)
        waited = {e: {} for e in ENGINES}

        def emit(eng_name, eng):
            for o in per_eng[eng_name]:
                for ek, r in sorted(o["deps"].items()):
                    dep = self.ops[r]
                    val = dep["rank"]
                    assert val is not None, (o["tag"], dep["tag"])
                    if waited[eng_name].get(ek, 0) < val:
                        eng.wait_ge(sems[ek], val)
                        waited[eng_name][ek] = val
                inst = o["fn"](eng)
                if o["need_inc"]:
                    inst.then_inc(sems[o["ekey"]], 16 if o["dma"] else 1)
        return emit


def build_nc(nchunks=NCH, debug=()):
    nc = bass.Bass("TRN2", target_bir_lowering=False)
    B = Builder()

    def din(name, shape):
        return nc.dram_tensor(name, shape, F32, kind="ExternalInput").ap()

    x_d = din("x", [T, D])
    wpf_d = din("wpf", [NPIECE, 128, PCOLS])
    convw_d = din("convw", [D, DCONV])
    convb_d = din("convb", [D])
    xpw_d = din("xpw", [96, D])
    dtw_d = din("dtw", [D, DTR])
    dtb_d = din("dtb", [D])
    alog_d = din("alog", [D, NST])
    dpar_d = din("dpar", [D])
    g1_d = din("g1", [D])
    b1_d = din("b1", [D])
    g2_d = din("g2", [D])
    b2_d = din("b2", [D])
    b1f_d = din("b1f", [HF])
    b2f_d = din("b2f", [D])
    mb1_d = din("mb1", [NE, DE])
    mb2_d = din("mb2", [NE, D])
    cent_d = din("cent", [NE, D])
    wrt_d = din("wrt", [NE, D])
    ident_d = din("ident", [128, 128])
    consts_d = din("consts", [128, len(CONSTS)])
    selbc_d = nc.dram_tensor("selbc", [32, 32 * 128], BF16, kind="ExternalInput").ap()
    selg_d = nc.dram_tensor("selg", [8, 8 * 128], BF16, kind="ExternalInput").ap()

    out_d = nc.dram_tensor("out", [T, D], F32, kind="ExternalOutput").ap()

    wpieces = nc.dram_tensor("wpieces", [NPIECE, 128, PCOLS], BF16).ap()

    def S(name, shape, dt=F32, **kw):
        return nc.alloc_sbuf_tensor(name, shape, dt, **kw)

    x_tm = [S(f"x_tm{i}", [128, NTS * D]) for i in range(2)]
    xn_tm = S("xn_tm", [128, NTS * D])
    xn_bf = [S(f"xn_bf{i}", [128, DG * TC], BF16) for i in range(2)]
    xns_bf = S("xns_bf", [128, DG * TC], BF16)
    s1b_sb = S("s1b_sb", [128, TC])
    rstd2_sb = S("rstd2_sb", [128, TC])
    m2neg_sb = S("m2neg_sb", [128, TC])
    ones_f = S("ones_f", [1, 128])
    xin = S("xin", [128, DG * (TC + 8)])
    sz = S("sz", [128, DG * TC])
    xc = S("xc", [128, DG * TC])
    xc_bf = S("xc_bf", [128, DG * TC], BF16)
    delta = S("delta", [128, DG * TC])
    uful = xn_tm
    dA = [S(f"dA{i}", [128, TC]) for i in range(3)]
    dBx = [S(f"dBx{i}", [128, TC]) for i in range(3)]
    hs = [S(f"hs{i}", [128, TC]) for i in range(3)]
    gy = [S(f"gy{i}", [128, TC], BF16) for i in range(8)]
    crep = [S(f"crep{i}", [128, TC]) for i in range(2)]
    carry = S("carry", [128, DG * NST])
    t1s = [S(f"t1s{i}", [128, TC]) for i in range(2)]
    ymix_bf = [S(f"ymix_bf{i}", [128, DG * TC], BF16) for i in range(2)]
    x1 = S("x1", [128, DG * TC])
    hgm = S("hgm", [128, 32 * TC], BF16)
    xsq_bf = hgm[:, 0:DG * TC]
    lnp_bf = S("lnp_bf", [128, DG * TC], BF16)
    sS = S("sS", [128, DG * TC])
    gfin = S("gfin", [128, DG * TC], BF16)
    brep_sb = [S(f"brep{i}", [128, TC]) for i in range(2)]
    etmp2 = S("etmp2", [128, TC])
    acc = [dA[0], dA[1]]
    etmp = dA[2]
    slots = [S(f"slot{i}", [128, PCOLS], BF16, is_transpose_dest=True)
             for i in range(2)]
    rep_f = x_tm

    statc = S("statc", [128, 16])
    stile_s1 = S("stile_s1", [128, NTS])
    stile_rn = S("stile_rn", [128, NTS])
    trow_s1 = S("trow_s1", [1, TC])
    trow_rn = [S(f"trow_rn{i}", [1, TC]) for i in range(2)]
    r_mu2 = S("r_mu2", [1, TC])
    r_rstd2 = S("r_rstd2", [1, TC])
    r_m2neg = S("r_m2neg", [1, TC])
    r_tmp = S("r_tmp", [1, TC])
    dt_bf = S("dt_bf", [DTR, TC], BF16)
    bc_bf = S("bc_bf", [2 * NST, TC], BF16)
    selbc_s = S("selbc_s", [32, 32 * 128], BF16)
    selg_s = S("selg_s", [8, 8 * 128], BF16)
    ltrow = S("ltrow", [8, TC])
    lt_tm = S("lt_tm", [128, 8 * NTS])
    gmax = S("gmax", [128, 8 * NTS])
    gwrk = S("gwrk", [128, 8 * NTS])
    gmsk = S("gmsk", [128, 8 * NTS])
    gneg = S("gneg", [128, NTS])
    gden = S("gden", [128, NTS])
    gT_bf = S("gT_bf", [8, TC], BF16)
    ident = S("identsb", [128, 128])
    cc = S("cc", [128, len(CONSTS)])
    ident_bf = S("ident_bf", [128, 128], BF16)
    ones_bf = S("ones_bf", [1, 128], BF16)
    zrow_bf = S("zrow_bf", [1, 512], BF16)
    onec_bf = S("onec_bf", [128, 1], BF16)
    Afm = S("Afm", [128, DG * NST])
    convw_s = S("convw_s", [128, DG * DCONV])
    convb_s = S("convb_s", [128, DG])
    dtb_s = S("dtb_s", [128, DG])
    dpar_s = S("dpar_s", [128, DG])
    g1_s = S("g1_s", [128, DG])
    g2_s = S("g2_s", [128, DG])
    b2f_s = S("b2f_s", [128, DG])
    b1_f = S("b1_f", [128, DG])
    b2_f = S("b2_f", [128, DG])
    b1_bf = S("b1_bf", [128, DG], BF16)
    b2ln_bf = S("b2ln_bf", [128, DG], BF16)
    wb1 = S("wb1", [128, 16])
    wb1n = S("wb1n", [128, 16])
    wb2f = S("wb2f", [128, 32])
    b1f_s = S("b1f_s", [128, 32])
    mb1_s = S("mb1_s", [128, NE * 4])
    mb2_bf = S("mb2_bf", [8, D], BF16)
    cw_bf = S("cw_bf", [128, DG * 16], BF16)
    rc_s = S("rc_s", [8, 4])
    xpT_bf = S("xpT_bf", [128, DG * 96], BF16)
    dtT_bf = S("dtT_bf", [DTR, DG * 128], BF16)
    wos = S("wos", [128, DG])
    wos_bf = S("wos_bf", [128, DG], BF16)

    csq = xn_tm[0:8, 0:1024]
    mb2_f = xn_tm[32:40, 0:1024]
    dtw_s = xn_tm[0:128, 1024:1536]
    xpw_s = x1[0:96, 0:1024]
    cent_s = x1[0:8, 1024:2048]
    wrt_s = x1[32:40, 1024:2048]
    pb = [nc.alloc_psum_tensor(f"pb{i}", [128, 512], F32) for i in range(8)]

    dbg_d = {}
    dbg_src = {}

    def half(bank, h):
        return pb[bank][:, h * TC:(h + 1) * TC]

    B.dma("gpsimd", lambda e: e.dma_start(out=ident[:, :], in_=ident_d),
          writes=["ident"], tag="ld_ident")
    B.dma("gpsimd", lambda e: e.dma_start(out=cc[:, :], in_=consts_d),
          writes=["cc"], tag="ld_cc")
    B.dma("gpsimd", lambda e: e.dma_start(out=selbc_s[:, :], in_=selbc_d),
          writes=["selbc_s"], tag="ld_selbc")
    B.dma("gpsimd", lambda e: e.dma_start(out=selg_s[:, :], in_=selg_d),
          writes=["selg_s"], tag="ld_selg")
    for ci, cval in enumerate(CONSTS):
        nc.const_aps.register(F32, cval, cc[:, ci:ci + 1])
    B.op("vector", lambda e: e.tensor_copy(out=ident_bf[:, :], in_=ident[:, :]),
         reads=["ident"], writes=["ident_bf"])
    B.op("vector", lambda e: e.memset(ones_bf[:, :], 1.0), writes=["ones_bf"])
    B.op("vector", lambda e: e.memset(ones_f[:, :], 1.0), writes=["ones_f"])
    B.op("vector", lambda e: e.memset(zrow_bf[:, :], 0.0), writes=["zrow_bf"])
    B.op("vector", lambda e: e.memset(onec_bf[:, :], 1.0), writes=["onec_bf"])
    B.op("gpsimd", lambda e: e.memset(carry[:, :], 0.0), writes=["carry"])
    for dg in range(DG):
        o = dg * (TC + 8)
        B.op("gpsimd", lambda e, o=o: e.memset(xin[:, o:o + 8], 0.0),
             writes=["xin_head"])

    def vec_load(dst, src_ap, name):
        B.dma("gpsimd",
              lambda e: e.dma_start(out=dst[:, :],
                                    in_=src_ap.rearrange("(g p) -> p g", p=128),
                                    allow_slow_non_contiguous=True),
              writes=[name], tag="ld_" + name)

    vec_load(convb_s, convb_d, "convb_s")
    vec_load(dtb_s, dtb_d, "dtb_s")
    vec_load(dpar_s, dpar_d, "dpar_s")
    vec_load(g1_s, g1_d, "g1_s")
    vec_load(g2_s, g2_d, "g2_s")
    vec_load(b2f_s, b2f_d, "b2f_s")
    vec_load(b1_f, b1_d, "b1_f")
    vec_load(b2_f, b2_d, "b2_f")
    B.op("vector", lambda e: e.tensor_copy(out=b1_bf[:, :], in_=b1_f[:, :]),
         reads=["b1_f"], writes=["b1_bf"])
    B.op("vector", lambda e: e.tensor_copy(out=b2ln_bf[:, :], in_=b2_f[:, :]),
         reads=["b2_f"], writes=["b2ln_bf"])
    B.dma("gpsimd", lambda e: e.dma_start(
        out=convw_s[:, :], in_=convw_d.rearrange("(g p) k -> p g k", p=128),
        allow_slow_non_contiguous=True),
        writes=["convw_s"], tag="ld_convw")
    B.dma("gpsimd", lambda e: e.dma_start(
        out=Afm[:, :], in_=alog_d.rearrange("(g p) n -> p g n", p=128),
        allow_slow_non_contiguous=True),
        writes=["Afm"], tag="ld_alog")
    B.dma("gpsimd", lambda e: e.dma_start(
        out=mb1_s[:, :], in_=mb1_d.rearrange("e (t p) -> p e t", p=128),
        allow_slow_non_contiguous=True),
        writes=["mb1_s"], tag="ld_mb1")
    B.dma("gpsimd", lambda e: e.dma_start(
        out=b1f_s[:, :], in_=b1f_d.rearrange("(t p) -> p t", p=128),
        allow_slow_non_contiguous=True),
        writes=["b1f_s"], tag="ld_b1f")
    B.dma("gpsimd", lambda e: e.dma_start(out=cent_s, in_=cent_d),
          writes=["x1"], tag="ld_cent")
    B.dma("gpsimd", lambda e: e.dma_start(out=wrt_s, in_=wrt_d),
          writes=["x1"], tag="ld_wrt")
    B.dma("gpsimd", lambda e: e.dma_start(out=mb2_f, in_=mb2_d),
          writes=["xn_tm"], tag="ld_mb2")
    B.op("vector", lambda e: e.tensor_copy(out=mb2_bf[:, :], in_=mb2_f),
         reads=["xn_tm"], writes=["mb2_bf"])
    B.dma("gpsimd", lambda e: e.dma_start(out=xpw_s, in_=xpw_d),
          writes=["x1"], tag="ld_xpw")
    B.dma("gpsimd", lambda e: e.dma_start(
        out=dtw_s, in_=dtw_d.rearrange("(g p) r -> p g r", p=128),
        allow_slow_non_contiguous=True),
        writes=["xn_tm"], tag="ld_dtw")

    B.op("scalar", lambda e: e.activation(out=Afm[:, :], in_=Afm[:, :], func=AF.Exp),
         reads=["Afm"], writes=["Afm"], tag="mkA")
    B.op("vector", lambda e: e.tensor_scalar_mul(Afm[:, :], Afm[:, :], -1.0),
         reads=["Afm"], writes=["Afm"])

    B.op("scalar", lambda e: e.activation(out=csq, in_=cent_s,
                                          func=AF.Square, accum_out=rc_s[:, 0:1]),
         reads=["x1"], writes=["xn_tm", "rc_s"], tag="rc_sq", late=True)
    B.op("scalar", lambda e: e.activation(out=rc_s[:, 1:2], in_=rc_s[:, 0:1],
                                          func=AF.Ln, bias=1e-24),
         reads=["rc_s", "cc"], writes=["rc_s"])
    B.op("scalar", lambda e: e.activation(out=rc_s[:, 2:3], in_=rc_s[:, 1:2],
                                          func=AF.Exp, scale=-0.5),
         reads=["rc_s", "cc"], writes=["rc_s"])

    for kc in range(DG):
        B.op("tensor", lambda e, kc=kc: e.transpose(
            out=pb[6][0:128, 0:96], in_=xpw_s[:, kc * 128:(kc + 1) * 128],
            identity=ident[0:96, 0:96]),
            reads=["x1", "ident", "pb6"], writes=["pb6"], tag=f"tr_xp{kc}")
        B.op("scalar", lambda e, kc=kc: e.copy(
            out=xpT_bf[:, kc * 96:(kc + 1) * 96], in_=pb[6][0:128, 0:96]),
            reads=["pb6"], writes=["xpT_bf"])
        B.op("tensor", lambda e, kc=kc: e.transpose(
            out=pb[7][0:DTR, 0:128], in_=dtw_s[:, kc * DTR:(kc + 1) * DTR],
            identity=ident[:, :]),
            reads=["xn_tm", "ident", "pb7"], writes=["pb7"], tag=f"tr_dt{kc}")
        B.op("scalar", lambda e, kc=kc: e.copy(
            out=dtT_bf[:, kc * 128:(kc + 1) * 128], in_=pb[7][0:DTR, 0:128]),
            reads=["pb7"], writes=["dtT_bf"])
        B.op("tensor", lambda e, kc=kc: e.transpose(
            out=pb[6][0:128, 96:104], in_=cent_s[:, kc * 128:(kc + 1) * 128],
            identity=ident[0:8, 0:8]),
            reads=["x1", "ident", "pb6"], writes=["pb6"], tag=f"tr_ce{kc}")
        B.op("tensor", lambda e, kc=kc: e.transpose(
            out=pb[6][0:128, 104:112], in_=wrt_s[:, kc * 128:(kc + 1) * 128],
            identity=ident[32:40, 32:40]),
            reads=["x1", "ident", "pb6"], writes=["pb6"])
        B.op("scalar", lambda e, kc=kc: e.copy(
            out=cw_bf[:, kc * 16:(kc + 1) * 16], in_=pb[6][0:128, 96:112]),
            reads=["pb6"], writes=["cw_bf"])

    LSZ = 1024
    NLANE = 2 * (PCOLS // LSZ)
    lane_free = list(range(NLANE))

    def lane_view(j):
        return slots[j // (PCOLS // LSZ)][
            :, (j % (PCOLS // LSZ)) * LSZ:((j % (PCOLS // LSZ)) + 1) * LSZ]

    wp_written = set()
    rgen_cell = [None]

    def load_lane(ch, pi, c0):
        while (pi, c0) not in wp_written and rgen_cell[0] is not None:
            if next(rgen_cell[0], "END") == "END":
                rgen_cell[0] = None
        assert (pi, c0) in wp_written, (pi, c0)
        j = alloc_lane()
        lv = lane_view(j)
        B.dma("sync", lambda e, pi=pi, lv=lv, c0=c0: e.dma_start(
            out=lv, in_=wpieces[pi][:, c0:c0 + LSZ]),
            reads=[f"wp{pi}"], writes=[f"lane{j}"], tag=f"c{ch}_p{pi}_{c0}")
        return lv, f"lane{j}", j

    def free_lane(j):
        lane_free.append(j)

    def alloc_lane():
        assert lane_free, "lane ring exhausted: missing free_lane"
        return lane_free.pop(0)

    def repack_gen():
        Builder.ctx = ("repack", -1)
        rp_stage = [(x1[0:128, 0:2048], "x1"), (sS[:, 0:2048], "sS")]
        rp_cast = [(lnp_bf, "lnp_bf"), (gfin, "gfin")]
        for pi in range(NPIECE):
            for q4 in range(4):
                p = (pi * 4 + q4) % 2
                sl, sln = rp_stage[p]
                cd, cdn = rp_cast[p]
                c0 = q4 * 2048
                B.dma("sync" if p == 0 else "scalar",
                      lambda e, pi=pi, c0=c0, sl=sl: e.dma_start(
                    out=sl[:, 0:2048], in_=wpf_d[pi][:, c0:c0 + 2048]),
                    writes=[sln], tag=f"rp_l{pi}_{c0}")
                if p == 0:
                    B.op("vector", lambda e, sl=sl, cd=cd: e.tensor_copy(
                        out=cd[:, 0:2048], in_=sl[:, 0:2048]),
                        reads=[sln], writes=[cdn], tag=f"rp_c{pi}_{c0}")
                else:
                    B.op("scalar", lambda e, sl=sl, cd=cd: e.copy(
                        out=cd[:, 0:2048], in_=sl[:, 0:2048]),
                        reads=[sln], writes=[cdn], tag=f"rp_c{pi}_{c0}")
                B.dma("gpsimd", lambda e, pi=pi, c0=c0, cd=cd: e.dma_start(
                    out=wpieces[pi][:, c0:c0 + 2048], in_=cd[:, 0:2048]),
                    reads=[cdn], writes=[f"wp{pi}"], tag=f"rp_s{pi}_{c0}")
                wp_written.add((pi, c0))
                wp_written.add((pi, c0 + 1024))
                yield


    def zero_mm(bank, tag):
        B.op("tensor", lambda e, bank=bank: e.matmul(
            out=pb[bank][:, 0:512], lhsT=zrow_bf[:, 0:128], rhs=zrow_bf[:, 0:512],
            start=True, stop=False),
            reads=["zrow_bf", f"pb{bank}"], writes=[f"pb{bank}"], tag=tag)

    def front_head(ch):
        Builder.ctx = ("front", ch)
        """P0..P3 (generator): x load, stats, xn, in_proj, conv, dbl, delta."""
        t0 = ch * TC
        xb = x_tm[ch % 2]
        xbn = f"x_tm{ch % 2}"
        pz = ch % 2
        xnb = xn_bf[pz]
        xnbn = f"xn_bf{pz}"
        B.dma("gpsimd", lambda e, t0=t0, xb=xb: e.dma_start(
            out=xb[:, :], in_=x_d[t0:t0 + TC, :].rearrange("(s p) d -> p s d", p=128)),
            writes=[xbn], tag=f"c{ch}_ldx")

        def ld_k(kc, rnd):
            return load_lane(ch, kc // 4, (kc % 4) * 2048 + rnd * 1024)

        for ts in range(NTS):
            B.op("vector", lambda e, ts=ts, xb=xb: e.tensor_scalar(
                out=xn_tm[:, ts * D:(ts + 1) * D], in0=xb[:, ts * D:(ts + 1) * D],
                scalar1=1.0, scalar2=0.0, op0=ALU.mult, op1=ALU.add,
                accum_out=statc[:, ts * 8:ts * 8 + 1]),
                reads=[xbn],
                writes=["xn_tm", "statc"] + [f"u{g}" for g in range(DG)],
                tag=f"c{ch}_sx{ts}",
                late=True)
            B.op("vector", lambda e, ts=ts, xb=xb: e.scalar_tensor_tensor(
                out=xn_tm[:, ts * D:(ts + 1) * D], in0=xb[:, ts * D:(ts + 1) * D],
                scalar=1.0, in1=xb[:, ts * D:(ts + 1) * D], op0=ALU.mult,
                op1=ALU.mult, accum_out=statc[:, ts * 8 + 1:ts * 8 + 2]),
                reads=[xbn],
                writes=["xn_tm", "statc"] + [f"u{g}" for g in range(DG)],
                tag=f"c{ch}_sxx{ts}",
                late=True)
        for ts in range(NTS):
            c = ts * 8
            B.op("vector", lambda e, c=c: e.tensor_scalar(
                out=statc[:, c + 2:c + 3], in0=statc[:, c:c + 1],
                scalar1=1.0 / D, scalar2=None, op0=ALU.mult),
                reads=["statc"], writes=["statc"])
            B.op("vector", lambda e, c=c: e.tensor_scalar(
                out=statc[:, c + 7:c + 8], in0=statc[:, c + 1:c + 2],
                scalar1=1.0 / D, scalar2=None, op0=ALU.mult),
                reads=["statc"], writes=["statc"])
            B.op("vector", lambda e, c=c: e.scalar_tensor_tensor(
                out=statc[:, c + 3:c + 4], in0=statc[:, c + 2:c + 3], scalar=-1.0,
                in1=statc[:, c + 2:c + 3], op0=ALU.mult, op1=ALU.mult),
                reads=["statc"], writes=["statc"])
            B.op("vector", lambda e, c=c: e.tensor_add(
                out=statc[:, c + 3:c + 4], in0=statc[:, c + 7:c + 8],
                in1=statc[:, c + 3:c + 4]),
                reads=["statc"], writes=["statc"])
            B.op("scalar", lambda e, c=c: e.activation(
                out=statc[:, c + 4:c + 5], in_=statc[:, c + 3:c + 4],
                func=AF.Ln, bias=LN_EPS),
                reads=["statc", "cc"], writes=["statc"], tag=f"c{ch}_l0{c}")
            B.op("scalar", lambda e, c=c: e.activation(
                out=statc[:, c + 4:c + 5], in_=statc[:, c + 4:c + 5],
                func=AF.Exp, scale=-0.5),
                reads=["statc", "cc"], writes=["statc"])
            B.op("vector", lambda e, c=c: e.scalar_tensor_tensor(
                out=statc[:, c + 7:c + 8], in0=statc[:, c + 4:c + 5], scalar=1.0,
                in1=statc[:, c + 4:c + 5], op0=ALU.mult, op1=ALU.mult),
                reads=["statc"], writes=["statc"])
            B.op("vector", lambda e, c=c: e.tensor_mul(
                out=statc[:, c + 7:c + 8], in0=statc[:, c + 7:c + 8],
                in1=statc[:, c + 3:c + 4]),
                reads=["statc"], writes=["statc"])
            B.op("scalar", lambda e, c=c: e.activation(
                out=statc[:, c + 5:c + 6], in_=statc[:, c + 7:c + 8],
                func=AF.Ln, bias=LN_EPS),
                reads=["statc", "cc"], writes=["statc"])
            B.op("scalar", lambda e, c=c: e.activation(
                out=statc[:, c + 5:c + 6], in_=statc[:, c + 5:c + 6],
                func=AF.Exp, scale=-0.5),
                reads=["statc", "cc"], writes=["statc"])
            B.op("scalar", lambda e, c=c: e.activation(
                out=statc[:, c + 6:c + 7], in_=statc[:, c + 7:c + 8],
                func=AF.Ln, scale=float(D), bias=1e-30),
                reads=["statc", "cc"], writes=["statc"])
            B.op("scalar", lambda e, c=c: e.activation(
                out=statc[:, c + 6:c + 7], in_=statc[:, c + 6:c + 7],
                func=AF.Exp, scale=-0.5),
                reads=["statc", "cc"], writes=["statc"])
            B.op("vector", lambda e, c=c, ts=ts: e.tensor_copy(
                out=stile_s1[:, ts:ts + 1], in_=statc[:, c + 5:c + 6]),
                reads=["statc"], writes=["stile"])
            B.op("vector", lambda e, c=c, ts=ts: e.tensor_copy(
                out=stile_rn[:, ts:ts + 1], in_=statc[:, c + 6:c + 7]),
                reads=["statc"], writes=["stile"])
            yield "H"
        for ts in range(NTS):
            c = ts * 8
            B.op("vector", lambda e, c=c, ts=ts, xb=xb: e.tensor_scalar(
                out=xn_tm[:, ts * D:(ts + 1) * D], in0=xb[:, ts * D:(ts + 1) * D],
                scalar1=statc[:, c + 2:c + 3], scalar2=statc[:, c + 4:c + 5],
                op0=ALU.subtract, op1=ALU.mult),
                reads=[xbn, "statc"],
                writes=["xn_tm"] + [f"u{g}" for g in range(DG)],
                tag=f"c{ch}_xn{ts}")

        for ts in range(NTS):
            B.op("tensor", lambda e, ts=ts: e.transpose(
                out=pb[2][0:1, 0:128], in_=stile_s1[:, ts:ts + 1],
                identity=ident[:, :]),
                reads=["stile", "ident", "pb2"], writes=["pb2"], tag=f"c{ch}_trs{ts}")
            B.op("vector", lambda e, ts=ts: e.tensor_copy(
                out=trow_s1[:, ts * 128:(ts + 1) * 128], in_=pb[2][0:1, 0:128]),
                reads=["pb2"], writes=["trow"])
            B.op("tensor", lambda e, ts=ts: e.transpose(
                out=pb[3][0:1, 0:128], in_=stile_rn[:, ts:ts + 1],
                identity=ident[:, :]),
                reads=["stile", "ident", "pb3"], writes=["pb3"], tag=f"c{ch}_trr{ts}")
            B.op("vector", lambda e, ts=ts, pz=pz: e.tensor_copy(
                out=trow_rn[pz][:, ts * 128:(ts + 1) * 128], in_=pb[3][0:1, 0:128]),
                reads=["pb3"], writes=[f"trow_rn{pz}"])
        B.op("tensor", lambda e: e.matmul(
            out=pb[2][:, 0:TC], lhsT=ones_f[:, :], rhs=trow_s1[:, :],
            start=True, stop=True),
            reads=["ones_f", "trow", "pb2"], writes=["pb2"], tag=f"c{ch}_bs1")
        B.op("vector", lambda e: e.tensor_copy(out=s1b_sb[:, :], in_=pb[2][:, 0:TC]),
             reads=["pb2"], writes=["s1b_sb"], tag=f"c{ch}_es1b")
        yield "H"

        for ti in range(8 * NTS):
            ts, dg = divmod(ti, 8)
            bank = ti % 2
            B.op("tensor", lambda e, ts=ts, dg=dg, bank=bank: e.transpose(
                out=pb[bank][:, 0:128],
                in_=xn_tm[:, ts * D + dg * 128:ts * D + (dg + 1) * 128],
                identity=ident[:, :]),
                reads=["xn_tm", "ident", f"pb{bank}"], writes=[f"pb{bank}"],
                tag=f"c{ch}_txn{ts}_{dg}")
            B.op("vector", lambda e, ts=ts, dg=dg, bank=bank: e.scalar_tensor_tensor(
                out=xns_bf[:, dg * TC + ts * 128:dg * TC + (ts + 1) * 128],
                in0=pb[bank][:, 0:128], scalar=g1_s[:, dg:dg + 1],
                in1=s1b_sb[:, ts * 128:(ts + 1) * 128], op0=ALU.mult, op1=ALU.mult),
                reads=[f"pb{bank}", "g1_s", "s1b_sb"], writes=["xns_bf"],
                tag=f"c{ch}_xns{ts}_{dg}")
            B.op("vector", lambda e, ts=ts, dg=dg, bank=bank, xnb=xnb: e.tensor_copy(
                out=xnb[:, dg * TC + ts * 128:dg * TC + (ts + 1) * 128],
                in_=pb[bank][:, 0:128]),
                reads=[f"pb{bank}"], writes=[xnbn], tag=f"c{ch}_exn{ts}_{dg}")
            if ti % 4 == 3:
                yield "H"

        if ch > 0:
            for dg in range(DG):
                o = dg * (TC + 8)
                B.op("vector", lambda e, o=o: e.tensor_copy(
                    out=xin[:, o + 1:o + 8], in_=xin[:, o + TC + 1:o + TC + 8]),
                    reads=["xin"], writes=["xin_head"], tag=f"c{ch}_cvh{dg}")

        if ch == 0:
            for kc in range(8):
                lns = [ld_k(kc, 0), ld_k(kc, 1)]
                for et in range(16):
                    sl, sn, _ = lns[et // 8]
                    B.op("tensor", lambda e, sl=sl, et=et, kc=kc: e.matmul(
                        out=pb[0][:, et:et + 1],
                        lhsT=sl[:, (et % 8) * 128:(et % 8 + 1) * 128],
                        rhs=b1_bf[:, kc:kc + 1],
                        start=(et == 0 and kc == 0), stop=(et == 15 and kc == 7)),
                        reads=[sn, "b1_bf", "pb0"], writes=["pb0"],
                        tag=f"wb1_{et}_{kc}")
                free_lane(lns[0][2])
                free_lane(lns[1][2])
                yield "H"
            B.op("vector", lambda e: e.tensor_copy(out=wb1[:, :], in_=pb[0][:, 0:16]),
                 reads=["pb0"], writes=["wb1"], tag="wb1_ev")
            B.op("vector", lambda e: e.tensor_scalar(
                out=wb1n[:, :], in0=wb1[:, :], scalar1=-1.0, scalar2=None,
                op0=ALU.mult),
                reads=["wb1"], writes=["wb1n"], tag="wb1n")

        def conv_unit(dg):
            o = dg * (TC + 8)
            a = acc[dg % 2]
            an = f"dA{dg % 2}"
            B.op("vector", lambda e, o=o, a=a, dg=dg: e.tensor_scalar(
                out=a[:, :], in0=xin[:, o + 1:o + 1 + TC],
                scalar1=convw_s[:, dg * DCONV:dg * DCONV + 1],
                scalar2=convb_s[:, dg:dg + 1], op0=ALU.mult, op1=ALU.add),
                reads=["xin", "xin_head", "convw_s", "convb_s"], writes=[an],
                tag=f"c{ch}_cv0_{dg}")
            for k in range(1, DCONV):
                B.op("vector", lambda e, o=o, a=a, dg=dg, k=k, an=an:
                     e.scalar_tensor_tensor(
                        out=a[:, :], in0=xin[:, o + 1 + k:o + 1 + k + TC],
                        scalar=convw_s[:, dg * DCONV + k:dg * DCONV + k + 1],
                        in1=a[:, :], op0=ALU.mult, op1=ALU.add),
                     reads=["xin", "xin_head", "convw_s", an], writes=[an],
                     tag=f"c{ch}_cv{k}_{dg}")
            B.op("scalar", lambda e, a=a, dg=dg, an=an: e.activation(
                out=xc[:, dg * TC:(dg + 1) * TC], in_=a[:, :], func=AF.Silu),
                reads=[an], writes=["xc"], tag=f"c{ch}_xc{dg}")

        for rnd in range(2):
            for b in range(4):
                zero_mm(b, f"c{ch}_iz{rnd}_{b}")
            pend = [ld_k(0, rnd), ld_k(1, rnd)]
            for kc in range(8):
                sl, sn, jl = pend.pop(0)
                if kc + 2 < 8:
                    pend.append(ld_k(kc + 2, rnd))
                for et in range(rnd * 8, rnd * 8 + 8):
                    bank, hh = et % 4, (et // 4) % 2
                    last = (et == rnd * 8 + 4 + bank and kc == 7)
                    B.op("tensor", lambda e, sl=sl, et=et, kc=kc, rnd=rnd,
                         bank=bank, hh=hh, last=last: e.matmul(
                            out=half(bank, hh),
                            lhsT=sl[:, (et - rnd * 8) * 128:
                                    (et - rnd * 8 + 1) * 128],
                            rhs=xns_bf[:, kc * TC:(kc + 1) * TC],
                            start=False, stop=last),
                         reads=[sn, "xns_bf", f"pb{bank}"], writes=[f"pb{bank}"],
                         tag=f"c{ch}_inp{et}_{kc}")
                free_lane(jl)
                if rnd == 1:
                    conv_unit(kc)
                yield "H"
            for et in range(rnd * 8, rnd * 8 + 8):
                bank, hh = et % 4, (et // 4) % 2
                if et < 8:
                    o = et * (TC + 8) + 8
                    B.op("scalar", lambda e, bank=bank, hh=hh, o=o, et=et:
                         e.activation(
                            out=xin[:, o:o + TC], in_=half(bank, hh),
                            func=AF.Identity, bias=wb1[:, et:et + 1]),
                         reads=[f"pb{bank}", "wb1", "xin_head"], writes=["xin"],
                         tag=f"c{ch}_exin{et}")
                else:
                    dg = et - 8
                    B.op("scalar", lambda e, bank=bank, hh=hh, dg=dg, et=et:
                         e.activation(
                            out=sz[:, dg * TC:(dg + 1) * TC], in_=half(bank, hh),
                            func=AF.Silu, bias=wb1[:, et:et + 1]),
                         reads=[f"pb{bank}", "wb1"], writes=["sz"],
                         tag=f"c{ch}_esz{dg}")
            yield "H"

        for dg in range(DG):
            B.op("vector", lambda e, dg=dg: e.tensor_copy(
                out=xc_bf[:, dg * TC:(dg + 1) * TC],
                in_=xc[:, dg * TC:(dg + 1) * TC]),
                reads=["xc"], writes=["xc_bf"], tag=f"c{ch}_xcbf{dg}")
            if dg % 4 == 3:
                yield "Hc"

        for kc in range(8):
            B.op("tensor", lambda e, kc=kc: e.matmul(
                out=pb[2][0:96, 0:TC], lhsT=xpT_bf[:, kc * 96:(kc + 1) * 96],
                rhs=xc_bf[:, kc * TC:(kc + 1) * TC],
                start=(kc == 0), stop=(kc == 7)),
                reads=["xpT_bf", "xc_bf", "pb2"], writes=["pb2"], tag=f"c{ch}_xp{kc}")
        B.op("scalar", lambda e: e.copy(out=dt_bf[:, :], in_=pb[2][0:DTR, 0:TC]),
             reads=["pb2"], writes=["dt_bf"], tag=f"c{ch}_edt")
        B.op("scalar", lambda e: e.copy(out=bc_bf[:, :], in_=pb[2][DTR:96, 0:TC]),
             reads=["pb2"], writes=["bc_bf"], tag=f"c{ch}_ebc")
        yield "Hc"
        for dg in range(DG):
            bank = dg % 2
            B.op("tensor", lambda e, dg=dg, bank=bank: e.matmul(
                out=pb[bank][:, 0:TC], lhsT=dtT_bf[:, dg * 128:(dg + 1) * 128],
                rhs=dt_bf[:, :], start=True, stop=True),
                reads=["dtT_bf", "dt_bf", f"pb{bank}"], writes=[f"pb{bank}"],
                tag=f"c{ch}_dtp{dg}")
            B.op("scalar", lambda e, dg=dg, bank=bank: e.activation(
                out=etmp[:, :], in_=pb[bank][:, 0:TC], func=AF.Exp,
                bias=dtb_s[:, dg:dg + 1]),
                reads=[f"pb{bank}", "dtb_s"], writes=["dA2"], tag=f"c{ch}_sp1_{dg}")
            B.op("scalar", lambda e, dg=dg: e.activation(
                out=delta[:, dg * TC:(dg + 1) * TC], in_=etmp[:, :], func=AF.Ln,
                bias=1.0),
                reads=["dA2", "cc"], writes=[f"delta{dg}"], tag=f"c{ch}_sp2_{dg}")
            if dg % 2 == 1:
                yield "Hc"
        for dg in range(DG):
            B.op("vector", lambda e, dg=dg: e.tensor_mul(
                out=uful[:, dg * TC:(dg + 1) * TC],
                in0=delta[:, dg * TC:(dg + 1) * TC],
                in1=xc[:, dg * TC:(dg + 1) * TC]),
                reads=[f"delta{dg}", "xc"], writes=["xn_tm", f"u{dg}"],
                tag=f"c{ch}_u{dg}")
        if ch == DBG_CH:
            for nm, hdl, tok in [("delta", delta, "delta0"), ("xcbf", xc_bf, "xc_bf"),
                                 ("dtbf", dt_bf, "dt_bf"), ("bcbf", bc_bf, "bc_bf"),
                                 ("xinx", xin, "xin"), ("xnsx", xns_bf, "xns_bf"),
                                 ("ufulx", uful, "xn_tm"), ("xcf", xc, "xc")]:
                dd = nc.dram_tensor(f"dbg_{nm}", [hdl.shape[0], hdl.shape[1]],
                                    hdl.dtype if hasattr(hdl, "dtype") else F32,
                                    kind="ExternalOutput").ap()
                B.dma("gpsimd", lambda e, dd=dd, hdl=hdl: e.dma_start(
                    out=dd, in_=hdl[:, :] if hasattr(hdl, "shape") else hdl),
                    reads=[tok], writes=[f"dbg{nm}"], tag=f"dbg_{nm}")

    def front_scan(ch):
        Builder.ctx = ("front", ch)
        """P4 scan: 2 passes of 4 d-groups; y in banks 0/1, B/C selector
        broadcasts in banks 2/3. ysum matmuls are emitted one state late so
        the tensor engine never blocks on the scan chain."""
        pz = ch % 2
        for p2 in range(2):
            dgs = [p2 * 4 + i for i in range(4)]
            zero_mm(0, f"c{ch}_yz{p2}_0")
            zero_mm(1, f"c{ch}_yz{p2}_1")
            pend = []
            for n in range(NST):
                bB = 2 + n % 2
                bC = 3 - n % 2
                B.op("tensor", lambda e, n=n, bB=bB: e.matmul(
                    out=pb[bB][:, 0:TC], lhsT=selbc_s[:, n * 128:(n + 1) * 128],
                    rhs=bc_bf[:, :], start=True, stop=True),
                    reads=["selbc_s", "bc_bf", f"pb{bB}"], writes=[f"pb{bB}"],
                    tag=f"c{ch}_brep{p2}_{n}")
                B.op("tensor", lambda e, n=n, bC=bC: e.matmul(
                    out=pb[bC][:, 0:TC],
                    lhsT=selbc_s[:, (NST + n) * 128:(NST + n + 1) * 128],
                    rhs=bc_bf[:, :], start=True, stop=True),
                    reads=["selbc_s", "bc_bf", f"pb{bC}"], writes=[f"pb{bC}"],
                    tag=f"c{ch}_crepm{p2}_{n}")
                cr = crep[n % 2]
                crn = f"crep{n % 2}"
                B.op("scalar", lambda e, cr=cr, bC=bC: e.copy(
                    out=cr[:, :], in_=pb[bC][:, 0:TC]),
                    reads=[f"pb{bC}"], writes=[crn], tag=f"c{ch}_ecr{p2}_{n}")
                for i, dg in enumerate(dgs):
                    par = (n * 4 + i) % 3
                    gpar = (n * 4 + i) % 8
                    col = dg * NST + n
                    B.op("scalar", lambda e, dg=dg, n=n, par=par: e.activation(
                        out=dA[par][:, :], in_=delta[:, dg * TC:(dg + 1) * TC],
                        func=AF.Exp,
                        scale=Afm[:, dg * NST + n:dg * NST + n + 1]),
                        reads=[f"delta{dg}", "Afm"], writes=[f"dA{par}"],
                        tag=f"c{ch}_dA{dg}_{n}")
                    B.op("vector", lambda e, dg=dg, par=par, bB=bB: e.tensor_mul(
                        out=dBx[par][:, :], in0=uful[:, dg * TC:(dg + 1) * TC],
                        in1=pb[bB][:, 0:TC]),
                        reads=[f"u{dg}", f"pb{bB}"], writes=[f"dBx{par}"],
                        tag=f"c{ch}_dBx{dg}_{n}")
                    B.op("vector", lambda e, par=par, col=col: e.tensor_tensor_scan(
                        out=hs[par][:, :], data0=dA[par][:, :], data1=dBx[par][:, :],
                        initial=carry[:, col:col + 1], op0=ALU.mult, op1=ALU.add),
                        reads=[f"dA{par}", f"dBx{par}", "carry"],
                        writes=[f"hs{par}"], tag=f"c{ch}_scan{dg}_{n}")
                    B.op("gpsimd", lambda e, par=par, col=col: e.tensor_copy(
                        out=carry[:, col:col + 1], in_=hs[par][:, TC - 1:TC]),
                        reads=[f"hs{par}"], writes=["carry"],
                        tag=f"c{ch}_cw{dg}_{n}")
                    B.op("gpsimd", lambda e, par=par, gpar=gpar, cr=cr, crn=crn:
                         e.tensor_mul(
                            out=gy[gpar][:, :], in0=hs[par][:, :], in1=cr[:, :]),
                         reads=[f"hs{par}", crn], writes=[f"gy{gpar}"],
                         tag=f"c{ch}_gy{dg}_{n}")
                for (gpar_, i_, n_) in pend:
                    B.op("tensor", lambda e, gpar_=gpar_, i_=i_, n_=n_: e.matmul(
                        out=pb[i_ % 2][:, (i_ // 2) * TC:(i_ // 2 + 1) * TC],
                        lhsT=ident_bf[:, :], rhs=gy[gpar_][:, :],
                        start=False, stop=False),
                        reads=["ident_bf", f"gy{gpar_}", f"pb{i_ % 2}"],
                        writes=[f"pb{i_ % 2}"], tag=f"c{ch}_ysum{i_}_{n_}")
                pend = [((n * 4 + i) % 8, i, n) for i in range(4)]
                yield "S"
            for (gpar_, i_, n_) in pend:
                B.op("tensor", lambda e, gpar_=gpar_, i_=i_, n_=n_: e.matmul(
                    out=pb[i_ % 2][:, (i_ // 2) * TC:(i_ // 2 + 1) * TC],
                    lhsT=ident_bf[:, :], rhs=gy[gpar_][:, :],
                    start=False, stop=(i_ >= 2)),
                    reads=["ident_bf", f"gy{gpar_}", f"pb{i_ % 2}"],
                    writes=[f"pb{i_ % 2}"], tag=f"c{ch}_ysumf{i_}")
            for i, dg in enumerate(dgs):
                t1 = t1s[i % 2]
                t1n = f"t1s{i % 2}"
                B.op("vector", lambda e, dg=dg, i=i, t1=t1: e.scalar_tensor_tensor(
                    out=t1[:, :], in0=xc[:, dg * TC:(dg + 1) * TC],
                    scalar=dpar_s[:, dg:dg + 1],
                    in1=pb[i % 2][:, (i // 2) * TC:(i // 2 + 1) * TC],
                    op0=ALU.mult, op1=ALU.add),
                    reads=["xc", "dpar_s", f"pb{i % 2}"], writes=[t1n],
                    tag=f"c{ch}_ym1_{dg}")
                B.op("vector", lambda e, dg=dg, t1=t1, t1n=t1n, pz=pz: e.tensor_mul(
                    out=ymix_bf[pz][:, dg * TC:(dg + 1) * TC], in0=t1[:, :],
                    in1=sz[:, dg * TC:(dg + 1) * TC]),
                    reads=[t1n, "sz"], writes=[f"ymix_bf{pz}"],
                    tag=f"c{ch}_ym2_{dg}")
            yield "S"

    def back(ch):
        Builder.ctx = ("back", ch)
        """P5..P8 for chunk ch on banks 4..7. Yields at unit boundaries."""
        t0 = ch * TC
        xb = x_tm[ch % 2]
        xbn = f"x_tm{ch % 2}"
        pz = ch % 2
        xnb = xn_bf[pz]
        xnbn = f"xn_bf{pz}"
        ymx = ymix_bf[pz]
        ymxn = f"ymix_bf{pz}"

        olanes = [load_lane(ch, 2, kc * 1024) for kc in range(8)]
        if ch == 0:
            for kc in range(8):
                sl, sn, _ = olanes[kc]
                B.op("vector", lambda e, sl=sl, kc=kc: e.reduce_sum(
                    out=wos[:, kc:kc + 1], in_=sl[:, :], axis=AXX),
                    reads=[sn], writes=["wos"], tag=f"wos{kc}")
            B.op("vector", lambda e: e.tensor_copy(out=wos_bf[:, :], in_=wos[:, :]),
                 reads=["wos"], writes=["wos_bf"])
        for dt in range(8):
            bank = 4 + dt % 2
            for kc in range(8):
                sl, sn, _ = olanes[kc]
                B.op("tensor", lambda e, sl=sl, dt=dt, kc=kc, bank=bank:
                     e.matmul(
                        out=pb[bank][:, 0:TC],
                        lhsT=sl[:, dt * 128:(dt + 1) * 128],
                        rhs=ymx[:, kc * TC:(kc + 1) * TC],
                        start=(kc == 0), stop=(kc == 7)),
                     reads=[sn, ymxn, f"pb{bank}"], writes=[f"pb{bank}"],
                     tag=f"c{ch}_op{dt}_{kc}")
            B.op("vector", lambda e, dt=dt, bank=bank, xnb=xnb: e.tensor_add(
                out=x1[:, dt * TC:(dt + 1) * TC],
                in0=xnb[:, dt * TC:(dt + 1) * TC], in1=pb[bank][:, 0:TC]),
                reads=[xnbn, f"pb{bank}"], writes=["x1"], tag=f"c{ch}_x1_{dt}")
            yield
        for _, _, jl in olanes:
            free_lane(jl)
        for kc in range(8):
            B.op("tensor", lambda e, kc=kc: e.matmul(
                out=pb[6][0:1, 0:TC], lhsT=wos_bf[:, kc:kc + 1],
                rhs=ymx[:, kc * TC:(kc + 1) * TC],
                start=(kc == 0), stop=(kc == 7)),
                reads=["wos_bf", ymxn, "pb6"], writes=["pb6"],
                tag=f"c{ch}_smo{kc}")
        yield

        for dg in range(DG):
            B.op("scalar", lambda e, dg=dg: e.activation(
                out=xsq_bf[:, dg * TC:(dg + 1) * TC],
                in_=x1[:, dg * TC:(dg + 1) * TC], func=AF.Square),
                reads=["x1"], writes=[f"hgm{dg // 4}"], tag=f"c{ch}_xsq{dg}")
        for kc in range(8):
            B.op("tensor", lambda e, kc=kc: e.matmul(
                out=pb[7][0:1, 0:TC], lhsT=onec_bf[:, :],
                rhs=xsq_bf[:, kc * TC:(kc + 1) * TC],
                start=(kc == 0), stop=(kc == 7)),
                reads=["onec_bf", f"hgm{kc // 4}", "pb7"], writes=["pb7"],
                tag=f"c{ch}_ssq{kc}")
        B.op("vector", lambda e: e.tensor_scalar(
            out=r_mu2[:, :], in0=pb[6][0:1, 0:TC], scalar1=1.0 / D, scalar2=None,
            op0=ALU.mult),
            reads=["pb6"], writes=["rowb"], tag=f"c{ch}_mu2")
        B.op("vector", lambda e: e.tensor_scalar(
            out=r_tmp[:, :], in0=pb[7][0:1, 0:TC], scalar1=1.0 / D, scalar2=None,
            op0=ALU.mult),
            reads=["pb7"], writes=["rowb"])
        B.op("vector", lambda e: e.scalar_tensor_tensor(
            out=r_rstd2[:, :], in0=r_mu2[:, :], scalar=-1.0, in1=r_mu2[:, :],
            op0=ALU.mult, op1=ALU.mult),
            reads=["rowb"], writes=["rowb"])
        B.op("vector", lambda e: e.tensor_add(
            out=r_rstd2[:, :], in0=r_tmp[:, :], in1=r_rstd2[:, :]),
            reads=["rowb"], writes=["rowb"])
        B.op("scalar", lambda e: e.activation(
            out=r_rstd2[:, :], in_=r_rstd2[:, :], func=AF.Ln, bias=LN_EPS),
            reads=["rowb", "cc"], writes=["rowb"], tag=f"c{ch}_lr2")
        B.op("scalar", lambda e: e.activation(
            out=r_rstd2[:, :], in_=r_rstd2[:, :], func=AF.Exp, scale=-0.5),
            reads=["rowb", "cc"], writes=["rowb"])
        B.op("vector", lambda e: e.scalar_tensor_tensor(
            out=r_m2neg[:, :], in0=r_mu2[:, :], scalar=-1.0, in1=r_rstd2[:, :],
            op0=ALU.mult, op1=ALU.mult),
            reads=["rowb"], writes=["rowb"])
        B.op("tensor", lambda e: e.matmul(
            out=pb[6][:, 0:TC], lhsT=ones_f[:, :], rhs=r_rstd2[:, :],
            start=True, stop=True),
            reads=["ones_f", "rowb", "pb6"], writes=["pb6"], tag=f"c{ch}_br2")
        B.op("vector", lambda e: e.tensor_copy(out=rstd2_sb[:, :], in_=pb[6][:, 0:TC]),
             reads=["pb6"], writes=["rstd2_sb"])
        B.op("tensor", lambda e: e.matmul(
            out=pb[7][:, 0:TC], lhsT=ones_f[:, :], rhs=r_m2neg[:, :],
            start=True, stop=True),
            reads=["ones_f", "rowb", "pb7"], writes=["pb7"], tag=f"c{ch}_bm2")
        B.op("vector", lambda e: e.tensor_copy(out=m2neg_sb[:, :], in_=pb[7][:, 0:TC]),
             reads=["pb7"], writes=["m2neg_sb"])
        for dg in range(DG):
            B.op("vector", lambda e, dg=dg: e.scalar_tensor_tensor(
                out=etmp2[:, :], in0=x1[:, dg * TC:(dg + 1) * TC],
                scalar=g2_s[:, dg:dg + 1], in1=rstd2_sb[:, :],
                op0=ALU.mult, op1=ALU.mult),
                reads=["x1", "g2_s", "rstd2_sb"], writes=["etmp2"],
                tag=f"c{ch}_lnp1_{dg}")
            B.op("vector", lambda e, dg=dg: e.scalar_tensor_tensor(
                out=lnp_bf[:, dg * TC:(dg + 1) * TC], in0=m2neg_sb[:, :],
                scalar=g2_s[:, dg:dg + 1], in1=etmp2[:, :],
                op0=ALU.mult, op1=ALU.add),
                reads=["m2neg_sb", "g2_s", "etmp2"], writes=["lnp_bf"],
                tag=f"c{ch}_lnp2_{dg}")
        yield

        B.op("tensor", lambda e, pz=pz: e.matmul(
            out=pb[4][0:8, 0:TC], lhsT=ones_f[:, 0:8], rhs=trow_rn[pz][:, :],
            start=True, stop=True),
            reads=["ones_f", f"trow_rn{pz}", "pb4"], writes=["pb4"],
            tag=f"c{ch}_brn")
        for kc in range(8):
            B.op("tensor", lambda e, kc=kc, xnb=xnb: e.matmul(
                out=pb[5][0:8, 0:TC], lhsT=cw_bf[:, kc * 16:kc * 16 + 8],
                rhs=xnb[:, kc * TC:(kc + 1) * TC],
                start=(kc == 0), stop=False),
                reads=["cw_bf", xnbn, "pb5"], writes=["pb5"], tag=f"c{ch}_rt{kc}")
            B.op("tensor", lambda e, kc=kc, xnb=xnb: e.matmul(
                out=pb[5][0:8, TC:2 * TC], lhsT=cw_bf[:, kc * 16 + 8:kc * 16 + 16],
                rhs=xnb[:, kc * TC:(kc + 1) * TC],
                start=False, stop=(kc == 7)),
                reads=["cw_bf", xnbn, "pb5"], writes=["pb5"], tag=f"c{ch}_rw{kc}")
        B.op("vector", lambda e: e.tensor_scalar(
            out=ltrow[:, :], in0=pb[5][0:8, 0:TC], scalar1=rc_s[:, 2:3],
            scalar2=None, op0=ALU.mult),
            reads=["pb5", "rc_s"], writes=["ltrow"], tag=f"c{ch}_lg1")
        B.op("vector", lambda e: e.tensor_mul(
            out=ltrow[:, :], in0=ltrow[:, :], in1=pb[4][0:8, 0:TC]),
            reads=["ltrow", "pb4"], writes=["ltrow"])
        B.op("vector", lambda e: e.tensor_add(
            out=ltrow[:, :], in0=ltrow[:, :], in1=pb[5][0:8, TC:2 * TC]),
            reads=["ltrow", "pb5"], writes=["ltrow"])
        for ts in range(NTS):
            bank = 6 + ts % 2
            B.op("tensor", lambda e, ts=ts, bank=bank: e.transpose(
                out=pb[bank][0:128, 0:8], in_=ltrow[:, ts * 128:(ts + 1) * 128],
                identity=ident[0:8, 0:8]),
                reads=["ltrow", "ident", f"pb{bank}"], writes=[f"pb{bank}"],
                tag=f"c{ch}_lt{ts}")
            B.op("vector", lambda e, ts=ts, bank=bank: e.tensor_copy(
                out=lt_tm[:, ts * 8:(ts + 1) * 8], in_=pb[bank][0:128, 0:8]),
                reads=[f"pb{bank}"], writes=["lt_tm"])
            B.op("vector", lambda e, ts=ts: e.max(
                out=gmax[:, ts * 8:(ts + 1) * 8], in_=lt_tm[:, ts * 8:(ts + 1) * 8]),
                reads=["lt_tm"], writes=["gmax"], tag=f"c{ch}_gmx{ts}")
            B.op("vector", lambda e, ts=ts: e.tensor_scalar(
                out=gmsk[:, ts * 8:(ts + 1) * 8], in0=lt_tm[:, ts * 8:(ts + 1) * 8],
                scalar1=gmax[:, ts * 8 + 1:ts * 8 + 2], scalar2=None,
                op0=ALU.is_ge),
                reads=["lt_tm", "gmax"], writes=["gmsk"])
            B.op("vector", lambda e, ts=ts: e.tensor_scalar(
                out=gneg[:, ts:ts + 1], in0=gmax[:, ts * 8:ts * 8 + 1],
                scalar1=-1.0, scalar2=None, op0=ALU.mult),
                reads=["gmax"], writes=["gneg"])
            B.op("scalar", lambda e, ts=ts: e.activation(
                out=gwrk[:, ts * 8:(ts + 1) * 8], in_=lt_tm[:, ts * 8:(ts + 1) * 8],
                func=AF.Exp, bias=gneg[:, ts:ts + 1]),
                reads=["lt_tm", "gneg"], writes=["gwrk"], tag=f"c{ch}_gex{ts}")
            B.op("vector", lambda e, ts=ts: e.tensor_mul(
                out=gwrk[:, ts * 8:(ts + 1) * 8], in0=gwrk[:, ts * 8:(ts + 1) * 8],
                in1=gmsk[:, ts * 8:(ts + 1) * 8]),
                reads=["gwrk", "gmsk"], writes=["gwrk"])
            B.op("vector", lambda e, ts=ts: e.reduce_sum(
                out=gden[:, ts:ts + 1], in_=gwrk[:, ts * 8:(ts + 1) * 8], axis=AXX),
                reads=["gwrk"], writes=["gden"])
            B.op("vector", lambda e, ts=ts: e.reciprocal(
                out=gden[:, ts:ts + 1], in_=gden[:, ts:ts + 1]),
                reads=["gden"], writes=["gden"])
            B.op("vector", lambda e, ts=ts: e.tensor_scalar(
                out=gwrk[:, ts * 8:(ts + 1) * 8], in0=gwrk[:, ts * 8:(ts + 1) * 8],
                scalar1=gden[:, ts:ts + 1], scalar2=None, op0=ALU.mult),
                reads=["gwrk", "gden"], writes=["gwrk"], tag=f"c{ch}_gnm{ts}")
            B.op("tensor", lambda e, ts=ts, bank=bank: e.transpose(
                out=pb[bank][0:8, 0:128], in_=gwrk[:, ts * 8:(ts + 1) * 8],
                identity=ident[:, :]),
                reads=["gwrk", "ident", f"pb{bank}"], writes=[f"pb{bank}"],
                tag=f"c{ch}_gtr{ts}")
            B.op("scalar", lambda e, ts=ts, bank=bank: e.copy(
                out=gT_bf[:, ts * 128:(ts + 1) * 128], in_=pb[bank][0:8, 0:128]),
                reads=[f"pb{bank}"], writes=["gT_bf"])
        yield

        grepsb = ymix_bf[pz]
        grepsn = ymxn
        for ei in range(NE):
            mlanes = [load_lane(ch, 3 + ei, kb * 1024) for kb in range(4)]
            B.op("tensor", lambda e, ei=ei: e.matmul(
                out=pb[6][:, 0:TC], lhsT=selg_s[:, ei * 128:(ei + 1) * 128],
                rhs=gT_bf[:, :], start=True, stop=True),
                reads=["selg_s", "gT_bf", "pb6"], writes=["pb6"],
                tag=f"c{ch}_grep{ei}")
            B.op("scalar", lambda e, ei=ei, grepsb=grepsb: e.copy(
                out=grepsb[:, ei * TC:(ei + 1) * TC], in_=pb[6][:, 0:TC]),
                reads=["pb6"], writes=[grepsn], tag=f"c{ch}_egrep{ei}")
            for ht in range(4):
                hb = 4 + ht % 2
                for kc in range(8):
                    sl, sn, _ = mlanes[kc // 2]
                    B.op("tensor", lambda e, sl=sl, ht=ht, kc=kc,
                         hb=hb, xnb=xnb: e.matmul(
                            out=pb[hb][:, 0:TC],
                            lhsT=sl[:, (kc % 2) * 512 + ht * 128:
                                    (kc % 2) * 512 + (ht + 1) * 128],
                            rhs=xnb[:, kc * TC:(kc + 1) * TC],
                            start=(kc == 0), stop=(kc == 7)),
                         reads=[sn, xnbn, f"pb{hb}"], writes=[f"pb{hb}"],
                         tag=f"c{ch}_mh{ei}_{ht}_{kc}")
                hi = ei * 4 + ht
                B.op("scalar", lambda e, ei=ei, ht=ht, hb=hb, hi=hi:
                     e.activation(
                        out=hgm[:, hi * TC:(hi + 1) * TC], in_=pb[hb][:, 0:TC],
                        func=AF.Identity,
                        bias=mb1_s[:, ei * 4 + ht:ei * 4 + ht + 1]),
                     reads=[f"pb{hb}", "mb1_s"], writes=[f"hgm{hi // 4}"],
                     tag=f"c{ch}_mg{ei}_{ht}")
                yield
            for _, _, jl in mlanes:
                free_lane(jl)
        for j in range(16):
            B.op("scalar", lambda e, j=j: e.activation(
                out=hgm[:, j * 512:(j + 1) * 512],
                in_=hgm[:, j * 512:(j + 1) * 512], func=AF.Gelu),
                reads=[f"hgm{j // 2}"], writes=[f"hgm{j // 2}"],
                tag=f"c{ch}_gelu{j}")
        for hi in range(32):
            ei = hi // 4
            B.op("gpsimd", lambda e, hi=hi, ei=ei, grepsb=grepsb: e.tensor_mul(
                out=hgm[:, hi * TC:(hi + 1) * TC],
                in0=hgm[:, hi * TC:(hi + 1) * TC],
                in1=grepsb[:, ei * TC:(ei + 1) * TC]),
                reads=[f"hgm{hi // 4}", grepsn], writes=[f"hgm{hi // 4}"],
                tag=f"c{ch}_mhg{hi}")
        yield

        for b in range(4):
            zero_mm(4 + b, f"c{ch}_mz{b}")
        m2q = [load_lane(ch, 3, 4096), load_lane(ch, 3, 4096 + 1024)]
        for ei in range(NE):
            for ht in range(4):
                hi = ei * 4 + ht
                sl, sn, jl = m2q.pop(0)
                nxt = ei * 4 + ht + 2
                if nxt < NE * 4:
                    m2q.append(load_lane(ch, 3 + nxt // 4,
                                         4096 + (nxt % 4) * 1024))
                for dt in range(8):
                    bank, hh = 4 + dt % 4, dt // 4
                    B.op("tensor", lambda e, sl=sl, ht=ht, dt=dt,
                         bank=bank, hh=hh, hi=hi: e.matmul(
                            out=half(bank, hh),
                            lhsT=sl[:, dt * 128:(dt + 1) * 128],
                            rhs=hgm[:, hi * TC:(hi + 1) * TC],
                            start=False, stop=False),
                         reads=[sn, f"hgm{hi // 4}", f"pb{bank}"],
                         writes=[f"pb{bank}"],
                         tag=f"c{ch}_mo{ei}_{ht}_{dt}")
                free_lane(jl)
                yield
        for dt in range(8):
            bank, hh = 4 + dt % 4, dt // 4
            B.op("tensor", lambda e, dt=dt, bank=bank, hh=hh: e.matmul(
                out=half(bank, hh), lhsT=mb2_bf[:, dt * 128:(dt + 1) * 128],
                rhs=gT_bf[:, :], start=False, stop=(dt >= 4)),
                reads=["mb2_bf", "gT_bf", f"pb{bank}"], writes=[f"pb{bank}"],
                tag=f"c{ch}_mb2_{dt}")
        for dt in range(8):
            bank, hh = 4 + dt % 4, dt // 4
            B.op("scalar", lambda e, dt=dt, bank=bank, hh=hh: e.activation(
                out=gfin[:, dt * TC:(dt + 1) * TC], in_=half(bank, hh),
                func=AF.Silu),
                reads=[f"pb{bank}"], writes=["gfin"], tag=f"c{ch}_gfin{dt}")
        yield

        if ch == 0:
            for hg in range(8):
                fblanes = [load_lane(ch, 11 + hg, kb * 1024) for kb in range(4)]
                for htl in range(4):
                    ht = hg * 4 + htl
                    for kc in range(8):
                        sl, sn, _ = fblanes[kc // 2]
                        B.op("tensor", lambda e, sl=sl, htl=htl,
                             kc=kc, ht=ht: e.matmul(
                                out=pb[7][:, ht:ht + 1],
                                lhsT=sl[:, (kc % 2) * 512 + htl * 128:
                                        (kc % 2) * 512 + (htl + 1) * 128],
                                rhs=b2ln_bf[:, kc:kc + 1],
                                start=(ht == 0 and kc == 0),
                                stop=(ht == 31 and kc == 7)),
                             reads=[sn, "b2ln_bf", "pb7"], writes=["pb7"],
                             tag=f"wb2f_{ht}_{kc}")
                for _, _, jl in fblanes:
                    free_lane(jl)
            B.op("vector", lambda e: e.tensor_add(
                out=wb2f[:, :], in0=pb[7][:, 0:32], in1=b1f_s[:, :]),
                reads=["pb7", "b1f_s"], writes=["wb2f"], tag="wb2f_ev")
        for hg in range(8):
            f1lanes = [load_lane(ch, 11 + hg, kb * 1024) for kb in range(4)]
            for htl in range(4):
                ht = hg * 4 + htl
                hb = 4 + ht % 2
                for kc in range(8):
                    sl, sn, _ = f1lanes[kc // 2]
                    B.op("tensor", lambda e, sl=sl, htl=htl, kc=kc,
                         hb=hb: e.matmul(
                            out=pb[hb][:, 0:TC],
                            lhsT=sl[:, (kc % 2) * 512 + htl * 128:
                                    (kc % 2) * 512 + (htl + 1) * 128],
                            rhs=lnp_bf[:, kc * TC:(kc + 1) * TC],
                            start=(kc == 0), stop=(kc == 7)),
                         reads=[sn, "lnp_bf", f"pb{hb}"], writes=[f"pb{hb}"],
                         tag=f"c{ch}_f1_{hg}_{htl}_{kc}")
                hp = htl % 2
                B.op("scalar", lambda e, hb=hb, hp=hp, ht=ht: e.activation(
                    out=hgm[:, ht * TC:(ht + 1) * TC], in_=pb[hb][:, 0:TC],
                    func=AF.Relu, bias=wb2f[:, ht:ht + 1]),
                    reads=[f"pb{hb}", "wb2f"], writes=[f"hgm{ht // 4}"],
                    tag=f"c{ch}_frl{ht}")
                yield
            for _, _, jl in f1lanes:
                free_lane(jl)

        for b in range(4):
            zero_mm(4 + b, f"c{ch}_fz{b}")
        f2q = [load_lane(ch, 11, 4096), load_lane(ch, 11, 4096 + 1024)]
        for hg in range(8):
            for htl in range(4):
                ht = hg * 4 + htl
                sl, sn, jl = f2q.pop(0)
                nxt = hg * 4 + htl + 2
                if nxt < 32:
                    f2q.append(load_lane(ch, 11 + nxt // 4,
                                         4096 + (nxt % 4) * 1024))
                for dt in range(8):
                    bank, hh = 4 + dt % 4, dt // 4
                    lastf2 = (hg == 7 and htl == 3 and dt >= 4)
                    B.op("tensor", lambda e, sl=sl, htl=htl, dt=dt,
                         bank=bank, hh=hh, ht=ht, lastf2=lastf2: e.matmul(
                            out=half(bank, hh),
                            lhsT=sl[:, dt * 128:(dt + 1) * 128],
                            rhs=hgm[:, ht * TC:(ht + 1) * TC],
                            start=False, stop=lastf2),
                         reads=[sn, f"hgm{ht // 4}", f"pb{bank}"],
                         writes=[f"pb{bank}"],
                         tag=f"c{ch}_f2_{hg}_{htl}_{dt}")
                free_lane(jl)
                yield
        for dt in range(8):
            bank, hh = 4 + dt % 4, dt // 4
            B.op("vector", lambda e, dt=dt, bank=bank, hh=hh: e.scalar_tensor_tensor(
                out=sS[:, dt * TC:(dt + 1) * TC], in0=half(bank, hh),
                scalar=b2f_s[:, dt:dt + 1], in1=x1[:, dt * TC:(dt + 1) * TC],
                op0=ALU.add, op1=ALU.add),
                reads=[f"pb{bank}", "b2f_s", "x1"], writes=["sS"],
                tag=f"c{ch}_sf{dt}")
            B.op("vector", lambda e, dt=dt: e.tensor_mul(
                out=sS[:, dt * TC:(dt + 1) * TC], in0=sS[:, dt * TC:(dt + 1) * TC],
                in1=gfin[:, dt * TC:(dt + 1) * TC]),
                reads=["sS", "gfin"], writes=["sS"], tag=f"c{ch}_sg{dt}")
        yield

        for ti in range(8 * NTS):
            ts, dg = divmod(ti, 8)
            bank = 6 + (ti % 2)
            B.op("tensor", lambda e, ts=ts, dg=dg, bank=bank: e.transpose(
                out=pb[bank][:, 0:128],
                in_=sS[:, dg * TC + ts * 128:dg * TC + (ts + 1) * 128],
                identity=ident[:, :]),
                reads=["sS", "ident", f"pb{bank}"], writes=[f"pb{bank}"],
                tag=f"c{ch}_tsg{ts}_{dg}")
            B.op("vector", lambda e, ts=ts, dg=dg, bank=bank, xb=xb: e.tensor_add(
                out=xb[:, ts * D + dg * 128:ts * D + (dg + 1) * 128],
                in0=xb[:, ts * D + dg * 128:ts * D + (dg + 1) * 128],
                in1=pb[bank][:, 0:128]),
                reads=[xbn, f"pb{bank}"], writes=[xbn],
                tag=f"c{ch}_oadd{ts}_{dg}")
            if ti % 4 == 3:
                yield
        B.dma("gpsimd", lambda e, t0=t0, xb=xb: e.dma_start(
            out=out_d[t0:t0 + TC, :].rearrange("(s p) d -> p s d", p=128),
            in_=xb[:, :]),
            reads=[xbn], writes=["out_d"], tag=f"c{ch}_st")
        yield

    def run_gen(g):
        for _ in g:
            pass

    def chain(*gens):
        for g in gens:
            for v in g:
                yield v

    if PIPELINE:
        prev_back = repack_gen()
        rgen_cell[0] = prev_back
        for ch in range(nchunks):
            fgen = chain(front_head(ch), front_scan(ch))
            if prev_back is None:
                run_gen(fgen)
            else:
                bgen = prev_back
                done_f = done_b = False
                budget = 0.0
                while not (done_f and done_b):
                    ph = None
                    if not done_f:
                        Builder.ctx = ("front", ch)
                        ph = next(fgen, "END")
                        done_f = ph == "END"
                    budget += 1e9 if done_f else {"S": 3.4, "H": 2.0,
                                  "Hc": 0.5}.get(ph, 1.0)
                    while budget >= 1.0 and not done_b:
                        Builder.ctx = ("back", ch - 1)
                        done_b = next(bgen, "END") == "END"
                        budget -= 1.0
            prev_back = back(ch)
        run_gen(prev_back)
    else:
        run_gen(repack_gen())
        for ch in range(nchunks):
            run_gen(chain(front_head(ch), front_scan(ch)))
            run_gen(back(ch))

    B.drain_all("sync")
    return nc, B


def _run(nc, B, in_maps, core_ids):
    emit = B.finalize(nc)
    with nc.Block() as block:
        block.sync(lambda e: emit("sync", e))
        block.tensor(lambda e: emit("tensor", e))
        block.vector(lambda e: emit("vector", e))
        block.scalar(lambda e: emit("scalar", e))
        block.gpsimd(lambda e: emit("gpsimd", e))
    return run_bass_kernel_spmd(nc, in_maps, core_ids=core_ids)


def _mk_sel(n):
    import ml_dtypes
    s = np.zeros((n, n, 128), np.float32)
    for k in range(n):
        s[k, k, :] = 1.0
    return s.reshape(n, n * 128).astype(ml_dtypes.bfloat16)


def _mk_wpf(inputs):
    f = np.float32

    def blkT(a, nb):
        C = a.shape[1]
        return np.ascontiguousarray(a).reshape(nb, 128, C).transpose(
            1, 0, 2).reshape(128, nb * C)

    inwT = inputs["L0_attn_in_proj_w"].T.astype(f)
    outwT = inputs["L0_attn_out_proj_w"].T.astype(f)
    w1T = inputs["L0_attn_w1"].T.astype(f)
    w2T = inputs["L0_attn_w2"].T.astype(f)
    mw1 = np.asarray(inputs["L0_ffn_w1"], f)
    mw2 = np.asarray(inputs["L0_ffn_w2"], f)
    wpf = np.empty((NPIECE, 128, PCOLS), f)
    for ph in range(2):
        wpf[ph] = blkT(inwT[ph * 512:(ph + 1) * 512], 4)
    wpf[2] = blkT(outwT, 8)
    for ei in range(NE):
        wpf[3 + ei, :, 0:4096] = blkT(mw1[ei], 8)
        wpf[3 + ei, :, 4096:8192] = blkT(mw2[ei], 4)
    for hg in range(8):
        wpf[11 + hg, :, 0:4096] = blkT(w1T[:, hg * 512:(hg + 1) * 512], 8)
        wpf[11 + hg, :, 4096:8192] = blkT(w2T[hg * 512:(hg + 1) * 512], 4)
    return wpf


def make_common(inputs):
    cv = lambda k: np.ascontiguousarray(inputs[k], dtype=np.float32)
    return {
        "wpf": _mk_wpf(inputs), "convw": cv("L0_attn_conv_w"),
        "convb": cv("L0_attn_conv_b"), "xpw": cv("L0_attn_x_proj_w"),
        "dtw": cv("L0_attn_dt_proj_w"), "dtb": cv("L0_attn_dt_proj_b"),
        "alog": cv("L0_attn_A_log"), "dpar": cv("L0_attn_D_param"),
        "g1": cv("L0_attn_ln1_g"),
        "b1": cv("L0_attn_ln1_b"), "g2": cv("L0_attn_ln2_g"),
        "b2": cv("L0_attn_ln2_b"),
        "b1f": cv("L0_attn_b1"),
        "b2f": cv("L0_attn_b2"),
        "mb1": cv("L0_ffn_b1"),
        "mb2": cv("L0_ffn_b2"), "cent": cv("L0_ffn_centroids"),
        "wrt": cv("L0_ffn_w_route"),
        "ident": np.eye(128, dtype=np.float32),
        "consts": np.tile(np.array(CONSTS, dtype=np.float32), (128, 1)),
        "selbc": _mk_sel(32),
        "selg": _mk_sel(8),
    }


def kernel(**inputs):
    x_full = np.ascontiguousarray(inputs["L0_attn_x"], dtype=np.float32)
    common = make_common(inputs)
    nc, B = build_nc()
    in_maps = []
    for c in range(8):
        m = dict(common)
        m["x"] = np.ascontiguousarray(x_full[c])
        in_maps.append(m)
    res = _run(nc, B, in_maps, list(range(8)))
    out = np.stack([np.asarray(res.results[c]["out"]) for c in range(8)], axis=0)
    return out.astype(np.float32)
